# version 08
# speedup vs baseline: 1.4037x; 1.0733x over previous
import contextlib
import numpy as np
import concourse.bass as bass
import concourse.mybir as mybir
from concourse.bass_utils import run_bass_kernel_spmd
from concourse.bass_core_rust import IndirectOffsetOnAxis

F32 = mybir.dt.float32
BF16 = mybir.dt.bfloat16
AF = mybir.ActivationFunctionType
ALU = mybir.AluOpType
AX = mybir.AxisListType

B, N, D = 4, 2048, 2048
NT = N // 128
ND = D // 128

VARS = {
    "xnt": ((N, D), BF16),
}

CONSTS_SHAPES = {
    "x0": ((N, D), F32),
    "eye_bf": ((128, 128), BF16),
}

PHASES = {}


def phase(name, ins, outs):
    def deco(fn):
        PHASES[name] = (fn, ins, outs)
        return fn
    return deco


class Chain:
    def __init__(self, sem):
        self.sem = sem
        self.n = 0

    def inc(self, instr, k=1):
        self.n += k
        instr.then_inc(self.sem, k)
        return self.n

    def wait(self, eng, v=None):
        eng.wait_ge(self.sem, self.n if v is None else v)


class _BlockProxy:
    def __init__(self, block, ctx):
        self._block = block
        self._ctx = ctx

    def __getattr__(self, name):
        sec = getattr(self._block, name)
        ctx = self._ctx

        def wrapped(fn):
            def inner(eng):
                if ctx.phase_idx > 0:
                    eng.wait_ge(ctx.phd.sem, ctx.phase_idx)
                return fn(eng)
            return sec(inner)
        return wrapped


class Ctx:
    def __init__(self, nc, block, aps, gstack):
        self.nc = nc
        self.rawblock = block
        self.block = _BlockProxy(block, self)
        self.aps = aps
        self.gstack = gstack
        self.stack = None
        self.sem_n = 0
        self.buf_n = 0
        self.phase_idx = 0
        self.phd = None
        self.phase_chains = []

    def chain(self):
        self.sem_n += 1
        ch = Chain(self.gstack.enter_context(self.nc.semaphore(f"s{self.sem_n}")))
        self.phase_chains.append(ch)
        return ch

    def sbuf(self, name, shape, dtype):
        self.buf_n += 1
        return self.stack.enter_context(
            self.nc.sbuf_tensor(f"{name}{self.buf_n}", list(shape), dtype))

    def psum(self, name, shape, dtype):
        self.buf_n += 1
        return self.stack.enter_context(
            self.nc.psum_tensor(f"{name}{self.buf_n}", list(shape), dtype))


def make_ln(name, src_name, dst_name):
    @phase(name, ins=[src_name], outs=[dst_name])
    def ph_ln(ctx, deps, src_name=src_name, dst_name=dst_name):
        return _ln_body(ctx, deps, src_name, dst_name)
    return ph_ln


def _ln_body(ctx, deps, src_name, dst_name):
    block = ctx.block
    x0, xnt = ctx.aps[src_name], ctx.aps[dst_name]
    dv = deps.get(src_name) if deps else None
    if dv:
        def _w(sync):
            sync.wait_ge(dv[0], dv[1])
        block.sync(_w)
    xt = ctx.sbuf("ln_xt", [128, D], F32)
    xnb = ctx.sbuf("ln_xnb", [128, D], BF16)
    bns = ctx.sbuf("ln_bns", [128, 24], F32)
    agg = ctx.sbuf("ln_agg", [128, 4], F32)

    LD, ST, CV, CS = ctx.chain(), ctx.chain(), ctx.chain(), ctx.chain()
    ld, st, cv, cs = LD.sem, ST.sem, CV.sem, CS.sem

    def _sync(sync):
        for i in range(NT):
            if i >= 1:
                sync.wait_ge(cv, 7 * i)
            sync.dma_start(out=xt[:], in_=x0[i * 128:(i + 1) * 128, :]).then_inc(ld, 16)
            sync.wait_ge(cv, 7 * i + 7)
            sync.dma_start(out=xnt[i * 128:(i + 1) * 128, :], in_=xnb[:]).then_inc(st, 16)
        sync.wait_ge(st, 16 * NT)

    def _vector(vector):
        for i in range(NT):
            vector.wait_ge(ld, 16 * (i + 1))
            for j in range(4):
                vector.bn_stats(bns[:, 6 * j:6 * (j + 1)],
                                xt[:, 512 * j:512 * (j + 1)]).then_inc(cv, 1)
            vector.wait_ge(cv, 7 * i + 4)
            vector.bn_aggr(agg[:, 0:2], bns[:]).then_inc(cv, 1)
            vector.wait_ge(cv, 7 * i + 5)
            vector.tensor_scalar(agg[:, 2:3], agg[:, 1:2], 1e-5, None, ALU.add) \
                .then_inc(cv, 1)
            vector.wait_ge(cs, 2 * i + 2)
            if i >= 1:
                vector.wait_ge(st, 16 * i)
            vector.tensor_scalar(xnb[:], xt[:], agg[:, 0:1], agg[:, 3:4],
                                 ALU.subtract, ALU.mult).then_inc(cv, 1)

    def _scalar(scalar):
        for i in range(NT):
            scalar.wait_ge(cv, 7 * i + 6)
            scalar.activation(agg[:, 3:4], agg[:, 2:3], AF.Ln).then_inc(cs, 1)
            scalar.wait_ge(cs, 2 * i + 1)
            scalar.activation(agg[:, 3:4], agg[:, 3:4], AF.Exp, scale=-0.5) \
                .then_inc(cs, 1)

    block.sync(_sync)
    block.vector(_vector)
    block.scalar(_scalar)
    ST.n = 16 * NT
    return {dst_name: (st, 16 * NT)}


make_ln("ln0", "x0", "xnt")
make_ln("ln1", "x1b", "xn1t")
VARS["xn1t"] = ((N, D), BF16)


@phase("l0qkv", ins=["xnt", "Wq", "Wk", "Wb", "ccq", "ssq", "cck", "ssk",
                     "onesc", "onesr", "eye_f32"],
       outs=["qf", "kf", "ksf", "betad", "betatd"])
def ph_l0qkv(ctx, deps):
    block = ctx.block
    A = ctx.aps
    xnt, Wq, Wk, Wb = A["xnt"], A["Wq"], A["Wk"], A["Wb"]
    qf, kf, ksf, betad, betatd = (A[n] for n in
                                  ["qf", "kf", "ksf", "betad", "betatd"])

    xnf = ctx.sbuf("qk_xnf", [128, ND * N], BF16)
    wq = ctx.sbuf("qk_wq", [128, ND * 128], BF16)
    wk = ctx.sbuf("qk_wk", [128, ND * 128], BF16)
    wb = ctx.sbuf("qk_wb", [128, ND * 16], BF16)
    ccq = ctx.sbuf("qk_ccq", [128, N], BF16)
    ssq = ctx.sbuf("qk_ssq", [128, N], BF16)
    cck = ctx.sbuf("qk_cck", [128, N], BF16)
    ssk = ctx.sbuf("qk_ssk", [128, N], BF16)
    onesc = ctx.sbuf("qk_onesc", [128, 1], F32)
    onesr = ctx.sbuf("qk_onesr", [1, 128], F32)
    eyef = ctx.sbuf("qk_eyef", [128, 128], F32)
    betaf = ctx.sbuf("qk_betaf", [16, N], F32)
    betat = ctx.sbuf("qk_betat", [128, 16], F32)
    brow = ctx.sbuf("qk_brow", [1, N], F32)
    qa = ctx.sbuf("qk_qa", [128, N], F32)
    qsw = ctx.sbuf("qk_qsw", [128, N], F32)
    qout = ctx.sbuf("qk_qout", [128, N], BF16)
    kall = ctx.sbuf("qk_kall", [128, N], F32)
    ksq = ctx.sbuf("qk_ksq", [128, N], F32)
    s_row = ctx.sbuf("qk_srow", [1, N], F32)
    knr = ctx.sbuf("qk_knr", [128, N], F32)
    ksw, kt1 = ksq, kall
    kfout = ctx.sbuf("qk_kfout", [128, N], BF16)
    ksfout = ctx.sbuf("qk_ksfout", [128, N], BF16)
    psq = ctx.psum("qk_psq", [128, 512], F32)
    ps1 = ctx.psum("qk_ps1", [1, 512], F32)
    psb = ctx.psum("qk_psb", [128, 512], F32)
    psbt = ctx.psum("qk_psbt", [16, 512], F32)
    pstr = ctx.psum("qk_pstr", [128, 16], F32)

    LDX = ctx.chain()
    LDW = ctx.chain()
    CP = ctx.chain()
    CE = ctx.chain()
    CS = ctx.chain()
    CG = ctx.chain()
    STO = ctx.chain()

    dep_xnt = deps.get("xnt") if deps else None

    def _s0_sync(sync):
        if dep_xnt:
            sync.wait_ge(dep_xnt[0], dep_xnt[1])
        for c in range(ND):
            LDX.inc(sync.dma_start(out=xnf[:, c * N:(c + 1) * N],
                                   in_=xnt[:, c * 128:(c + 1) * 128], transpose=True), 16)
        for name, buf in (("ccq", ccq), ("ssq", ssq), ("cck", cck), ("ssk", ssk)):
            LDX.inc(sync.dma_start(out=buf[:], in_=A[name][:]), 16)
        LDX.inc(sync.dma_start(out=onesc[:], in_=A["onesc"][:]), 16)
        LDX.inc(sync.dma_start(out=onesr[:], in_=A["onesr"][:]), 16)
        LDX.inc(sync.dma_start(out=eyef[:], in_=A["eye_f32"][:]), 16)
    block.sync(_s0_sync)
    ldx0 = LDX.n

    def _s0_gp(gpsimd):
        LDW.inc(gpsimd.dma_start(out=wb[:].rearrange("p (c o) -> p c o", c=ND),
                                 in_=Wb.rearrange("(c p) o -> p c o", p=128)), 16)
    block.gpsimd(_s0_gp)

    for t4 in range(4):
        def _pe(tensor, t4=t4):
            tensor.wait_ge(LDX.sem, ldx0)
            LDW.wait(tensor)
            CS.wait(tensor)
            for c in range(ND):
                mm = tensor.matmul(psbt[:], wb[:, c * 16:(c + 1) * 16],
                                   xnf[:, c * N + t4 * 512: c * N + (t4 + 1) * 512],
                                   start=(c == 0), stop=(c == ND - 1))
            CP.inc(mm)
        block.tensor(_pe)

        def _sc(scalar, t4=t4):
            CP.wait(scalar)
            CS.inc(scalar.activation(betaf[:, t4 * 512:(t4 + 1) * 512], psbt[:],
                                     AF.Sigmoid))
        block.scalar(_sc)

    def _bv(vector):
        CS.wait(vector)
        CE.inc(vector.tensor_scalar(betaf[:], betaf[:], 2.0, None, ALU.mult))
    block.vector(_bv)

    def _bsy(sync):
        CE.wait(sync)
        STO.inc(sync.dma_start(out=betad[:], in_=betaf[:]), 16)
    block.sync(_bsy)
    betad_sto = STO.n

    for c in range(NT):
        def _pe(tensor, c=c):
            CE.wait(tensor)
            tensor.wait_ge(LDX.sem, ldx0)
            CP.inc(tensor.transpose(pstr[:], betaf[:, c * 128:(c + 1) * 128],
                                    eyef[0:16, 0:16]))
        block.tensor(_pe)

        def _ve(vector, c=c):
            CP.wait(vector)
            STO.wait(vector)
            CE.inc(vector.tensor_copy(betat[:], pstr[:]))
        block.vector(_ve)

        def _sy(sync, c=c):
            CE.wait(sync)
            STO.inc(sync.dma_start(out=betatd[c * 128:(c + 1) * 128, :], in_=betat[:]), 16)
        block.sync(_sy)

    for h in range(16):
        def _gp(gpsimd, h=h):
            CP.wait(gpsimd)
            LDW.inc(gpsimd.dma_start(
                out=wq[:].rearrange("p (c o) -> p c o", c=ND),
                in_=Wq[:, h * 128:(h + 1) * 128].rearrange("(c p) o -> p c o", p=128)), 16)
            LDW.inc(gpsimd.dma_start(
                out=wk[:].rearrange("p (c o) -> p c o", c=ND),
                in_=Wk[:, h * 128:(h + 1) * 128].rearrange("(c p) o -> p c o", p=128)), 16)
        block.gpsimd(_gp)

        for t4 in range(4):
            def _pe(tensor, t4=t4):
                LDW.wait(tensor)
                tensor.wait_ge(LDX.sem, ldx0)
                CE.wait(tensor)
                for c in range(ND):
                    mm = tensor.matmul(psq[:], wq[:, c * 128:(c + 1) * 128],
                                       xnf[:, c * N + t4 * 512: c * N + (t4 + 1) * 512],
                                       start=(c == 0), stop=(c == ND - 1))
                CP.inc(mm)
            block.tensor(_pe)

            def _ve(vector, t4=t4):
                CP.wait(vector)
                CG.wait(vector)
                CE.inc(vector.tensor_copy(qa[:, t4 * 512:(t4 + 1) * 512], psq[:]))
            block.vector(_ve)

        def _gp2(gpsimd, h=h):
            CE.wait(gpsimd)
            gpsimd.wait_ge(LDX.sem, ldx0)
            STO.wait(gpsimd)
            CG.inc(gpsimd.tensor_copy(qsw[0:64, :], qa[64:128, :]))
            CG.wait(gpsimd)
            CG.inc(gpsimd.tensor_copy(qsw[64:128, :], qa[0:64, :]))
            CG.wait(gpsimd)
            CG.inc(gpsimd.tensor_tensor(qa[:], qa[:], ccq[:], ALU.mult))
            CG.wait(gpsimd)
            CG.inc(gpsimd.tensor_tensor(qsw[:], qsw[:], ssq[:], ALU.mult))
            CG.wait(gpsimd)
            CG.inc(gpsimd.tensor_add(qout[:], qa[:], qsw[:]))
        block.gpsimd(_gp2)
        qrope_done = CG.n

        def _qst(sync, h=h):
            sync.wait_ge(CG.sem, qrope_done)
            STO.inc(sync.dma_start(out=qf[h * 128:(h + 1) * 128, :], in_=qout[:]), 16)
        block.sync(_qst)

        for t4 in range(4):
            def _pe(tensor, t4=t4):
                CE.wait(tensor)
                for c in range(ND):
                    mm = tensor.matmul(psq[:], wk[:, c * 128:(c + 1) * 128],
                                       xnf[:, c * N + t4 * 512: c * N + (t4 + 1) * 512],
                                       start=(c == 0), stop=(c == ND - 1))
                CP.inc(mm)
            block.tensor(_pe)

            def _ve(vector, t4=t4):
                CP.wait(vector)
                CS.wait(vector)
                CE.inc(vector.tensor_copy(kall[:, t4 * 512:(t4 + 1) * 512], psq[:]))
            block.vector(_ve)

        def _sc(scalar):
            CE.wait(scalar)
            CS.inc(scalar.activation(ksq[:], kall[:], AF.Square))
        block.scalar(_sc)

        for t4 in range(4):
            def _pe(tensor, t4=t4):
                CS.wait(tensor)
                CE.wait(tensor)
                CP.inc(tensor.matmul(ps1[:], onesc[:], ksq[:, t4 * 512:(t4 + 1) * 512],
                                     start=True, stop=True))
            block.tensor(_pe)

            def _ve(vector, t4=t4):
                CP.wait(vector)
                CE.inc(vector.tensor_copy(s_row[0:1, t4 * 512:(t4 + 1) * 512], ps1[:]))
            block.vector(_ve)

        def _ve2(vector):
            CE.wait(vector)
            CE.inc(vector.tensor_scalar(s_row[:], s_row[:], 1e-6, None, ALU.add))
        block.vector(_ve2)

        def _sc2(scalar):
            CE.wait(scalar)
            CS.inc(scalar.activation(s_row[:], s_row[:], AF.Ln))
            CS.wait(scalar)
            CS.inc(scalar.activation(s_row[:], s_row[:], AF.Exp, scale=-0.5))
        block.scalar(_sc2)

        for t4 in range(4):
            def _pe(tensor, t4=t4):
                CS.wait(tensor)
                CE.wait(tensor)
                CP.inc(tensor.matmul(psb[:], onesr[:],
                                     s_row[0:1, t4 * 512:(t4 + 1) * 512],
                                     start=True, stop=True))
            block.tensor(_pe)

            def _ve(vector, t4=t4):
                CP.wait(vector)
                CE.inc(vector.tensor_tensor(knr[:, t4 * 512:(t4 + 1) * 512],
                                            kall[:, t4 * 512:(t4 + 1) * 512],
                                            psb[:], ALU.mult))
            block.vector(_ve)

        def _krope(vector):
            CE.wait(vector)
            CE.inc(vector.tensor_copy(ksw[0:64, :], knr[64:128, :]))
            CE.wait(vector)
            CE.inc(vector.tensor_copy(ksw[64:128, :], knr[0:64, :]))
            CE.wait(vector)
            CE.inc(vector.tensor_tensor(kt1[:], knr[:], cck[:], ALU.mult))
            CE.wait(vector)
            CE.inc(vector.tensor_tensor(ksw[:], ksw[:], ssk[:], ALU.mult))
            CE.wait(vector)
            STO.wait(vector)
            CE.inc(vector.tensor_add(knr[:], kt1[:], ksw[:]))
            CE.wait(vector)
            CE.inc(vector.tensor_copy(kfout[:], knr[:]))
        block.vector(_krope)

        def _kst(sync, h=h):
            CE.wait(sync)
            STO.inc(sync.dma_start(out=kf[h * 128:(h + 1) * 128, :], in_=kfout[:]), 16)
            CP.wait(sync)
            LDX.inc(sync.dma_start(out=brow[:], in_=betad[h:h + 1, :]), 16)
        block.sync(_kst)
        brow_ld = LDX.n

        for t4 in range(4):
            def _pe(tensor, t4=t4):
                tensor.wait_ge(LDX.sem, brow_ld)
                CE.wait(tensor)
                CP.inc(tensor.matmul(psb[:], onesr[:],
                                     brow[0:1, t4 * 512:(t4 + 1) * 512],
                                     start=True, stop=True))
            block.tensor(_pe)

            def _ve(vector, t4=t4):
                CP.wait(vector)
                if t4 == 0:
                    STO.wait(vector)
                CE.inc(vector.tensor_tensor(ksfout[:, t4 * 512:(t4 + 1) * 512],
                                            knr[:, t4 * 512:(t4 + 1) * 512],
                                            psb[:], ALU.mult))
            block.vector(_ve)

        def _ksst(sync, h=h):
            CE.wait(sync)
            STO.inc(sync.dma_start(out=ksf[h * 128:(h + 1) * 128, :], in_=ksfout[:]), 16)
        block.sync(_ksst)

    def _fin(sync):
        STO.wait(sync)
        LDX.wait(sync)
        LDW.wait(sync)
    block.sync(_fin)

    done = (STO.sem, STO.n)
    return {n: done for n in ["qf", "kf", "ksf", "betad", "betatd"]}


@phase("l0v", ins=["xnt", "Wv"], outs=["vt"])
def ph_l0v(ctx, deps):
    block = ctx.block
    A = ctx.aps
    xnt, Wv, vt = A["xnt"], A["Wv"], A["vt"]
    xnf = ctx.sbuf("v_xnf", [128, ND * N], BF16)
    wv = ctx.sbuf("v_wv", [128, ND * N], BF16)
    vrow = ctx.sbuf("v_vrow", [128, N], BF16)
    psv = ctx.psum("v_psv", [128, 512], F32)
    LDX, LDW, CP, CE, STO = (ctx.chain() for _ in range(5))
    dep_xnt = deps.get("xnt") if deps else None

    def _s0(sync):
        if dep_xnt:
            sync.wait_ge(dep_xnt[0], dep_xnt[1])
        for c in range(ND):
            LDX.inc(sync.dma_start(out=xnf[:, c * N:(c + 1) * N],
                                   in_=xnt[:, c * 128:(c + 1) * 128], transpose=True), 16)
    block.sync(_s0)
    ldx0 = LDX.n

    def _b_gp(gpsimd):
        for c in range(ND):
            LDW.inc(gpsimd.dma_start(out=wv[:, c * N:(c + 1) * N],
                                     in_=Wv[c * 128:(c + 1) * 128, :]), 16)
    block.gpsimd(_b_gp)
    ldv = LDW.n

    for tt in range(NT):
        for dv4 in range(4):
            def _pe(tensor, tt=tt, dv4=dv4):
                tensor.wait_ge(LDW.sem, ldv)
                tensor.wait_ge(LDX.sem, ldx0)
                CE.wait(tensor)
                for c in range(ND):
                    mm = tensor.matmul(psv[:],
                                       xnf[:, c * N + tt * 128: c * N + (tt + 1) * 128],
                                       wv[:, c * N + dv4 * 512: c * N + (dv4 + 1) * 512],
                                       start=(c == 0), stop=(c == ND - 1))
                CP.inc(mm)
            block.tensor(_pe)

            def _ve(vector, tt=tt, dv4=dv4):
                CP.wait(vector)
                if dv4 == 0:
                    STO.wait(vector)
                CE.inc(vector.tensor_copy(vrow[:, dv4 * 512:(dv4 + 1) * 512], psv[:]))
            block.vector(_ve)

        def _sy(sync, tt=tt):
            CE.wait(sync)
            STO.inc(sync.dma_start(out=vt[tt * 128:(tt + 1) * 128, :], in_=vrow[:]), 16)
        block.sync(_sy)

    def _fin(sync):
        STO.wait(sync)
        LDX.wait(sync)
        LDW.wait(sync)
    block.sync(_fin)
    return {"vt": (STO.sem, STO.n)}


VARS.update({
    "qf": ((D, N), BF16),
    "kf": ((D, N), BF16),
    "ksf": ((D, N), BF16),
    "vt": ((N, D), BF16),
    "betad": ((16, N), F32),
    "betatd": ((N, 16), F32),
})
CONSTS_SHAPES.update({
    "Wq": ((D, D), F32), "Wk": ((D, D), F32), "Wv": ((D, D), F32), "Wb": ((D, 16), F32),
    "ccq": ((128, N), BF16), "ssq": ((128, N), BF16),
    "cck": ((128, N), BF16), "ssk": ((128, N), BF16),
    "onesc": ((128, 1), F32), "onesr": ((1, 128), F32), "eye_f32": ((128, 128), F32),
})


@phase("l0reg", ins=["kf", "vt", "betatd", "eye_bf", "mtriu"], outs=["vnt"])
def ph_l0reg(ctx, deps):
    block = ctx.block
    A = ctx.aps
    kf, vt, betatd, vnt = A["kf"], A["vt"], A["betatd"], A["vnt"]
    NH, CH, NCH = 16, 64, 32

    kall = ctx.sbuf("rg_kall", [128, NH * N], BF16)
    vall = ctx.sbuf("rg_vall", [128, 16 * D], BF16)
    bth = ctx.sbuf("rg_bth", [64, 32 * NH], F32)
    eye = ctx.sbuf("rg_eye", [128, 128], BF16)
    mtriu = ctx.sbuf("rg_mtriu", [64, 64], F32)
    S_f = ctx.sbuf("rg_Sf", [128, NH * 128], F32)
    S_b = ctx.sbuf("rg_Sb", [128, NH * 128], BF16)
    t1 = ctx.sbuf("rg_t1", [64, NH * 64], F32)
    nt_b = ctx.sbuf("rg_nt", [64, NH * 64], BF16)
    n_b = ctx.sbuf("rg_n", [64, NH * 64], BF16)
    nt2_b = ctx.sbuf("rg_nt2", [64, NH * 64], BF16)
    n2_b = ctx.sbuf("rg_n2", [64, NH * 64], BF16)
    r_b = ctx.sbuf("rg_r", [64, NH * 128], BF16)
    vnb = ctx.sbuf("rg_vnb", [64, NH * 128], BF16)
    kt_b = ctx.sbuf("rg_kt", [64, NH * 128], BF16)
    psbig = ctx.psum("rg_psbig", [64, 2048], F32)
    pstr = ctx.psum("rg_pstr", [64, 2048], BF16)
    psS = ctx.psum("rg_psS", [128, 512], F32)

    LDL, CP, CE, STO = (ctx.chain() for _ in range(4))
    for nm in ("kf", "vt", "betatd"):
        dv = deps.get(nm) if deps else None
        if dv:
            def _w(sync, dv=dv):
                sync.wait_ge(dv[0], dv[1])
            block.sync(_w)

    def _ld(sync):
        for h in range(NH):
            LDL.inc(sync.dma_start(out=kall[:, h * N:(h + 1) * N],
                                   in_=kf[h * 128:(h + 1) * 128, :]), 16)
        LDL.inc(sync.dma_start(out=vall[:].rearrange("p (g d) -> p g d", g=16),
                               in_=vt.rearrange("(g p) d -> p g d", p=128)), 16)
        LDL.inc(sync.dma_start(out=bth[:].rearrange("p (c h) -> p c h", c=32),
                               in_=betatd.rearrange("(c p) h -> p c h", p=64)), 16)
        LDL.inc(sync.dma_start(out=eye[:], in_=A["eye_bf"][:]), 16)
        LDL.inc(sync.dma_start(out=mtriu[:], in_=A["mtriu"][:]), 16)
    block.sync(_ld)

    def _init(vector):
        LDL.wait(vector)
        CE.inc(vector.memset(S_f[:], 0.0))
        CE.wait(vector)
        CE.inc(vector.memset(S_b[:], 0.0))
    block.vector(_init)

    for c in range(NCH):
        po = (c % 2) * 64
        g = c // 2

        def kslice(h, c=c):
            return kall[:, h * N + c * 64: h * N + (c + 1) * 64]

        def _pe_kk(tensor, c=c):
            LDL.wait(tensor)
            CE.wait(tensor)
            for h in range(NH):
                mm = tensor.matmul(psbig[:, h * 64:(h + 1) * 64], kslice(h), kslice(h),
                                   start=True, stop=True)
            CP.inc(mm)
        block.tensor(_pe_kk)

        def _ve_nt(vector, c=c, po=po, g=g):
            CP.wait(vector)
            CE.wait(vector)
            btc = bth[:].rearrange("p (c h) -> p c h", c=32)[:, c, :]
            btb = btc.unsqueeze(2).broadcast_to([64, NH, 64])
            CE.inc(vector.tensor_tensor(
                t1[:].rearrange("p (h j) -> p h j", h=NH),
                psbig[:, 0:1024].rearrange("p (h j) -> p h j", h=NH),
                btb, ALU.mult))
            CE.wait(vector)
            mtb = mtriu[:].unsqueeze(1).broadcast_to([64, NH, 64])
            CE.inc(vector.tensor_tensor(
                nt_b[:].rearrange("p (h j) -> p h j", h=NH),
                t1[:].rearrange("p (h j) -> p h j", h=NH),
                mtb, ALU.mult))
        block.vector(_ve_nt)

        def _pe_tr(tensor, c=c):
            CE.wait(tensor)
            for h in range(NH):
                mm = tensor.transpose(pstr[:, h * 64:(h + 1) * 64],
                                      nt_b[:, h * 64:(h + 1) * 64], eye[0:64, 0:64])
            CP.inc(mm)
        block.tensor(_pe_tr)

        def _ve_n(vector, c=c):
            CP.wait(vector)
            CE.inc(vector.tensor_copy(n_b[:], pstr[:, 0:1024]))
        block.vector(_ve_n)

        def _pe_ks(tensor, c=c):
            CE.wait(tensor)
            for h in range(NH):
                mm = tensor.matmul(psbig[:, h * 128:(h + 1) * 128], kslice(h),
                                   S_b[:, h * 128:(h + 1) * 128], start=True, stop=True)
            CP.inc(mm)
        block.tensor(_pe_ks)

        def _ve_r(vector, c=c, po=po, g=g):
            CP.wait(vector)
            STO.wait(vector)
            vsl = vall[po:po + 64, :].rearrange("p (g d) -> p g d", g=16)[:, g, :]
            CE.inc(vector.tensor_tensor(r_b[:], vsl, psbig[:, 0:2048], ALU.subtract))
        block.vector(_ve_r)

        cn, cnt = n_b, nt_b
        nn, nnt = n2_b, nt2_b
        for lvl in range(6):
            def _pe_f(tensor, cnt=cnt, c=c):
                CE.wait(tensor)
                for h in range(NH):
                    mm = tensor.matmul(psbig[:, h * 128:(h + 1) * 128],
                                       cnt[:, h * 64:(h + 1) * 64],
                                       r_b[:, h * 128:(h + 1) * 128],
                                       start=True, stop=True)
                CP.inc(mm)
            block.tensor(_pe_f)

            def _ve_f(vector, c=c):
                CP.wait(vector)
                CE.inc(vector.tensor_add(r_b[:], r_b[:], psbig[:, 0:2048]))
            block.vector(_ve_f)

            if lvl < 5:
                def _pe_s(tensor, cn=cn, cnt=cnt, c=c):
                    CE.wait(tensor)
                    for h in range(NH):
                        mm = tensor.matmul(psbig[:, h * 64:(h + 1) * 64],
                                           cnt[:, h * 64:(h + 1) * 64],
                                           cn[:, h * 64:(h + 1) * 64],
                                           start=True, stop=True)
                    CP.inc(mm)
                block.tensor(_pe_s)

                def _ve_s(vector, nn=nn, c=c):
                    CP.wait(vector)
                    CE.inc(vector.tensor_copy(nn[:], psbig[:, 0:1024]))
                block.vector(_ve_s)

                def _pe_s2(tensor, nn=nn, c=c):
                    CE.wait(tensor)
                    for h in range(NH):
                        mm = tensor.transpose(pstr[:, h * 64:(h + 1) * 64],
                                              nn[:, h * 64:(h + 1) * 64],
                                              eye[0:64, 0:64])
                    CP.inc(mm)
                block.tensor(_pe_s2)

                def _ve_s2(vector, nnt=nnt, c=c):
                    CP.wait(vector)
                    CE.inc(vector.tensor_copy(nnt[:], pstr[:, 0:1024]))
                block.vector(_ve_s2)
                cn, cnt, nn, nnt = nn, nnt, cn, cnt

        def _ve_vnb(vector, c=c, po=po, g=g):
            CE.wait(vector)
            btc = bth[:].rearrange("p (c h) -> p c h", c=32)[:, c, :]
            btb = btc.unsqueeze(2).broadcast_to([64, NH, 128])
            CE.inc(vector.tensor_tensor(
                vnb[:].rearrange("p (h d) -> p h d", h=NH),
                r_b[:].rearrange("p (h d) -> p h d", h=NH),
                btb, ALU.mult))
        block.vector(_ve_vnb)

        def _sy_vn(sync, c=c):
            CE.wait(sync)
            STO.inc(sync.dma_start(out=vnt[c * 64:(c + 1) * 64, :], in_=r_b[:]), 16)
        block.sync(_sy_vn)

        def _pe_kt(tensor, c=c):
            CE.wait(tensor)
            for h in range(NH):
                mm = tensor.transpose(pstr[:, h * 128:(h + 1) * 128], kslice(h), eye[:])
            CP.inc(mm)
        block.tensor(_pe_kt)

        def _ve_kt(vector, c=c):
            CP.wait(vector)
            CE.inc(vector.tensor_copy(kt_b[:], pstr[:, 0:2048]))
        block.vector(_ve_kt)

        for hg in range(4):
            def _pe_ds(tensor, hg=hg, c=c):
                CE.wait(tensor)
                for j in range(4):
                    h = hg * 4 + j
                    mm = tensor.matmul(psS[:, j * 128:(j + 1) * 128],
                                       kt_b[:, h * 128:(h + 1) * 128],
                                       vnb[:, h * 128:(h + 1) * 128],
                                       start=True, stop=True)
                CP.inc(mm)
            block.tensor(_pe_ds)

            def _ve_ds(vector, hg=hg, c=c):
                CP.wait(vector)
                CE.inc(vector.tensor_add(S_f[:, hg * 512:(hg + 1) * 512],
                                         S_f[:, hg * 512:(hg + 1) * 512], psS[:]))
            block.vector(_ve_ds)

        def _ve_sb(vector, c=c):
            CE.wait(vector)
            CE.inc(vector.tensor_copy(S_b[:], S_f[:]))
        block.vector(_ve_sb)

    def _fin(sync):
        STO.wait(sync)
        LDL.wait(sync)
    block.sync(_fin)
    return {"vnt": (STO.sem, STO.n)}


VARS["vnt"] = ((N, D), BF16)
CONSTS_SHAPES["mtriu"] = ((64, 64), F32)


@phase("l0attn", ins=["qf", "ksf", "vnt", "Wo", "x0", "eye_bf", "cmaskb"],
       outs=["x1a"])
def ph_l0attn(ctx, deps):
    block = ctx.block
    A = ctx.aps
    qf, ksf, vnt, Wo, x0, x1a = (A[n] for n in ["qf", "ksf", "vnt", "Wo", "x0", "x1a"])
    NH = 16

    kbuf = ctx.sbuf("at_kbuf", [128, N], BF16)
    vh = ctx.sbuf("at_vh", [128, 16 * 128], BF16)
    qrow = ctx.sbuf("at_qrow", [128, N], BF16)
    eye = ctx.sbuf("at_eye", [128, 128], BF16)
    cmask = ctx.sbuf("at_cmask", [128, 128], BF16)
    avf = ctx.sbuf("at_avf", [128, NH * N], BF16)
    pb = [ctx.sbuf("at_pb", [128, N], BF16) for _ in range(2)]
    ptall = [ctx.sbuf("at_ptall", [128, N], BF16) for _ in range(2)]
    srow = [ctx.sbuf("at_srow", [128, 4], F32) for _ in range(2)]
    psS = ctx.psum("at_psS", [128, 2048], F32)
    pstr = ctx.psum("at_pstr", [128, 2048], BF16)
    psA = ctx.psum("at_psA", [128, 512], F32)

    LDL, LDG, LDX2, CP, CE, CS, STO = (ctx.chain() for _ in range(7))
    for nm in ("qf", "ksf", "vnt"):
        dv = deps.get(nm) if deps else None
        if dv:
            def _w(sync, dv=dv):
                sync.wait_ge(dv[0], dv[1])
            block.sync(_w)

    def _ld0(sync):
        LDL.inc(sync.dma_start(out=eye[:], in_=A["eye_bf"][:]), 16)
        LDL.inc(sync.dma_start(out=cmask[:], in_=A["cmaskb"][:]), 16)
    block.sync(_ld0)

    SMM, MAXT, EXPT, NRM, TDONE, PTC, AVT, AVC = {}, {}, {}, {}, {}, {}, {}, {}

    def emit_T_AV(h, i):
        par = i % 2

        def _pe_t(tensor, h=h, i=i, par=par):
            tensor.wait_ge(CE.sem, NRM[h, i])
            tensor.wait_ge(CE.sem, PTC.get((h, i - 2), 0))
            for kc in range(i + 1):
                mm = tensor.transpose(pstr[:, kc * 128:(kc + 1) * 128],
                                      pb[par][:, kc * 128:(kc + 1) * 128], eye[:])
            CP.inc(mm)
            TDONE[h, i] = CP.n
        block.tensor(_pe_t)

        def _ve_t(vector, h=h, i=i, par=par, kr=(i + 1) * 128):
            vector.wait_ge(CP.sem, TDONE[h, i])
            vector.wait_ge(CP.sem, AVT.get((h, i - 2), 0))
            CE.inc(vector.tensor_copy(ptall[par][:, 0:kr], pstr[:, 0:kr]))
            PTC[h, i] = CE.n
        block.vector(_ve_t)

        def _pe_av(tensor, h=h, i=i, par=par):
            tensor.wait_ge(CE.sem, PTC[h, i])
            tensor.wait_ge(CE.sem, AVC.get((h, i - 1), 0))
            for kc in range(i + 1):
                mm = tensor.matmul(psA[:, 0:128],
                                   vh[:].rearrange("p (g d) -> p g d", g=16)[:, kc, :],
                                   ptall[par][:, kc * 128:(kc + 1) * 128],
                                   start=(kc == 0), stop=(kc == i))
            CP.inc(mm)
            AVT[h, i] = CP.n
        block.tensor(_pe_av)

        def _ve_av(vector, h=h, i=i):
            vector.wait_ge(CP.sem, AVT[h, i])
            CE.inc(vector.tensor_copy(avf[:, h * N + i * 128: h * N + (i + 1) * 128],
                                      psA[:, 0:128]))
            AVC[h, i] = CE.n
        block.vector(_ve_av)

    for h in range(NH):
        def _ldh(sync, h=h):
            CP.wait(sync)
            LDL.inc(sync.dma_start(out=kbuf[:], in_=ksf[h * 128:(h + 1) * 128, :]), 16)
            LDL.inc(sync.dma_start(
                out=vh[:].rearrange("p (g d) -> p g d", g=16),
                in_=vnt[:, h * 128:(h + 1) * 128].rearrange("(g p) d -> p g d", p=128)), 16)
            LDL.inc(sync.dma_start(out=qrow[:], in_=qf[h * 128:(h + 1) * 128, :]), 16)
        block.sync(_ldh)
        ldh_t = LDL.n

        for i in range(NT):
            kr = (i + 1) * 128
            par = i % 2

            def _pe_s(tensor, h=h, i=i, ldh_t=ldh_t):
                tensor.wait_ge(LDL.sem, ldh_t)
                tensor.wait_ge(CS.sem, EXPT.get((h, i - 1), 0))
                qt = qrow[:, i * 128:(i + 1) * 128]
                f = (i * 128) // 512
                for b in range(f):
                    tensor.matmul(psS[:, b * 512:(b + 1) * 512], qt,
                                  kbuf[:, b * 512:(b + 1) * 512], start=True, stop=True)
                for b in range(i - 4 * f):
                    o = f * 512 + b * 128
                    tensor.matmul(psS[:, o:o + 128], qt, kbuf[:, o:o + 128],
                                  start=True, stop=True)
                o = i * 128
                tensor.matmul(psS[:, o:o + 128], qt, kbuf[:, o:o + 128],
                              start=True, stop=False)
                mm = tensor.matmul(psS[:, o:o + 128], eye[:], cmask[:],
                                   start=False, stop=True)
                CP.inc(mm)
                SMM[h, i] = CP.n
            block.tensor(_pe_s)

            def _ve_m(vector, h=h, i=i, kr=kr, par=par):
                vector.wait_ge(CP.sem, SMM[h, i])
                vector.wait_ge(CS.sem, EXPT.get((h, i - 2), 0))
                CE.inc(vector.reduce_max(srow[par][:, 0:1], psS[:, 0:kr], axis=AX.X,
                                         negate=True))
                MAXT[h, i] = CE.n
            block.vector(_ve_m)

            def _sc_e(scalar, h=h, i=i, kr=kr, par=par):
                scalar.wait_ge(CE.sem, MAXT[h, i])
                scalar.wait_ge(CP.sem, TDONE.get((h, i - 2), 0))
                CS.inc(scalar.activation(pb[par][:, 0:kr], psS[:, 0:kr], AF.Exp,
                                         bias=srow[par][:, 0:1],
                                         accum_out=srow[par][:, 1:2]))
                EXPT[h, i] = CS.n
            block.scalar(_sc_e)

            def _ve_n(vector, h=h, i=i, kr=kr, par=par):
                vector.wait_ge(CS.sem, EXPT[h, i])
                CE.inc(vector.reciprocal(srow[par][:, 2:3], srow[par][:, 1:2]))
                CE.wait(vector)
                CE.inc(vector.tensor_scalar(pb[par][:, 0:kr], pb[par][:, 0:kr],
                                            srow[par][:, 2:3], None, ALU.mult))
                NRM[h, i] = CE.n
            block.vector(_ve_n)

            if i >= 1:
                emit_T_AV(h, i - 1)
        emit_T_AV(h, NT - 1)
    wosb = ctx.sbuf("at_wo", [128, NH * N], BF16)
    x0row = ctx.sbuf("at_x0r", [128, N], F32)
    x1row = ctx.sbuf("at_x1r", [128, N], F32)
    psW = ctx.psum("at_psW", [128, 512], F32)

    def _ldwo(gpsimd):
        for hc in range(NH):
            LDG.inc(gpsimd.dma_start(out=wosb[:, hc * N:(hc + 1) * N],
                                     in_=Wo[hc * 128:(hc + 1) * 128, :]), 16)
    block.gpsimd(_ldwo)
    ldwo = LDG.n

    for i in range(NT):
        def _ldx(sync, i=i):
            STO.wait(sync)
            LDX2.inc(sync.dma_start(out=x0row[:], in_=x0[i * 128:(i + 1) * 128, :]), 16)
        block.sync(_ldx)

        for d4 in range(4):
            def _pe_w(tensor, i=i, d4=d4):
                tensor.wait_ge(LDG.sem, ldwo)
                CE.wait(tensor)
                for hc in range(NH):
                    mm = tensor.matmul(psW[:],
                                       avf[:, hc * N + i * 128: hc * N + (i + 1) * 128],
                                       wosb[:, hc * N + d4 * 512: hc * N + (d4 + 1) * 512],
                                       start=(hc == 0), stop=(hc == NH - 1))
                CP.inc(mm)
            block.tensor(_pe_w)

            def _ve_w(vector, i=i, d4=d4):
                CP.wait(vector)
                LDX2.wait(vector)
                CE.inc(vector.tensor_add(x1row[:, d4 * 512:(d4 + 1) * 512],
                                         x0row[:, d4 * 512:(d4 + 1) * 512], psW[:]))
            block.vector(_ve_w)

        def _st(sync, i=i):
            CE.wait(sync)
            STO.inc(sync.dma_start(out=x1a[i * 128:(i + 1) * 128, :], in_=x1row[:]), 16)
        block.sync(_st)

    def _fin(sync):
        STO.wait(sync)
        LDL.wait(sync)
        LDG.wait(sync)
        LDX2.wait(sync)
    block.sync(_fin)
    return {"x1a": (STO.sem, STO.n)}


VARS["x1a"] = ((N, D), F32)
CONSTS_SHAPES["Wo"] = ((D, D), F32)
CONSTS_SHAPES["cmaskb"] = ((128, 128), BF16)


@phase("l0moe", ins=["xnt", "x1a", "gwt", "sgw", "w_gate", "w_up", "w_down",
                     "sw_gate", "sw_up", "sw_down", "onesr", "eye_f32"],
       outs=["x1b"])
def ph_l0moe(ctx, deps):
    nc, block = ctx.nc, ctx.block
    A = ctx.aps
    xnt, x1a, x1b = A["xnt"], A["x1a"], A["x1b"]
    macc = nc.dram_tensor("macc", [N, D], F32, kind="Internal").ap()
    cmbd = nc.dram_tensor("cmbd", [16, N], F32, kind="Internal").ap()
    HT = N
    NI = 1408
    NIC = NI // 128

    xnf = ctx.sbuf("mo_xnf", [128, ND * HT], BF16)
    gw = ctx.sbuf("mo_gw", [128, ND * 8], BF16)
    sgwb = ctx.sbuf("mo_sgw", [128, ND * 1], BF16)
    onesr = ctx.sbuf("mo_onesr", [1, 128], F32)
    eyef = ctx.sbuf("mo_eyef", [128, 128], F32)
    wg = ctx.sbuf("mo_wg", [128, 2 * ND * 128], BF16)
    wu = ctx.sbuf("mo_wu", [128, 2 * ND * 128], BF16)
    wd = ctx.sbuf("mo_wd", [128, 2 * NIC * 512], BF16)
    hsb = ctx.sbuf("mo_h", [128, NIC * HT], BF16)
    sg = ctx.sbuf("mo_sg", [128, 1024], F32)
    cb = ctx.sbuf("mo_cb", [128, HT], F32)
    crow = ctx.sbuf("mo_crow", [1, HT], F32)
    dall = ctx.sbuf("mo_dall", [128, 16 * 512], F32)
    prob = ctx.sbuf("mo_prob", [128, 16], F32)
    tst = ctx.sbuf("mo_tst", [128, 8], F32)
    ctok = ctx.sbuf("mo_ctok", [128, 16], F32)
    ctr = ctx.sbuf("mo_ctr", [16, 128], F32)
    x0r, x1r = None, None
    x0r = dall[:, 0:N]
    x1r = dall[:, N:2 * N]
    psG = ctx.psum("mo_psG", [128, 1024], F32)
    psU = ctx.psum("mo_psU", [128, 1024], F32)
    psD = ctx.psum("mo_psD", [128, 1024], F32)
    psB = ctx.psum("mo_psB", [128, 512], F32)
    psT = ctx.psum("mo_psT", [16, 128], F32)
    psL = psD

    LDXC, LDXN, LDCR, LDF, LDW, MST, CP, CE, CS, STO = (ctx.chain() for _ in range(10))
    for nm in ("xnt", "x1a"):
        dv = deps.get(nm) if deps else None
        if dv:
            def _w(sync, dv=dv):
                sync.wait_ge(dv[0], dv[1])
            block.sync(_w)

    def _ld0(sync):
        LDXC.inc(sync.dma_start(out=onesr[:], in_=A["onesr"][:]), 16)
        LDXC.inc(sync.dma_start(out=eyef[:], in_=A["eye_f32"][:]), 16)
    block.sync(_ld0)

    def _ldg(gpsimd):
        LDW.inc(gpsimd.dma_start(out=gw[:].rearrange("p (c o) -> p c o", c=ND),
                                 in_=A["gwt"].rearrange("(c p) o -> p c o", p=128)), 16)
        LDW.inc(gpsimd.dma_start(out=sgwb[:], in_=A["sgw"][:]), 16)
        LDW.wait(gpsimd)
    block.gpsimd(_ldg)

    GUK, GUG, DNK, DNG = [0], [0], [0], [0]
    GUMM, SILU, HDONE, DNMM, DEV = {}, {}, {}, {}, {}
    LDWBASE = 32
    pending_acc = [None]

    def flush_acc():
        if pending_acc[0] is not None:
            f = pending_acc[0]
            pending_acc[0] = None
            block.gpsimd(f)
    for s in range(1):
        t0 = 0

        def _ldxn(sync, s=s, t0=t0):
            CP.wait(sync)
            for c in range(ND):
                LDXN.inc(sync.dma_start(out=xnf[:, c * HT:(c + 1) * HT],
                                        in_=xnt[t0:t0 + HT, c * 128:(c + 1) * 128],
                                        transpose=True), 16)
        block.sync(_ldxn)

        for tt in range(16):
            def _pe_l(tensor, s=s, tt=tt):
                LDXN.wait(tensor)
                LDXC.wait(tensor)
                LDW.wait(tensor)
                CE.wait(tensor)
                for c in range(ND):
                    tensor.matmul(psL[:, 0:8],
                                  xnf[:, c * HT + tt * 128: c * HT + (tt + 1) * 128],
                                  gw[:, c * 8:(c + 1) * 8],
                                  start=(c == 0), stop=(c == ND - 1))
                for c in range(ND):
                    mm = tensor.matmul(psL[:, 8:9],
                                       xnf[:, c * HT + tt * 128: c * HT + (tt + 1) * 128],
                                       sgwb[:, c:c + 1],
                                       start=(c == 0), stop=(c == ND - 1))
                CP.inc(mm)
            block.tensor(_pe_l)

            def _ve_sm(vector, s=s, tt=tt):
                CP.wait(vector)
                CE.inc(vector.reduce_max(tst[:, 4:5], psL[:, 0:8], axis=AX.X,
                                         negate=True))
            block.vector(_ve_sm)

            def _sc_sm(scalar, s=s, tt=tt):
                CE.wait(scalar)
                CS.inc(scalar.activation(prob[:, 0:8], psL[:, 0:8], AF.Exp,
                                         bias=tst[:, 4:5], accum_out=tst[:, 5:6]))
                CS.wait(scalar)
                CS.inc(scalar.activation(prob[:, 8:9], psL[:, 8:9], AF.Sigmoid))
            block.scalar(_sc_sm)

            def _ve_c(vector, s=s, tt=tt):
                CS.wait(vector)
                CE.inc(vector.reciprocal(tst[:, 6:7], tst[:, 5:6]))
                CE.wait(vector)
                CE.inc(vector.tensor_scalar(prob[:, 0:8], prob[:, 0:8], tst[:, 6:7],
                                            None, ALU.mult))
                CE.wait(vector)
                CE.inc(vector.reduce_max(tst[:, 0:1], prob[:, 0:8], axis=AX.X))
                CE.wait(vector)
                CE.inc(vector.tensor_scalar(ctok[:, 0:8], prob[:, 0:8], tst[:, 0:1],
                                            None, ALU.is_ge))
                CE.wait(vector)
                CE.inc(vector.scalar_tensor_tensor(ctok[:, 8:16], ctok[:, 0:8], -1e30,
                                                   prob[:, 0:8], ALU.mult, ALU.add))
                CE.wait(vector)
                CE.inc(vector.reduce_max(tst[:, 1:2], ctok[:, 8:16], axis=AX.X))
                CE.wait(vector)
                CE.inc(vector.tensor_add(tst[:, 2:3], tst[:, 0:1], tst[:, 1:2]))
                CE.wait(vector)
                CE.inc(vector.reciprocal(tst[:, 3:4], tst[:, 2:3]))
                CE.wait(vector)
                CE.inc(vector.tensor_scalar(ctok[:, 0:8], prob[:, 0:8], tst[:, 1:2],
                                            None, ALU.is_ge))
                CE.wait(vector)
                CE.inc(vector.tensor_tensor(ctok[:, 0:8], ctok[:, 0:8], prob[:, 0:8],
                                            ALU.mult))
                CE.wait(vector)
                CE.inc(vector.tensor_scalar(ctok[:, 0:8], ctok[:, 0:8], tst[:, 3:4],
                                            None, ALU.mult))
                CE.wait(vector)
                CE.inc(vector.tensor_copy(ctok[:, 8:9], prob[:, 8:9]))
            block.vector(_ve_c)

            def _pe_t(tensor, s=s, tt=tt):
                CE.wait(tensor)
                CP.inc(tensor.transpose(psT[:], ctok[:], eyef[:]))
            block.tensor(_pe_t)

            def _ve_t(vector, s=s, tt=tt):
                CP.wait(vector)
                STO.wait(vector)
                CE.inc(vector.tensor_copy(ctr[:], psT[:, 0:128]))
            block.vector(_ve_t)

            def _sy_t(sync, s=s, tt=tt, t0=t0):
                CE.wait(sync)
                STO.inc(sync.dma_start(out=cmbd[:, t0 + tt * 128: t0 + (tt + 1) * 128],
                                       in_=ctr[:]), 16)
            block.sync(_sy_t)

        for e in range(12):
            if e < 8:
                wgs = A["w_gate"][e]
                wus = A["w_up"][e]
                wds = A["w_down"][e]
                cri = e
            else:
                b = e - 8
                wgs = A["sw_gate"][:, b * NI:(b + 1) * NI]
                wus = A["sw_up"][:, b * NI:(b + 1) * NI]
                wds = A["sw_down"][b * NI:(b + 1) * NI, :]
                cri = 8

            def _ldcr(sync, cri=cri, t0=t0, e=e):
                CP.wait(sync)
                STO.wait(sync)
                LDCR.inc(sync.dma_start(out=crow[:], in_=cmbd[cri:cri + 1, t0:t0 + HT]), 16)
            block.sync(_ldcr)

            for b2 in range(4):
                def _pe_b(tensor, b2=b2, e=e):
                    LDCR.wait(tensor)
                    LDXC.wait(tensor)
                    CE.wait(tensor)
                    CP.inc(tensor.matmul(psB[:], onesr[:],
                                         crow[0:1, b2 * 512:(b2 + 1) * 512],
                                         start=True, stop=True))
                block.tensor(_pe_b)

                def _ve_b(vector, b2=b2, e=e):
                    CP.wait(vector)
                    CE.inc(vector.tensor_copy(cb[:, b2 * 512:(b2 + 1) * 512],
                                              psB[:]))
                block.vector(_ve_b)

            for ic in range(NIC):
                kk = GUK[0]
                GUK[0] += 1
                kb = (kk % 2) * ND * 128

                def _ldw(gpsimd, wgs=wgs, wus=wus, ic=ic, kk=kk, kb=kb):
                    if kk >= 2:
                        gpsimd.wait_ge(CP.sem, GUMM.get(kk - 2, 0))
                    LDW.inc(gpsimd.dma_start(
                        out=wg[:, kb:kb + ND * 128].rearrange("p (c o) -> p c o", c=ND),
                        in_=wgs[:, ic * 128:(ic + 1) * 128]
                        .rearrange("(c p) o -> p c o", p=128)), 16)
                    LDW.inc(gpsimd.dma_start(
                        out=wu[:, kb:kb + ND * 128].rearrange("p (c o) -> p c o", c=ND),
                        in_=wus[:, ic * 128:(ic + 1) * 128]
                        .rearrange("(c p) o -> p c o", p=128)), 16)
                    LDW.wait(gpsimd)
                block.gpsimd(_ldw)
                flush_acc()
                LDWT = LDW.n

                for t4 in range(4):
                    gg = GUG[0]
                    GUG[0] += 1
                    sl = (gg % 2) * 512

                    def _pe_gu(tensor, ic=ic, t4=t4, e=e, kk=kk, kb=kb, gg=gg, sl=sl,
                               LDWT=None):
                        tensor.wait_ge(LDW.sem, LDWT)
                        LDXN.wait(tensor)
                        if gg >= 2:
                            tensor.wait_ge(CS.sem, SILU.get(gg - 2, 0))
                            tensor.wait_ge(CE.sem, HDONE.get(gg - 2, 0))
                        for c in range(ND):
                            tensor.matmul(psG[:, sl:sl + 512],
                                          wg[:, kb + c * 128: kb + (c + 1) * 128],
                                          xnf[:, c * HT + t4 * 512: c * HT + (t4 + 1) * 512],
                                          start=(c == 0), stop=(c == ND - 1))
                        for c in range(ND):
                            mm = tensor.matmul(psU[:, sl:sl + 512],
                                               wu[:, kb + c * 128: kb + (c + 1) * 128],
                                               xnf[:, c * HT + t4 * 512: c * HT + (t4 + 1) * 512],
                                               start=(c == 0), stop=(c == ND - 1))
                        CP.inc(mm)
                        GUMM[kk] = CP.n
                    block.tensor(lambda tensor, f=_pe_gu, L=LDWT, ic=ic, t4=t4, e=e,
                                 kk=kk, kb=kb, gg=gg, sl=sl:
                                 f(tensor, ic=ic, t4=t4, e=e, kk=kk, kb=kb, gg=gg,
                                   sl=sl, LDWT=L))

                    def _sc_g(scalar, gg=gg, sl=sl):
                        CP.wait(scalar)
                        if gg >= 2:
                            scalar.wait_ge(CE.sem, HDONE.get(gg - 2, 0))
                        CS.inc(scalar.activation(sg[:, sl:sl + 512],
                                                 psG[:, sl:sl + 512], AF.Silu))
                        SILU[gg] = CS.n
                    block.scalar(_sc_g)

                    def _ve_h(vector, ic=ic, t4=t4, gg=gg, sl=sl):
                        CS.wait(vector)
                        CE.inc(vector.tensor_tensor(sg[:, sl:sl + 512], sg[:, sl:sl + 512],
                                                    psU[:, sl:sl + 512], ALU.mult))
                        CE.wait(vector)
                        CE.inc(vector.tensor_tensor(
                            hsb[:, ic * HT + t4 * 512: ic * HT + (t4 + 1) * 512],
                            sg[:, sl:sl + 512], cb[:, t4 * 512:(t4 + 1) * 512], ALU.mult))
                        HDONE[gg] = CE.n
                    block.vector(_ve_h)

            hdone_e = CE.n
            for d4 in range(4):
                dk = DNK[0]
                DNK[0] += 1
                db = (dk % 2) * NIC * 512

                def _ldwd(gpsimd, wds=wds, d4=d4, dk=dk, db=db):
                    if dk >= 2:
                        gpsimd.wait_ge(CP.sem, DNMM.get(dk - 2, 0))
                    LDW.inc(gpsimd.dma_start(
                        out=wd[:, db:db + NIC * 512].rearrange("p (c o) -> p c o", c=NIC),
                        in_=wds[:, d4 * 512:(d4 + 1) * 512]
                        .rearrange("(c p) o -> p c o", p=128)), 16)
                    LDW.wait(gpsimd)
                block.gpsimd(_ldwd)
                flush_acc()
                wdtarget = LDW.n

                for tt in range(16):
                    dg = DNG[0]
                    DNG[0] += 1
                    dsl = (dg % 2) * 512

                    def _pe_d(tensor, d4=d4, tt=tt, e=e, dk=dk, db=db, dg=dg, dsl=dsl,
                              wdtarget=wdtarget, hdone_e=hdone_e):
                        tensor.wait_ge(LDW.sem, wdtarget)
                        tensor.wait_ge(CE.sem, hdone_e)
                        if dg >= 2:
                            tensor.wait_ge(CE.sem, DEV.get(dg - 2, 0))
                        for ic in range(NIC):
                            mm = tensor.matmul(
                                psD[:, dsl:dsl + 512],
                                hsb[:, ic * HT + tt * 128: ic * HT + (tt + 1) * 128],
                                wd[:, db + ic * 512: db + (ic + 1) * 512],
                                start=(ic == 0), stop=(ic == NIC - 1))
                        CP.inc(mm)
                        DNMM[dk] = CP.n
                    block.tensor(_pe_d)

                    def _ve_d(vector, d4=d4, tt=tt, e=e, dg=dg, dsl=dsl):
                        CP.wait(vector)
                        MST.wait(vector)
                        CE.inc(vector.tensor_copy(dall[:, tt * 512:(tt + 1) * 512],
                                                  psD[:, dsl:dsl + 512]))
                        DEV[dg] = CE.n
                    block.vector(_ve_d)

                def _acc(gpsimd, d4=d4, e=e, t0=t0):
                    CE.wait(gpsimd)
                    MST.wait(gpsimd)
                    op = ALU.bypass if e == 0 else ALU.add
                    MST.inc(gpsimd.dma_start(
                        out=macc[t0:t0 + HT, d4 * 512:(d4 + 1) * 512]
                        .rearrange("(a p) o -> p a o", p=128),
                        in_=dall[:].rearrange("p (a o) -> p a o", a=16),
                        accum_op=op), 16)
                pending_acc[0] = _acc

    flush_acc()
    for i in range(NT):
        def _ldf(gpsimd, i=i):
            MST.wait(gpsimd)
            CE.wait(gpsimd)
            STO.wait(gpsimd)
            LDF.inc(gpsimd.dma_start(out=x0r[:], in_=x1a[i * 128:(i + 1) * 128, :]), 16)
            LDF.inc(gpsimd.dma_start(out=x1r[:], in_=macc[i * 128:(i + 1) * 128, :]), 16)
            LDF.wait(gpsimd)
        block.gpsimd(_ldf)

        def _vef(vector, i=i):
            LDF.wait(vector)
            CE.inc(vector.tensor_add(x1r[:], x1r[:], x0r[:]))
        block.vector(_vef)

        def _stf(sync, i=i):
            CE.wait(sync)
            STO.inc(sync.dma_start(out=x1b[i * 128:(i + 1) * 128, :], in_=x1r[:]), 16)
        block.sync(_stf)

    def _fin(sync):
        STO.wait(sync)
        LDXC.wait(sync)
        LDXN.wait(sync)
        LDCR.wait(sync)
        LDF.wait(sync)
        LDW.wait(sync)
        MST.wait(sync)
    block.sync(_fin)
    return {"x1b": (STO.sem, STO.n)}


VARS["x1b"] = ((N, D), F32)
CONSTS_SHAPES.update({
    "gwt": ((D, 8), F32), "sgw": ((128, ND), F32),
    "w_gate": ((8, D, 1408), F32), "w_up": ((8, D, 1408), F32),
    "w_down": ((8, 1408, D), F32),
    "sw_gate": ((D, 5632), F32), "sw_up": ((D, 5632), F32),
    "sw_down": ((5632, D), F32),
})


NROW1 = 4416


@phase("l1proj", ins=["xn1t", "W_inT"], outs=["xbcf", "zt"])
def ph_l1proj(ctx, deps):
    block = ctx.block
    A = ctx.aps
    xn1t, W_inT, xbcf, zt = A["xn1t"], A["W_inT"], A["xbcf"], A["zt"]

    xnf = ctx.sbuf("p1_xnf", [128, ND * N], BF16)
    wch = ctx.sbuf("p1_wch", [128, ND * 128], BF16)
    wz = ctx.sbuf("p1_wz", [128, ND * 512], BF16)
    orow = ctx.sbuf("p1_orow", [128, N], F32)
    zrow = ctx.sbuf("p1_zrow", [128, 512 * 4], F32)
    psA = ctx.psum("p1_psA", [128, 512], F32)
    LDX, LDW, CP, CE, STO = (ctx.chain() for _ in range(5))

    dv = deps.get("xn1t") if deps else None
    if dv:
        def _w(sync):
            sync.wait_ge(dv[0], dv[1])
        block.sync(_w)

    def _ldxn(sync):
        for c in range(ND):
            LDX.inc(sync.dma_start(out=xnf[:, c * N:(c + 1) * N],
                                   in_=xn1t[:, c * 128:(c + 1) * 128], transpose=True), 16)
    block.sync(_ldxn)

    for rc in range(35):
        rows = 128 if rc < 34 else 64
        r0 = 4096 + rc * 128

        def _ldw(gpsimd, rc=rc, rows=rows, r0=r0):
            CP.wait(gpsimd)
            LDW.inc(gpsimd.dma_start(
                out=wch[:, 0:ND * rows].rearrange("p (c o) -> p c o", c=ND),
                in_=W_inT[:, r0:r0 + rows].rearrange("(c p) o -> p c o", p=128)), 16)
        block.gpsimd(_ldw)

        for t4 in range(4):
            def _pe(tensor, rc=rc, rows=rows, t4=t4):
                LDW.wait(tensor)
                LDX.wait(tensor)
                CE.wait(tensor)
                for c in range(ND):
                    mm = tensor.matmul(psA[:, 0:512] if rows == 128 else psA[0:64, 0:512],
                                       wch[:, 0:ND * rows]
                                       .rearrange("p (c o) -> p c o", c=ND)[:, c, :],
                                       xnf[:, c * N + t4 * 512: c * N + (t4 + 1) * 512],
                                       start=(c == 0), stop=(c == ND - 1))
                CP.inc(mm)
            block.tensor(_pe)

            def _ve(tensor_v, rc=rc, rows=rows, t4=t4):
                CP.wait(tensor_v)
                if t4 == 0:
                    STO.wait(tensor_v)
                CE.inc(tensor_v.tensor_copy(orow[0:rows, t4 * 512:(t4 + 1) * 512],
                                            psA[0:rows, 0:512]))
            block.vector(_ve)

        def _st(sync, rc=rc, rows=rows):
            CE.wait(sync)
            STO.inc(sync.dma_start(out=xbcf[rc * 128: rc * 128 + rows, :],
                                   in_=orow[0:rows, :]), 16)
        block.sync(_st)

    for zc in range(8):
        def _ldwz(gpsimd, zc=zc):
            CP.wait(gpsimd)
            LDW.inc(gpsimd.dma_start(
                out=wz[:].rearrange("p (c o) -> p c o", c=ND),
                in_=W_inT[:, zc * 512:(zc + 1) * 512].rearrange("(c p) o -> p c o", p=128)), 16)
        block.gpsimd(_ldwz)

        for tt in range(NT):
            def _pe(tensor, zc=zc, tt=tt):
                LDW.wait(tensor)
                LDX.wait(tensor)
                CE.wait(tensor)
                for c in range(ND):
                    mm = tensor.matmul(psA[:],
                                       xnf[:, c * N + tt * 128: c * N + (tt + 1) * 128],
                                       wz[:, c * 512:(c + 1) * 512],
                                       start=(c == 0), stop=(c == ND - 1))
                CP.inc(mm)
            block.tensor(_pe)

            def _ve(vector, zc=zc, tt=tt):
                CP.wait(vector)
                if tt % 4 == 0:
                    STO.wait(vector)
                CE.inc(vector.tensor_copy(zrow[:, (tt % 4) * 512:(tt % 4 + 1) * 512],
                                          psA[:]))
            block.vector(_ve)

            if tt % 4 == 3:
                def _st(sync, zc=zc, tt=tt):
                    CE.wait(sync)
                    STO.inc(sync.dma_start(
                        out=zt[(tt - 3) * 128:(tt + 1) * 128, zc * 512:(zc + 1) * 512]
                        .rearrange("(a p) o -> p a o", p=128),
                        in_=zrow[:].rearrange("p (a o) -> p a o", a=4)), 16)
                block.sync(_st)

    def _fin(sync):
        STO.wait(sync)
        LDX.wait(sync)
        LDW.wait(sync)
    block.sync(_fin)
    return {"xbcf": (STO.sem, STO.n), "zt": (STO.sem, STO.n)}


VARS["xbcf"] = ((NROW1, N), F32)
VARS["zt"] = ((N, 4096), F32)
CONSTS_SHAPES["W_inT"] = ((D, 8512), F32)


@phase("l1conv", ins=["xbcf", "convw", "convb"], outs=["xcb"])
def ph_l1conv(ctx, deps):
    block = ctx.block
    A = ctx.aps
    xbcf, xcb = A["xbcf"], A["xcb"]
    NCC = 34

    xp = ctx.sbuf("cv_xp", [128, N + 4], F32)
    acc = ctx.sbuf("cv_acc", [128, N], F32)
    ob = ctx.sbuf("cv_ob", [128, N], BF16)
    cw = ctx.sbuf("cv_cw", [128, NCC * 4], F32)
    cbb = ctx.sbuf("cv_cb", [128, NCC], F32)
    LDX, CP, CE, CS, STO = (ctx.chain() for _ in range(5))

    dv = deps.get("xbcf") if deps else None
    if dv:
        def _w(sync):
            sync.wait_ge(dv[0], dv[1])
        block.sync(_w)

    def _ld0(sync):
        LDX.inc(sync.dma_start(out=cw[:].rearrange("p (c k) -> p c k", c=NCC),
                               in_=A["convw"].rearrange("(c p) k -> p c k", p=128)), 16)
        LDX.inc(sync.dma_start(out=cbb[:], in_=A["convb"][:]), 16)
    block.sync(_ld0)

    def _init(vector):
        CE.inc(vector.memset(xp[:, 0:4], 0.0))
    block.vector(_init)

    for cc in range(NCC):
        def _ldx(sync, cc=cc):
            CS.wait(sync)
            CE.wait(sync)
            LDX.inc(sync.dma_start(out=xp[:, 3:3 + N],
                                   in_=xbcf[cc * 128:(cc + 1) * 128, :]), 16)
        block.sync(_ldx)

        def _ve(vector, cc=cc):
            LDX.wait(vector)
            CE.wait(vector)
            CE.inc(vector.tensor_scalar(acc[:], xp[:, 0:N],
                                        cw[:, cc * 4:cc * 4 + 1], None, ALU.mult))
            for j in range(1, 4):
                CE.wait(vector)
                CE.inc(vector.scalar_tensor_tensor(acc[:], xp[:, j:j + N],
                                                   cw[:, cc * 4 + j:cc * 4 + j + 1],
                                                   acc[:], ALU.mult, ALU.add))
        block.vector(_ve)

        def _sc(scalar, cc=cc):
            CE.wait(scalar)
            if cc >= 1:
                STO.wait(scalar)
            CS.inc(scalar.activation(ob[:], acc[:], AF.Silu,
                                     bias=cbb[:, cc:cc + 1]))
        block.scalar(_sc)

        def _st(sync, cc=cc):
            CS.wait(sync)
            STO.inc(sync.dma_start(out=xcb[cc * 128:(cc + 1) * 128, :], in_=ob[:]), 16)
        block.sync(_st)

    def _fin(sync):
        STO.wait(sync)
        LDX.wait(sync)
    block.sync(_fin)
    return {"xcb": (STO.sem, STO.n)}


VARS["xcb"] = ((4352, N), BF16)
CONSTS_SHAPES["convw"] = ((4352, 4), F32)
CONSTS_SHAPES["convb"] = ((128, 34), F32)


@phase("l1scan", ins=["xcb", "xbcf", "dt_biasv", "A_logv", "d_rep", "onesr",
                      "eye_bf", "eye_f32", "mnegf"],
       outs=["yd"])
def ph_l1scan(ctx, deps):
    block = ctx.block
    A = ctx.aps
    xcb, xbcf, yd = A["xcb"], A["xbcf"], A["yd"]
    NHm, HDm, Q, NCH = 64, 64, 128, 16
    DS = 4096

    bch = ctx.sbuf("sc_bch", [128, N], BF16)
    cch = ctx.sbuf("sc_cch", [128, N], BF16)
    dtf = ctx.sbuf("sc_dtf", [64, N], F32)
    dtb = ctx.sbuf("sc_dtb", [64, 1], F32)
    alg = ctx.sbuf("sc_alg", [64, 4], F32)
    drep = ctx.sbuf("sc_drep", [1, DS], F32)
    onesr = ctx.sbuf("sc_onesr", [1, 128], F32)
    eyeb = ctx.sbuf("sc_eyeb", [128, 128], BF16)
    eyef = ctx.sbuf("sc_eyef", [128, 128], F32)
    mneg = ctx.sbuf("sc_mneg", [128, 128], F32)
    ones_t = ctx.sbuf("sc_ones", [64, N], F32)
    sp1 = ctx.sbuf("sc_sp1", [64, N], F32)
    sp2 = ctx.sbuf("sc_sp2", [64, N], F32)
    dts = ctx.sbuf("sc_dts", [64, N], F32)
    cg = ctx.sbuf("sc_cg", [64, N], F32)
    ncg = ctx.sbuf("sc_ncg", [64, N], F32)
    cgt = ctx.sbuf("sc_cgt", [128, NCH * 64], F32)
    dtst = ctx.sbuf("sc_dtst", [128, NCH * 64], F32)
    dbc = ctx.sbuf("sc_dbc", [128, DS], F32)
    st_f = ctx.sbuf("sc_stf", [128, DS], F32)
    st_b = ctx.sbuf("sc_stb", [128, DS], BF16)
    klhs = ctx.sbuf("sc_klhs", [2, 1024], F32)
    krhs = ctx.sbuf("sc_krhs", [2, 1024], F32)
    csr = ctx.sbuf("sc_csr", [1, 64], F32)
    cer = ctx.sbuf("sc_cer", [1, 64], F32)
    erow = ctx.sbuf("sc_erow", [1, 64], F32)
    ecirel = ctx.sbuf("sc_eci", [128, 64], F32)
    wfac = ctx.sbuf("sc_wfac", [128, 64], F32)
    etb = ctx.sbuf("sc_etb", [128, 64], F32)
    tsc1 = ctx.sbuf("sc_tsc1", [128, 64], F32)
    xt_tok = ctx.sbuf("sc_xtok", [128, DS], BF16)
    xw = ctx.sbuf("sc_xw", [128, DS], BF16)
    xdt = ctx.sbuf("sc_xdt", [128, DS], BF16)
    btok = ctx.sbuf("sc_btok", [128, 128], BF16)
    dq = ctx.sbuf("sc_dq", [128, 1024], F32)
    dex = ctx.sbuf("sc_dex", [128, 1024], F32)
    mgt = ctx.sbuf("sc_mgt", [128, 1024], BF16)
    ty = ctx.sbuf("sc_ty", [128, 512], F32)
    ty2 = ctx.sbuf("sc_ty2", [128, 512], F32)
    yrow = ctx.sbuf("sc_yrow", [128, DS], F32)
    psD = ctx.psum("sc_psD", [128, 1024], F32)
    psY = ctx.psum("sc_psY", [128, 512], F32)
    psI = ctx.psum("sc_psI", [128, 512], F32)
    psS2 = ctx.psum("sc_psS2", [128, 512], F32)
    psG2 = ctx.psum("sc_psG2", [128, 128], F32)
    psB2 = ctx.psum("sc_psB2", [128, 64], F32)
    pstr2 = ctx.psum("sc_pstr2", [128, 128], BF16)

    LDC, LDR, LDT, CP, CE, CS, STO = (ctx.chain() for _ in range(7))
    nc = ctx.nc
    cgd = nc.dram_tensor("sc_cgd", [64, N], F32, kind="Internal").ap()
    ncgd = nc.dram_tensor("sc_ncgd", [64, N], F32, kind="Internal").ap()
    cgtd = nc.dram_tensor("sc_cgtd", [N, 64], F32, kind="Internal").ap()

    for nm in ("xcb", "xbcf"):
        dv = deps.get(nm) if deps else None
        if dv:
            def _w(sync, dv=dv):
                sync.wait_ge(dv[0], dv[1])
            block.sync(_w)

    def _ld0(sync):
        LDC.inc(sync.dma_start(out=bch[:], in_=xcb[4096:4224, :]), 16)
        LDC.inc(sync.dma_start(out=cch[:], in_=xcb[4224:4352, :]), 16)
        LDC.inc(sync.dma_start(out=dtf[:], in_=xbcf[4352:4416, :]), 16)
        LDC.inc(sync.dma_start(out=dtb[:], in_=A["dt_biasv"][:]), 16)
        LDC.inc(sync.dma_start(out=alg[:, 0:1], in_=A["A_logv"][:]), 16)
        LDC.inc(sync.dma_start(out=drep[:], in_=A["d_rep"][:]), 16)
        LDC.inc(sync.dma_start(out=onesr[:], in_=A["onesr"][:]), 16)
        LDC.inc(sync.dma_start(out=eyeb[:], in_=A["eye_bf"][:]), 16)
        LDC.inc(sync.dma_start(out=eyef[:], in_=A["eye_f32"][:]), 16)
        LDC.inc(sync.dma_start(out=mneg[:], in_=A["mnegf"][:]), 16)
    block.sync(_ld0)

    def _sp_sc(scalar):
        LDC.wait(scalar)
        CS.inc(scalar.activation(sp1[:], dtf[:], AF.Relu, bias=dtb[:, 0:1]))
        CS.wait(scalar)
        CS.inc(scalar.activation(sp2[:], dtf[:], AF.Abs, bias=dtb[:, 0:1]))
        CS.wait(scalar)
        CS.inc(scalar.activation(sp2[:], sp2[:], AF.Exp, scale=-1.0))
    block.scalar(_sp_sc)

    def _sp_ve(vector):
        CS.wait(vector)
        CE.inc(vector.tensor_scalar(sp2[:], sp2[:], 1.0, None, ALU.add))
    block.vector(_sp_ve)

    def _sp_sc2(scalar):
        CE.wait(scalar)
        CS.inc(scalar.activation(sp2[:], sp2[:], AF.Ln))
        CS.wait(scalar)
        CS.inc(scalar.activation(alg[:, 1:2], alg[:, 0:1], AF.Exp))
    block.scalar(_sp_sc2)

    def _sp_ve2(vector):
        CS.wait(vector)
        CE.inc(vector.tensor_add(dts[:], sp1[:], sp2[:]))
        CE.wait(vector)
        CE.inc(vector.tensor_scalar(alg[:, 2:3], alg[:, 1:2], -1.0, None, ALU.mult))
        CE.wait(vector)
        CE.inc(vector.tensor_scalar(sp1[:], dts[:], alg[:, 2:3], None, ALU.mult))
        CE.wait(vector)
        CE.inc(vector.memset(ones_t[:], 1.0))
        CE.wait(vector)
        CE.inc(vector.tensor_tensor_scan(cg[:], ones_t[:], sp1[:], 0.0,
                                         ALU.mult, ALU.add))
        CE.wait(vector)
        CE.inc(vector.tensor_scalar(ncg[:], cg[:], -1.0, None, ALU.mult))
        CE.wait(vector)
        CE.inc(vector.memset(klhs[:], 1.0))
        CE.wait(vector)
        CE.inc(vector.memset(krhs[:], 1.0))
        CE.wait(vector)
        CE.inc(vector.memset(st_f[:], 0.0))
        CE.wait(vector)
        CE.inc(vector.memset(st_b[:], 0.0))
    block.vector(_sp_ve2)

    def _stg(sync):
        CE.wait(sync)
        STO.inc(sync.dma_start(out=cgd[:], in_=cg[:]), 16)
        STO.inc(sync.dma_start(out=ncgd[:], in_=ncg[:]), 16)
    block.sync(_stg)

    for c in range(NCH):
        def _tr(tensor, c=c):
            CE.wait(tensor)
            CP.inc(tensor.transpose(pstr2[:].bitcast(F32)[:, 0:64] if False else psB2[:],
                                    cg[:, c * 128:(c + 1) * 128], eyef[0:64, 0:64]))
            CP.wait(tensor)
            CP.inc(tensor.transpose(psG2[:, 0:64], dts[:, c * 128:(c + 1) * 128],
                                    eyef[0:64, 0:64]))
        block.tensor(_tr)

        def _cp(vector, c=c):
            CP.wait(vector)
            STO.wait(vector)
            CE.inc(vector.tensor_copy(cgt[:, c * 64:(c + 1) * 64], psB2[:]))
            CE.wait(vector)
            CE.inc(vector.tensor_copy(dtst[:, c * 64:(c + 1) * 64], psG2[:, 0:64]))
        block.vector(_cp)

        def _st(sync, c=c):
            CE.wait(sync)
            STO.inc(sync.dma_start(out=cgtd[c * 128:(c + 1) * 128, :],
                                   in_=cgt[:, c * 64:(c + 1) * 64]), 16)
        block.sync(_st)

    for b8 in range(8):
        def _pe(tensor, b8=b8):
            LDC.wait(tensor)
            CE.wait(tensor)
            CP.inc(tensor.matmul(psY[:], onesr[:], drep[0:1, b8 * 512:(b8 + 1) * 512],
                                 start=True, stop=True))
        block.tensor(_pe)

        def _ve(vector, b8=b8):
            CP.wait(vector)
            CE.inc(vector.tensor_copy(dbc[:, b8 * 512:(b8 + 1) * 512], psY[:]))
        block.vector(_ve)

    for c in range(NCH):
        def _ldr(sync, c=c):
            CP.wait(sync)
            CE.wait(sync)
            STO.wait(sync)
            if c > 0:
                LDR.inc(sync.dma_start(out=csr[:], in_=cgtd[c * 128 - 1:c * 128, :]), 16)
            LDR.inc(sync.dma_start(out=cer[:], in_=cgtd[(c + 1) * 128 - 1:(c + 1) * 128, :]), 16)
        block.sync(_ldr)

        def _ldt(sync, c=c):
            CE.wait(sync)
            LDT.inc(sync.dma_start(out=xt_tok[:], in_=xcb[0:4096, c * 128:(c + 1) * 128],
                                   transpose=True), 16)
        block.sync(_ldt)

        def _pe_cs(tensor, c=c):
            LDR.wait(tensor)
            CE.wait(tensor)
            if c > 0:
                tensor.matmul(psB2[:], onesr[:], csr[0:1, :], start=True, stop=True)
            CP.inc(tensor.matmul(psI[:, 0:64], onesr[:], cer[0:1, :], start=True, stop=True))
        block.tensor(_pe_cs)

        def _ve_cs(vector, c=c):
            CP.wait(vector)
            if c > 0:
                CE.inc(vector.tensor_tensor(tsc1[:], cgt[:, c * 64:(c + 1) * 64],
                                            psB2[:], ALU.subtract))
            else:
                CE.inc(vector.tensor_copy(tsc1[:], cgt[:, 0:64]))
            CE.wait(vector)
            CE.inc(vector.tensor_tensor(wfac[:], psI[:, 0:64],
                                        cgt[:, c * 64:(c + 1) * 64], ALU.subtract))
        block.vector(_ve_cs)

        def _sc_cs(scalar, c=c):
            CE.wait(scalar)
            CS.inc(scalar.activation(ecirel[:], tsc1[:], AF.Exp))
            CS.wait(scalar)
            CS.inc(scalar.activation(wfac[:], wfac[:], AF.Exp))
        block.scalar(_sc_cs)

        def _ve_cs2(vector, c=c):
            CS.wait(vector)
            CE.inc(vector.tensor_tensor(wfac[:], wfac[:], dtst[:, c * 64:(c + 1) * 64],
                                        ALU.mult))
            if c > 0:
                CE.wait(vector)
                CE.inc(vector.tensor_tensor(erow[:], cer[:], csr[:], ALU.subtract))
        block.vector(_ve_cs2)

        def _sc_cs2(scalar, c=c):
            CE.wait(scalar)
            CS.inc(scalar.activation(erow[:], erow[:] if c > 0 else cer[:], AF.Exp))
        block.scalar(_sc_cs2)

        def _pe_et(tensor, c=c):
            CS.wait(tensor)
            CE.wait(tensor)
            CP.inc(tensor.matmul(psB2[:], onesr[:], erow[:], start=True, stop=True))
        block.tensor(_pe_et)

        def _ve_et(vector, c=c):
            CP.wait(vector)
            CE.inc(vector.tensor_copy(etb[:], psB2[:]))
        block.vector(_ve_et)

        def _ve_x(vector, c=c):
            LDT.wait(vector)
            CE.wait(vector)
            wb3 = wfac[:].unsqueeze(2).broadcast_to([128, 64, 64])
            CE.inc(vector.tensor_tensor(xw[:].rearrange("p (h d) -> p h d", h=64),
                                        xt_tok[:].rearrange("p (h d) -> p h d", h=64),
                                        wb3, ALU.mult))
            CE.wait(vector)
            db3 = dtst[:, c * 64:(c + 1) * 64].unsqueeze(2).broadcast_to([128, 64, 64])
            CE.inc(vector.tensor_tensor(xdt[:].rearrange("p (h d) -> p h d", h=64),
                                        xt_tok[:].rearrange("p (h d) -> p h d", h=64),
                                        db3, ALU.mult))
        block.vector(_ve_x)

        def _pe_bg(tensor, c=c):
            LDC.wait(tensor)
            CE.wait(tensor)
            CP.inc(tensor.transpose(pstr2[:], bch[:, c * 128:(c + 1) * 128], eyeb[:]))
            CP.wait(tensor)
            CP.inc(tensor.matmul(psG2[:], bch[:, c * 128:(c + 1) * 128],
                                 cch[:, c * 128:(c + 1) * 128], start=True, stop=True))
        block.tensor(_pe_bg)

        def _ve_bt(vector, c=c):
            CP.wait(vector)
            CE.inc(vector.tensor_copy(btok[:], pstr2[:]))
        block.vector(_ve_bt)

        for hg in range(8):
            h0 = hg * 8

            def _ld_k(sync, c=c, h0=h0):
                CP.wait(sync)
                LDR.inc(sync.dma_start(
                    out=klhs[0:1, :].rearrange("p (h t) -> p h t", h=8),
                    in_=ncgd[h0:h0 + 8, c * 128:(c + 1) * 128].unsqueeze(0)), 16)
                LDR.inc(sync.dma_start(
                    out=krhs[1:2, :].rearrange("p (h t) -> p h t", h=8),
                    in_=cgd[h0:h0 + 8, c * 128:(c + 1) * 128].unsqueeze(0)), 16)
            block.sync(_ld_k)

            def _pe_d(tensor, c=c, h0=h0):
                LDR.wait(tensor)
                CE.wait(tensor)
                for j in range(8):
                    mm = tensor.matmul(psD[:, j * 128:(j + 1) * 128],
                                       klhs[:, j * 128:(j + 1) * 128],
                                       krhs[:, j * 128:(j + 1) * 128],
                                       start=True, stop=True)
                CP.inc(mm)
            block.tensor(_pe_d)

            def _ve_d(vector, c=c, h0=h0):
                CP.wait(vector)
                mb3 = mneg[:].unsqueeze(1).broadcast_to([128, 8, 128])
                CE.inc(vector.tensor_tensor(dq[:].rearrange("p (j t) -> p j t", j=8),
                                            psD[:].rearrange("p (j t) -> p j t", j=8),
                                            mb3, ALU.add))
            block.vector(_ve_d)

            def _sc_d(scalar, c=c, h0=h0):
                CE.wait(scalar)
                CS.inc(scalar.activation(dex[:], dq[:], AF.Exp))
            block.scalar(_sc_d)

            def _ve_mgt(vector, c=c, h0=h0):
                CS.wait(vector)
                gb3 = psG2[:].unsqueeze(1).broadcast_to([128, 8, 128])
                CE.inc(vector.tensor_tensor(mgt[:].rearrange("p (j t) -> p j t", j=8),
                                            dex[:].rearrange("p (j t) -> p j t", j=8),
                                            gb3, ALU.mult))
            block.vector(_ve_mgt)

            def _pe_y(tensor, c=c, h0=h0):
                CE.wait(tensor)
                for j in range(8):
                    h = h0 + j
                    tensor.matmul(psY[:, j * 64:(j + 1) * 64],
                                  mgt[:, j * 128:(j + 1) * 128],
                                  xdt[:, h * 64:(h + 1) * 64], start=True, stop=True)
                for j in range(8):
                    h = h0 + j
                    tensor.matmul(psI[:, j * 64:(j + 1) * 64],
                                  cch[:, c * 128:(c + 1) * 128],
                                  st_b[:, h * 64:(h + 1) * 64], start=True, stop=True)
                for j in range(8):
                    h = h0 + j
                    mm = tensor.matmul(psS2[:, j * 64:(j + 1) * 64], btok[:],
                                       xw[:, h * 64:(h + 1) * 64], start=True, stop=True)
                CP.inc(mm)
            block.tensor(_pe_y)

            def _ve_y(vector, c=c, h0=h0, hg=hg):
                CP.wait(vector)
                if hg == 0:
                    STO.wait(vector)
                eb3 = ecirel[:, h0:h0 + 8].unsqueeze(2).broadcast_to([128, 8, 64])
                CE.inc(vector.tensor_tensor(ty[:].rearrange("p (j d) -> p j d", j=8),
                                            psI[:].rearrange("p (j d) -> p j d", j=8),
                                            eb3, ALU.mult))
                CE.wait(vector)
                CE.inc(vector.tensor_tensor(ty2[:], xt_tok[:, h0 * 64:(h0 + 8) * 64],
                                            dbc[:, h0 * 64:(h0 + 8) * 64], ALU.mult))
                CE.wait(vector)
                CE.inc(vector.tensor_add(ty[:], ty[:], ty2[:]))
                CE.wait(vector)
                CE.inc(vector.tensor_add(yrow[:, h0 * 64:(h0 + 8) * 64], ty[:], psY[:]))
                CE.wait(vector)
                et3 = etb[:, h0:h0 + 8].unsqueeze(2).broadcast_to([128, 8, 64])
                CE.inc(vector.tensor_tensor(
                    st_f[:, h0 * 64:(h0 + 8) * 64].rearrange("p (j d) -> p j d", j=8),
                    st_f[:, h0 * 64:(h0 + 8) * 64].rearrange("p (j d) -> p j d", j=8),
                    et3, ALU.mult))
                CE.wait(vector)
                CE.inc(vector.tensor_add(st_f[:, h0 * 64:(h0 + 8) * 64],
                                         st_f[:, h0 * 64:(h0 + 8) * 64], psS2[:]))
                CE.wait(vector)
                CE.inc(vector.tensor_copy(st_b[:, h0 * 64:(h0 + 8) * 64],
                                          st_f[:, h0 * 64:(h0 + 8) * 64]))
            block.vector(_ve_y)

        def _st_y(sync, c=c):
            CE.wait(sync)
            STO.inc(sync.dma_start(out=yd[c * 128:(c + 1) * 128, :], in_=yrow[:]), 16)
        block.sync(_st_y)

    def _fin(sync):
        STO.wait(sync)
        LDC.wait(sync)
        LDR.wait(sync)
        LDT.wait(sync)
    block.sync(_fin)
    return {"yd": (STO.sem, STO.n)}


VARS["yd"] = ((N, 4096), F32)
CONSTS_SHAPES.update({
    "dt_biasv": ((64, 1), F32), "A_logv": ((64, 1), F32),
    "d_rep": ((1, 4096), F32), "mnegf": ((128, 128), F32),
})


@phase("l1post", ins=["yd", "zt", "W_outT", "norm_wv", "rowsel", "onesr", "eye_bf"],
       outs=["att1h", "ynd"])
def ph_l1post(ctx, deps):
    block = ctx.block
    A = ctx.aps
    yd, zt, att1h, ynd = A["yd"], A["zt"], A["att1h"], A["ynd"]
    DS = 4096

    ix = ctx.sbuf("po_ix", [128, 8], mybir.dt.int32)
    nwrow = ctx.sbuf("po_nwr", [1, DS], F32)
    nwb = ctx.sbuf("po_nwb", [128, DS], F32)
    onesr = ctx.sbuf("po_onesr", [1, 128], F32)
    yrow = ctx.sbuf("po_yrow", [128, DS], F32)
    zrow = ctx.sbuf("po_zrow", [128, DS], F32)
    sq = ctx.sbuf("po_sq", [128, DS], F32)
    stt = ctx.sbuf("po_stt", [128, 4], F32)
    ynb = ctx.sbuf("po_ynb", [128, DS], BF16)
    psB = ctx.psum("po_psB", [128, 512], F32)
    LDI, LDG, CP, CE, CS, STO = (ctx.chain() for _ in range(6))

    for nm in ("yd", "zt"):
        dv = deps.get(nm) if deps else None
        if dv:
            def _w(sync, dv=dv):
                sync.wait_ge(dv[0], dv[1])
            block.sync(_w)

    def _ld0(sync):
        LDI.inc(sync.dma_start(out=ix[:], in_=A["rowsel"][:]), 16)
        LDI.inc(sync.dma_start(out=nwrow[:], in_=A["norm_wv"][:]), 16)
        LDI.inc(sync.dma_start(out=onesr[:], in_=A["onesr"][:]), 16)
    block.sync(_ld0)

    for b8 in range(8):
        def _pe(tensor, b8=b8):
            LDI.wait(tensor)
            CE.wait(tensor)
            CP.inc(tensor.matmul(psB[:], onesr[:], nwrow[0:1, b8 * 512:(b8 + 1) * 512],
                                 start=True, stop=True))
        block.tensor(_pe)

        def _ve(vector, b8=b8):
            CP.wait(vector)
            CE.inc(vector.tensor_copy(nwb[:, b8 * 512:(b8 + 1) * 512], psB[:]))
        block.vector(_ve)

    for g in range(8):
        def _gth(gpsimd, g=g):
            CE.wait(gpsimd)
            CS.wait(gpsimd)
            LDG.inc(gpsimd.indirect_dma_start(
                yrow[:], None, yd[:, :], IndirectOffsetOnAxis(ix[:, g:g + 1], 0)), 16)
            LDG.inc(gpsimd.indirect_dma_start(
                zrow[:], None, zt[:, :], IndirectOffsetOnAxis(ix[:, g:g + 1], 0)), 16)
        block.gpsimd(_gth)

        def _sc(scalar, g=g):
            LDG.wait(scalar)
            CS.inc(scalar.activation(zrow[:], zrow[:], AF.Silu))
        block.scalar(_sc)

        def _ve(vector, g=g):
            CS.wait(vector)
            CE.inc(vector.tensor_tensor(yrow[:], yrow[:], zrow[:], ALU.mult))
        block.vector(_ve)

        def _sc2(scalar, g=g):
            CE.wait(scalar)
            CS.inc(scalar.activation(sq[:], yrow[:], AF.Square, accum_out=stt[:, 0:1]))
        block.scalar(_sc2)

        def _ve2(vector, g=g):
            CS.wait(vector)
            CE.inc(vector.tensor_scalar(stt[:, 1:2], stt[:, 0:1], 1.0 / DS, None,
                                        ALU.mult))
            CE.wait(vector)
            CE.inc(vector.tensor_scalar(stt[:, 1:2], stt[:, 1:2], 1e-5, None, ALU.add))
        block.vector(_ve2)

        def _sc3(scalar, g=g):
            CE.wait(scalar)
            CS.inc(scalar.activation(stt[:, 2:3], stt[:, 1:2], AF.Ln))
            CS.wait(scalar)
            CS.inc(scalar.activation(stt[:, 2:3], stt[:, 2:3], AF.Exp, scale=-0.5))
        block.scalar(_sc3)

        def _ve3(vector, g=g):
            CS.wait(vector)
            if g >= 1:
                STO.wait(vector)
            CE.inc(vector.tensor_scalar(yrow[:], yrow[:], stt[:, 2:3], None, ALU.mult))
            CE.wait(vector)
            CE.inc(vector.tensor_tensor(ynb[:], yrow[:], nwb[:], ALU.mult))
        block.vector(_ve3)

        def _st(sync, g=g):
            CE.wait(sync)
            STO.inc(sync.dma_start(out=ynd[g * 128:(g + 1) * 128, :], in_=ynb[:]), 16)
        block.sync(_st)

    wog = ctx.sbuf("po_wog", [128, 8 * N], BF16)
    ynf = ctx.sbuf("po_ynf", [128, 8 * 128], BF16)
    drow = ctx.sbuf("po_drow", [128, N], F32)
    psW = ctx.psum("po_psW", [128, 512], F32)
    LDW2, LDT2, MST = (ctx.chain() for _ in range(3))

    for cg in range(4):
        def _ldw(gpsimd, cg=cg):
            CP.wait(gpsimd)
            MST.wait(gpsimd)
            for j in range(8):
                LDW2.inc(gpsimd.dma_start(
                    out=wog[:, j * N:(j + 1) * N],
                    in_=A["W_outT"][(cg * 8 + j) * 128:(cg * 8 + j + 1) * 128, :]), 16)
        block.gpsimd(_ldw)

        for g in range(8):
            def _ldt(sync, cg=cg, g=g):
                CP.wait(sync)
                STO.wait(sync)
                for j in range(8):
                    LDT2.inc(sync.dma_start(
                        out=ynf[:, j * 128:(j + 1) * 128],
                        in_=ynd[g * 128:(g + 1) * 128,
                                (cg * 8 + j) * 128:(cg * 8 + j + 1) * 128],
                        transpose=True), 16)
            block.sync(_ldt)

            for d4 in range(4):
                def _pe(tensor, cg=cg, g=g, d4=d4):
                    LDW2.wait(tensor)
                    LDT2.wait(tensor)
                    CE.wait(tensor)
                    for j in range(8):
                        mm = tensor.matmul(psW[:], ynf[:, j * 128:(j + 1) * 128],
                                           wog[:, j * N + d4 * 512: j * N + (d4 + 1) * 512],
                                           start=(j == 0), stop=(j == 7))
                    CP.inc(mm)
                block.tensor(_pe)

                def _ve(vector, cg=cg, g=g, d4=d4):
                    CP.wait(vector)
                    if d4 == 0:
                        MST.wait(vector)
                    CE.inc(vector.tensor_copy(drow[:, d4 * 512:(d4 + 1) * 512], psW[:]))
                block.vector(_ve)

            def _acc(gpsimd, cg=cg, g=g):
                CE.wait(gpsimd)
                MST.wait(gpsimd)
                op = ALU.bypass if cg == 0 else ALU.add
                MST.inc(gpsimd.dma_start(out=att1h[g * 128:(g + 1) * 128, :],
                                         in_=drow[:], accum_op=op), 16)
            block.gpsimd(_acc)

    def _fin(sync):
        STO.wait(sync)
        LDI.wait(sync)
        LDG.wait(sync)
        LDW2.wait(sync)
        LDT2.wait(sync)
        MST.wait(sync)
    block.sync(_fin)
    return {"att1h": (MST.sem, MST.n), "ynd": (STO.sem, STO.n)}


VARS["att1h"] = ((1024, D), F32)
VARS["ynd"] = ((1024, 4096), BF16)
CONSTS_SHAPES.update({
    "W_outT": ((4096, D), F32), "norm_wv": ((1, 4096), F32),
    "rowsel": ((128, 8), mybir.dt.int32),
})


HH1 = 8192


@phase("l1quant", ins=["w_gate_up", "w_proj"], outs=["whgud", "whpd"])
def ph_l1quant(ctx, deps):
    block = ctx.block
    A = ctx.aps

    wqf = ctx.sbuf("fq_wqf", [128, 2 * HH1], F32)
    wvb = [ctx.sbuf("fq_wvb", [128, HH1], BF16) for _ in range(2)]
    absw_b = [ctx.sbuf("fq_absw", [128, HH1], BF16) for _ in range(2)]
    sgn_b = [ctx.sbuf("fq_sgn", [128, HH1], BF16) for _ in range(2)]
    gs = ctx.sbuf("fq_gs", [128, 128], F32)
    gsb = ctx.sbuf("fq_gsb", [128, 128], BF16)
    hgs = ctx.sbuf("fq_hgs", [128, 128], F32)
    what = [ctx.sbuf("fq_what", [128, HH1], BF16) for _ in range(2)]
    LDQ0, LDQ1, CE, CS, STO = (ctx.chain() for _ in range(5))
    SC_DONE, DVE_DONE = {}, {}

    jobs = []
    for rc in range(128):
        jobs.append((A["w_gate_up"], A["whgud"], rc, D))
    for rc in range(16):
        jobs.append((A["w_proj"], A["whpd"], rc, HH1))

    for j, (wsrc, wdst, rc, ncols) in enumerate(jobs):
        LQ = LDQ0 if j % 2 == 0 else LDQ1
        s = j % 2
        ng = ncols // 64
        wv = wqf[:, s * HH1:s * HH1 + ncols]
        absw = absw_b[s]
        sgn = sgn_b[s]
        dste = absw

        def _ldq(sync, j=j, wsrc=wsrc, rc=rc, ncols=ncols, LQ=LQ, s=s):
            if j >= 2:
                sync.wait_ge(CS.sem, SC_DONE.get(j - 2, 0))
            LQ.inc(sync.dma_start(out=wqf[:, s * HH1:s * HH1 + ncols],
                                  in_=wsrc[rc * 128:(rc + 1) * 128, :]), 16)
        block.sync(_ldq)

        def _sc(scalar, j=j, wv=wv, ncols=ncols, LQ=LQ, s=s, absw=absw, sgn=sgn):
            scalar.wait_ge(LQ.sem, 16 * (j // 2 + 1))
            if j >= 2:
                scalar.wait_ge(CE.sem, DVE_DONE.get(j - 2, 0))
            CS.inc(scalar.activation(wvb[s][:, 0:ncols], wv, AF.Copy))
            CS.wait(scalar)
            CS.inc(scalar.activation(absw[:, 0:ncols], wv, AF.Abs))
            CS.wait(scalar)
            CS.inc(scalar.activation(sgn[:, 0:ncols], wv, AF.Sign))
            SC_DONE[j] = CS.n
        block.scalar(_sc)
        sc_done = CS.n

        def _ve(vector, j=j, ncols=ncols, ng=ng, s=s, sc_done=sc_done,
                absw=absw, sgn=sgn, dste=dste):
            vector.wait_ge(CS.sem, sc_done)
            CE.inc(vector.tensor_reduce(
                gs[:, 0:ng], absw[:, 0:ncols].rearrange("p (g k) -> p g k", k=64),
                AX.X, ALU.add))
            CE.wait(vector)
            CE.inc(vector.tensor_scalar(gs[:, 0:ng], gs[:, 0:ng], 1.0 / 64, None,
                                        ALU.mult))
            CE.wait(vector)
            CE.inc(vector.tensor_scalar(gs[:, 0:ng], gs[:, 0:ng], 1e-8, None, ALU.max))
            CE.wait(vector)
            CE.inc(vector.tensor_copy(gsb[:, 0:ng], gs[:, 0:ng]))
            CE.wait(vector)
            CE.inc(vector.tensor_scalar(hgs[:, 0:ng], gsb[:, 0:ng], 0.501953125,
                                        None, ALU.mult))
            CE.wait(vector)
            CE.inc(vector.tensor_tensor(
                absw[:, 0:ncols].rearrange("p (g k) -> p g k", k=64),
                absw[:, 0:ncols].rearrange("p (g k) -> p g k", k=64),
                hgs[:, 0:ng].unsqueeze(2).broadcast_to([128, ng, 64]), ALU.is_gt))
            CE.wait(vector)
            CE.inc(vector.tensor_tensor(
                absw[:, 0:ncols].rearrange("p (g k) -> p g k", k=64),
                absw[:, 0:ncols].rearrange("p (g k) -> p g k", k=64),
                gsb[:, 0:ng].unsqueeze(2).broadcast_to([128, ng, 64]), ALU.mult))
            CE.wait(vector)
            if j >= 2:
                vector.wait_ge(STO.sem, 32 * (j // 2))
            CE.inc(vector.tensor_tensor(what[s][:, 0:ncols], sgn[:, 0:ncols],
                                        absw[:, 0:ncols], ALU.mult))
            CE.wait(vector)
            CE.inc(vector.tensor_tensor(dste[:, 0:ncols], what[s][:, 0:ncols],
                                        wvb[s][:, 0:ncols], ALU.subtract))
            CE.wait(vector)
            CE.inc(vector.tensor_add(what[s][:, 0:ncols], wvb[s][:, 0:ncols],
                                     dste[:, 0:ncols]))
            DVE_DONE[j] = CE.n
        block.vector(_ve)
        dv_done = CE.n

        def _st(sync, j=j, wdst=wdst, rc=rc, ncols=ncols, s=s, dv_done=dv_done):
            sync.wait_ge(CE.sem, dv_done)
            STO.inc(sync.dma_start(out=wdst[rc * 128:(rc + 1) * 128, :],
                                   in_=what[s][:, 0:ncols]), 16)
        block.sync(_st)

    def _fin(sync):
        STO.wait(sync)
        LDQ0.wait(sync)
        LDQ1.wait(sync)
    block.sync(_fin)
    done = (STO.sem, STO.n)
    return {"whgud": done, "whpd": done}


VARS["whgud"] = ((16384, D), BF16)
VARS["whpd"] = ((D, HH1), BF16)


@phase("l1gu", ins=["xn1t", "whgud", "rowsel", "onesc"], outs=["hfd", "rsd"])
def ph_l1gu(ctx, deps):
    nc, block = ctx.nc, ctx.block
    A = ctx.aps
    whgud, hfd, rsd = A["whgud"], A["hfd"], A["rsd"]
    HT = 1024
    xn1hd = nc.dram_tensor("xn1hd", [HT, D], BF16, kind="Internal").ap()
    gufd = nc.dram_tensor("gufd", [16384, HT], BF16, kind="Internal").ap()

    ix = ctx.sbuf("fg_ix", [128, 8], mybir.dt.int32)
    onesc = ctx.sbuf("fg_onesc", [128, 1], F32)
    xrow = ctx.sbuf("fg_xrow", [128, D], BF16)
    xnf1 = ctx.sbuf("fg_xnf1", [128, ND * HT], BF16)
    wtall = ctx.sbuf("fg_wtall", [128, ND * 2048], BF16)
    gur = ctx.sbuf("fg_gur", [128, HT], BF16)
    psA = ctx.psum("fg_psA", [128, 1024], F32)
    LDI, LDQ, STOG, LDT2, LDW2, CP, CE, CS, STO = (ctx.chain() for _ in range(9))
    for nm in ("xn1t", "whgud"):
        dv = deps.get(nm) if deps else None
        if dv:
            def _w(sync, dv=dv):
                sync.wait_ge(dv[0], dv[1])
            block.sync(_w)

    def _ld0(sync):
        LDI.inc(sync.dma_start(out=ix[:], in_=A["rowsel"][:]), 16)
        LDI.inc(sync.dma_start(out=onesc[:], in_=A["onesc"][:]), 16)
    block.sync(_ld0)

    for g in range(8):
        def _gth(gpsimd, g=g):
            LDI.wait(gpsimd)
            LDQ.inc(gpsimd.indirect_dma_start(
                xrow[:], None, A["xn1t"][:, :], IndirectOffsetOnAxis(ix[:, g:g + 1], 0)), 16)
            LDQ.wait(gpsimd)
            STOG.inc(gpsimd.dma_start(out=xn1hd[g * 128:(g + 1) * 128, :], in_=xrow[:]), 16)
            STOG.wait(gpsimd)
        block.gpsimd(_gth)

    def _ldxn(sync):
        STOG.wait(sync)
        for c in range(ND):
            LDT2.inc(sync.dma_start(out=xnf1[:, c * HT:(c + 1) * HT],
                                    in_=xn1hd[:, c * 128:(c + 1) * 128], transpose=True), 16)
    block.sync(_ldxn)

    for hb in range(8):
        def _ldw(sync, hb=hb):
            CP.wait(sync)
            for c in range(ND):
                LDW2.inc(sync.dma_start(
                    out=wtall[:, c * 2048:(c + 1) * 2048],
                    in_=whgud[hb * 2048:(hb + 1) * 2048, c * 128:(c + 1) * 128],
                    transpose=True), 16)
        block.sync(_ldw)

        for hc in range(16):
            for t4 in range(2):
                def _pe(tensor, hb=hb, hc=hc, t4=t4):
                    LDW2.wait(tensor)
                    LDT2.wait(tensor)
                    CE.wait(tensor)
                    sl = (t4 % 2) * 512
                    for c in range(ND):
                        mm = tensor.matmul(
                            psA[:, sl:sl + 512],
                            wtall[:, c * 2048 + hc * 128: c * 2048 + (hc + 1) * 128],
                            xnf1[:, c * HT + t4 * 512: c * HT + (t4 + 1) * 512],
                            start=(c == 0), stop=(c == ND - 1))
                    CP.inc(mm)
                block.tensor(_pe)

                def _ve(vector, hb=hb, hc=hc, t4=t4):
                    CP.wait(vector)
                    if t4 == 0:
                        STO.wait(vector)
                    CE.inc(vector.tensor_copy(gur[:, t4 * 512:(t4 + 1) * 512],
                                              psA[:, (t4 % 2) * 512:(t4 % 2) * 512 + 512]))
                block.vector(_ve)

            def _st(sync, hb=hb, hc=hc):
                CE.wait(sync)
                STO.inc(sync.dma_start(
                    out=gufd[(hb * 16 + hc) * 128:(hb * 16 + hc + 1) * 128, :],
                    in_=gur[:]), 16)
            block.sync(_st)

    gate_s = ctx.sbuf("fg_gate", [128, HT], BF16)
    up_s = ctx.sbuf("fg_up", [128, HT], BF16)
    sgf = ctx.sbuf("fg_sgf", [128, HT], F32)
    hf32 = ctx.sbuf("fg_hf32", [128, HT], F32)
    hbf = ctx.sbuf("fg_hbf", [128, HT], BF16)
    sqh = ctx.sbuf("fg_sqh", [128, HT], F32)
    ssr = ctx.sbuf("fg_ssr", [1, HT], F32)
    psR = ctx.psum("fg_psR", [1, 1024], F32)
    LDH = ctx.chain()

    for hc in range(64):
        def _ldh(sync, hc=hc):
            CS.wait(sync)
            CE.wait(sync)
            CP.wait(sync)
            LDH.inc(sync.dma_start(out=gate_s[:], in_=gufd[hc * 128:(hc + 1) * 128, :]), 16)
            LDH.inc(sync.dma_start(out=up_s[:],
                                   in_=gufd[HH1 + hc * 128: HH1 + (hc + 1) * 128, :]), 16)
        block.sync(_ldh)

        def _sc(scalar, hc=hc):
            LDH.wait(scalar)
            CS.inc(scalar.activation(sgf[:], gate_s[:], AF.Silu))
        block.scalar(_sc)

        def _ve(vector, hc=hc):
            CS.wait(vector)
            CE.inc(vector.tensor_tensor(hf32[:], sgf[:], up_s[:], ALU.mult))
            CE.wait(vector)
            STO.wait(vector)
            CE.inc(vector.tensor_copy(hbf[:], hf32[:]))
        block.vector(_ve)

        def _sc2(scalar, hc=hc):
            CE.wait(scalar)
            CS.inc(scalar.activation(sqh[:], hf32[:], AF.Square))
        block.scalar(_sc2)

        def _pe2(tensor, hc=hc):
            CS.wait(tensor)
            if hc == 0:
                CE.wait(tensor)
            tensor.matmul(psR[:, 0:512], onesc[:], sqh[:, 0:512],
                          start=(hc == 0), stop=(hc == 63))
            mm = tensor.matmul(psR[:, 512:1024], onesc[:], sqh[:, 512:1024],
                               start=(hc == 0), stop=(hc == 63))
            CP.inc(mm)
        block.tensor(_pe2)

        def _st(sync, hc=hc):
            CE.wait(sync)
            STO.inc(sync.dma_start(out=hfd[hc * 128:(hc + 1) * 128, :], in_=hbf[:]), 16)
        block.sync(_st)

    def _ver(vector):
        CP.wait(vector)
        CE.inc(vector.tensor_copy(ssr[:, 0:1024], psR[:, 0:1024]))
        CE.wait(vector)
        CE.inc(vector.tensor_scalar(ssr[:, 0:1024], ssr[:, 0:1024], 1.0 / HH1, None,
                                    ALU.mult))
        CE.wait(vector)
        CE.inc(vector.tensor_scalar(ssr[:, 0:1024], ssr[:, 0:1024],
                                    1.1920928955078125e-07, None, ALU.add))
    block.vector(_ver)

    def _scr(scalar):
        CE.wait(scalar)
        CS.inc(scalar.activation(ssr[:, 0:1024], ssr[:, 0:1024], AF.Ln))
        CS.wait(scalar)
        CS.inc(scalar.activation(ssr[:, 0:1024], ssr[:, 0:1024], AF.Exp, scale=-0.5))
    block.scalar(_scr)

    def _str(sync):
        CS.wait(sync)
        STO.inc(sync.dma_start(out=A["rsd"][:].flatten().unsqueeze(0), in_=ssr[:, 0:1024]), 16)
    block.sync(_str)

    def _fin(sync):
        STO.wait(sync)
        LDI.wait(sync)
        LDQ.wait(sync)
        STOG.wait(sync)
        LDT2.wait(sync)
        LDW2.wait(sync)
        LDH.wait(sync)
    block.sync(_fin)
    done = (STO.sem, STO.n)
    return {"hfd": done, "rsd": done}


VARS["hfd"] = ((HH1, 1024), BF16)
VARS["rsd"] = ((8, 128), F32)


@phase("l1proj2", ins=["hfd", "whpd", "rsd", "x1b", "att1h", "rowsel", "eye_f32"],
       outs=["outh"])
def ph_l1proj2(ctx, deps):
    block = ctx.block
    A = ctx.aps
    hfd, whpd, x1b, att1h, outh = A["hfd"], A["whpd"], A["x1b"], A["att1h"], A["outh"]
    HT = 1024

    ix = ctx.sbuf("fp_ix", [128, 8], mybir.dt.int32)
    eyef = ctx.sbuf("fp_eyef", [128, 128], F32)
    rr8 = ctx.sbuf("fp_rr8", [8, 128], F32)
    rstd_tok = ctx.sbuf("fp_rstok", [128, 8], F32)
    wptg = ctx.sbuf("fp_wptg", [128, 8 * 2048], BF16)
    hfb = ctx.sbuf("fp_hfb", [128, 8 * 128], BF16)
    pacc8 = ctx.sbuf("fp_pacc8", [128, 8 * 2048], F32)
    x1br = ctx.sbuf("fp_x1br", [128, D], F32)
    a1r = ctx.sbuf("fp_a1r", [128, D], F32)
    orow = ctx.sbuf("fp_orow", [128, D], F32)
    psQ = ctx.psum("fp_psQ", [128, 1024], F32)
    psT8 = ctx.psum("fp_psT8", [128, 8], F32)
    LDI, LDG, LDW, LDH, LDF, CP, CE, STO = (ctx.chain() for _ in range(8))

    for nm in ("hfd", "whpd", "rsd", "att1h"):
        dv = deps.get(nm) if deps else None
        if dv:
            def _w(sync, dv=dv):
                sync.wait_ge(dv[0], dv[1])
            block.sync(_w)

    def _ld0(sync):
        LDI.inc(sync.dma_start(out=ix[:], in_=A["rowsel"][:]), 16)
        LDI.inc(sync.dma_start(out=eyef[:], in_=A["eye_f32"][:]), 16)
        LDI.inc(sync.dma_start(out=rr8[:], in_=A["rsd"][:]), 16)
    block.sync(_ld0)

    def _per(tensor):
        LDI.wait(tensor)
        CP.inc(tensor.transpose(psT8[:], rr8[:], eyef[0:8, 0:8]))
    block.tensor(_per)

    def _ver2(vector):
        CP.wait(vector)
        CE.inc(vector.tensor_copy(rstd_tok[:], psT8[:]))
    block.vector(_ver2)

    for hg in range(8):
        def _ldw(sync, hg=hg):
            CP.wait(sync)
            for j in range(8):
                LDW.inc(sync.dma_start(
                    out=wptg[:, j * 2048:(j + 1) * 2048],
                    in_=whpd[:, (hg * 8 + j) * 128:(hg * 8 + j + 1) * 128],
                    transpose=True), 16)
        block.sync(_ldw)

        for tt in range(8):
            def _ldh(sync, hg=hg, tt=tt):
                CP.wait(sync)
                LDH.inc(sync.dma_start(
                    out=hfb[:].rearrange("p (c t) -> p c t", c=8),
                    in_=hfd[hg * 1024:(hg + 1) * 1024, tt * 128:(tt + 1) * 128]
                    .rearrange("(c p) t -> p c t", p=128)), 16)
            block.sync(_ldh)

            for d2 in range(2):
                def _pe(tensor, hg=hg, tt=tt, d2=d2):
                    LDW.wait(tensor)
                    LDH.wait(tensor)
                    CE.wait(tensor)
                    sl = d2 * 1024
                    for j in range(8):
                        tensor.matmul(psQ[:, 0:512],
                                      hfb[:, j * 128:(j + 1) * 128],
                                      wptg[:, j * 2048 + sl: j * 2048 + sl + 512],
                                      start=(j == 0), stop=(j == 7))
                    for j in range(8):
                        mm = tensor.matmul(psQ[:, 512:1024],
                                           hfb[:, j * 128:(j + 1) * 128],
                                           wptg[:, j * 2048 + sl + 512: j * 2048 + sl + 1024],
                                           start=(j == 0), stop=(j == 7))
                    CP.inc(mm)
                block.tensor(_pe)

                def _ve(vector, hg=hg, tt=tt, d2=d2):
                    CP.wait(vector)
                    o = tt * 2048 + d2 * 1024
                    if hg == 0:
                        CE.inc(vector.tensor_copy(pacc8[:, o:o + 1024], psQ[:, 0:1024]))
                    else:
                        CE.inc(vector.tensor_add(pacc8[:, o:o + 1024],
                                                 pacc8[:, o:o + 1024], psQ[:, 0:1024]))
                block.vector(_ve)

    for tt in range(8):
        def _gth(gpsimd, tt=tt):
            LDI.wait(gpsimd)
            CE.wait(gpsimd)
            LDG.inc(gpsimd.indirect_dma_start(
                x1br[:], None, x1b[:, :], IndirectOffsetOnAxis(ix[:, tt:tt + 1], 0)), 16)
        block.gpsimd(_gth)

        def _ldf(sync, tt=tt):
            CE.wait(sync)
            STO.wait(sync)
            LDF.inc(sync.dma_start(out=a1r[:], in_=att1h[tt * 128:(tt + 1) * 128, :]), 16)
        block.sync(_ldf)

        def _vef(vector, tt=tt):
            LDG.wait(vector)
            LDF.wait(vector)
            CE.inc(vector.tensor_scalar(orow[:], pacc8[:, tt * 2048:(tt + 1) * 2048],
                                        rstd_tok[:, tt:tt + 1], None, ALU.mult))
            CE.wait(vector)
            CE.inc(vector.tensor_add(orow[:], orow[:], a1r[:]))
            CE.wait(vector)
            CE.inc(vector.tensor_add(orow[:], orow[:], x1br[:]))
        block.vector(_vef)

        def _stf(sync, tt=tt):
            CE.wait(sync)
            STO.inc(sync.dma_start(out=outh[tt * 128:(tt + 1) * 128, :], in_=orow[:]), 16)
        block.sync(_stf)

    def _fin(sync):
        STO.wait(sync)
        LDI.wait(sync)
        LDG.wait(sync)
        LDW.wait(sync)
        LDH.wait(sync)
        LDF.wait(sync)
    block.sync(_fin)
    return {"outh": (STO.sem, STO.n)}


VARS["outh"] = ((1024, D), F32)
CONSTS_SHAPES.update({
    "w_gate_up": ((16384, D), F32), "w_proj": ((D, HH if False else 8192), F32),
})


LAUNCH_GROUPS = [
    ["ln0", "l0qkv", "l0v", "l0reg", "l0attn", "l0moe", "ln1",
     "l1proj", "l1conv", "l1scan", "l1post", "l1quant", "l1gu", "l1proj2"],
]


def build_group(group, ext_out=None):
    nc = bass.Bass("TRN2", target_bir_lowering=False)
    need_in, produced = set(), set()
    for pname in group:
        fn, ins, outs = PHASES[pname]
        for n in ins:
            if n not in produced:
                need_in.add(n)
        produced.update(outs)
    if ext_out is None:
        ext_out = produced
    aps = {}
    for n in sorted(need_in):
        shape, dt = CONSTS_SHAPES[n] if n in CONSTS_SHAPES else VARS[n]
        aps[n] = nc.dram_tensor(n, list(shape), dt, kind="ExternalInput").ap()
    for n in sorted(produced):
        shape, dt = VARS[n]
        kind = "ExternalOutput" if n in ext_out else "Internal"
        aps[n] = nc.dram_tensor(n, list(shape), dt, kind=kind).ap()
    with contextlib.ExitStack() as gstack:
        with nc.Block() as block:
            ctx = Ctx(nc, block, aps, gstack)
            ctx.phd = ctx.chain()
            deps = {}
            for pi, pname in enumerate(group):
                fn, ins, outs = PHASES[pname]
                ctx.phase_idx = pi
                ctx.phase_chains = []
                with contextlib.ExitStack() as pstack:
                    ctx.stack = pstack
                    newdeps = fn(ctx, deps)
                if newdeps:
                    deps.update(newdeps)
                chains = ctx.phase_chains

                def _pb(sync, chains=chains):
                    for ch in chains:
                        ch.wait(sync)
                    ctx.phd.inc(sync.nop())
                ctx.block.sync(_pb)
    return nc, sorted(need_in), sorted(n for n in produced if n in ext_out)


def prepare_consts():
    import ml_dtypes
    c = {}
    c["eye_bf"] = np.eye(128, dtype=ml_dtypes.bfloat16)
    c["eye_f32"] = np.eye(128, dtype=np.float32)
    c["onesc"] = np.ones((128, 1), dtype=np.float32)
    c["onesr"] = np.ones((1, 128), dtype=np.float32)
    inv = (10000.0 ** (-np.arange(0, 128, 2, dtype=np.float64) / 128.0))
    th = np.arange(N, dtype=np.float64)[None, :] * inv[:, None]
    cos = np.cos(th).astype(np.float32)
    sin = np.sin(th).astype(np.float32)
    s = np.float32(128.0 ** -0.5)
    c["ccq"] = np.concatenate([cos * s, cos * s], 0).astype(ml_dtypes.bfloat16)
    c["ssq"] = np.concatenate([-sin * s, sin * s], 0).astype(ml_dtypes.bfloat16)
    c["cck"] = np.concatenate([cos, cos], 0).astype(ml_dtypes.bfloat16)
    c["ssk"] = np.concatenate([-sin, sin], 0).astype(ml_dtypes.bfloat16)
    c["mtriu"] = (-np.triu(np.ones((64, 64), dtype=np.float32), 1))
    cm = np.where(np.tril(np.ones((128, 128), bool)), 0.0, -1e30).astype(np.float32)
    c["cmaskb"] = cm.astype(ml_dtypes.bfloat16)
    jj, ii = np.meshgrid(np.arange(128), np.arange(128), indexing="ij")
    c["mnegf"] = np.where(jj <= ii, 0.0, -1e30).astype(np.float32)
    return c


NCORES = 8


def kernel(**inputs):
    consts = prepare_consts()
    asc = np.ascontiguousarray
    shared = {
        "Wq": inputs["L0_attn_Wq"], "Wk": inputs["L0_attn_Wk"],
        "Wv": inputs["L0_attn_Wv"], "Wb": inputs["L0_attn_Wb"],
        "Wo": inputs["L0_attn_Wo"],
        "gwt": asc(inputs["L0_ffn_gate_w"].T),
        "sgw": asc(inputs["L0_ffn_shared_gate_w"].reshape(16, 128).T),
        "w_gate": inputs["L0_ffn_w_gate"], "w_up": inputs["L0_ffn_w_up"],
        "w_down": inputs["L0_ffn_w_down"],
        "sw_gate": inputs["L0_ffn_sw_gate"], "sw_up": inputs["L0_ffn_sw_up"],
        "sw_down": inputs["L0_ffn_sw_down"],
        "W_inT": asc(inputs["L1_attn_W_in"].T),
        "convw": asc(inputs["L1_attn_conv_w"].reshape(4352, 4)),
        "convb": asc(inputs["L1_attn_conv_b"].reshape(34, 128).T),
        "dt_biasv": asc(inputs["L1_attn_dt_bias"].reshape(64, 1)),
        "A_logv": asc(inputs["L1_attn_A_log"].reshape(64, 1)),
        "d_rep": asc(np.repeat(inputs["L1_attn_D"], 64).reshape(1, 4096)),
        "norm_wv": asc(inputs["L1_attn_norm_w"].reshape(1, 4096)),
        "W_outT": asc(inputs["L1_attn_W_out"].T),
        "w_gate_up": inputs["L1_ffn_w_gate_up"], "w_proj": inputs["L1_ffn_w_proj"],
    }
    shared.update(consts)
    x0 = inputs["L0_attn_x"]
    percore = []
    for c in range(NCORES):
        p = c % 2
        rs = (p * 1024 + np.arange(1024)).astype(np.int32).reshape(8, 128).T.copy()
        percore.append({"x0": asc(x0[c // 2]), "rowsel": rs})

    state = [dict() for _ in range(NCORES)]

    def get(c, n):
        if n in percore[c]:
            return percore[c][n]
        if n in shared:
            return shared[n]
        return state[c][n]

    produced_by = {}
    for gi, group in enumerate(LAUNCH_GROUPS):
        for pname in group:
            for n in PHASES[pname][2]:
                produced_by[n] = gi
    for gi, group in enumerate(LAUNCH_GROUPS):
        ext = {"outh"}
        for gj in range(gi + 1, len(LAUNCH_GROUPS)):
            for pname in LAUNCH_GROUPS[gj]:
                for n in PHASES[pname][1]:
                    if produced_by.get(n, 99) <= gi:
                        ext.add(n)
        nc, in_names, out_names = build_group(group, ext_out=ext)
        in_maps = [{n: get(c, n) for n in in_names} for c in range(NCORES)]
        res = run_bass_kernel_spmd(nc, in_maps, core_ids=list(range(NCORES)))
        for c in range(NCORES):
            for n in out_names:
                state[c][n] = np.asarray(res.results[c][n])

    out = np.empty((B, N, D), dtype=np.float32)
    for b in range(B):
        out[b, 0:1024] = state[2 * b]["outh"]
        out[b, 1024:2048] = state[2 * b + 1]["outh"]
    return out


# version 10
# speedup vs baseline: 1.4333x; 1.0210x over previous
import contextlib
import numpy as np
import concourse.bass as bass
import concourse.mybir as mybir
from concourse.bass_utils import run_bass_kernel_spmd
from concourse.bass_core_rust import IndirectOffsetOnAxis

F32 = mybir.dt.float32
BF16 = mybir.dt.bfloat16
AF = mybir.ActivationFunctionType
ALU = mybir.AluOpType
AX = mybir.AxisListType

B, N, D = 4, 2048, 2048
NT = N // 128
ND = D // 128

VARS = {
    "xnt": ((N, D), BF16),
}

CONSTS_SHAPES = {
    "x0": ((N, D), F32),
    "eye_bf": ((128, 128), BF16),
}

PHASES = {}


def phase(name, ins, outs):
    def deco(fn):
        PHASES[name] = (fn, ins, outs)
        return fn
    return deco


SELF_EDGES = True


class Chain:
    def __init__(self, sem):
        self.sem = sem
        self.n = 0

    def swait(self, eng, v=None):
        if SELF_EDGES:
            eng.wait_ge(self.sem, self.n if v is None else v)

    def inc(self, instr, k=1):
        self.n += k
        instr.then_inc(self.sem, k)
        return self.n

    def wait(self, eng, v=None):
        eng.wait_ge(self.sem, self.n if v is None else v)


class _BlockProxy:
    def __init__(self, block, ctx):
        self._block = block
        self._ctx = ctx

    def __getattr__(self, name):
        sec = getattr(self._block, name)
        ctx = self._ctx

        def wrapped(fn):
            def inner(eng):
                if ctx.phase_idx > 0:
                    eng.wait_ge(ctx.phd.sem, ctx.phase_idx)
                return fn(eng)
            return sec(inner)
        return wrapped


class Ctx:
    def __init__(self, nc, block, aps, gstack):
        self.nc = nc
        self.rawblock = block
        self.block = _BlockProxy(block, self)
        self.aps = aps
        self.gstack = gstack
        self.stack = None
        self.sem_n = 0
        self.buf_n = 0
        self.phase_idx = 0
        self.phd = None
        self.phase_chains = []

    def chain(self):
        self.sem_n += 1
        ch = Chain(self.gstack.enter_context(self.nc.semaphore(f"s{self.sem_n}")))
        self.phase_chains.append(ch)
        return ch

    def sbuf(self, name, shape, dtype):
        self.buf_n += 1
        return self.stack.enter_context(
            self.nc.sbuf_tensor(f"{name}{self.buf_n}", list(shape), dtype))

    def psum(self, name, shape, dtype):
        self.buf_n += 1
        return self.stack.enter_context(
            self.nc.psum_tensor(f"{name}{self.buf_n}", list(shape), dtype))


def make_ln(name, src_name, dst_name):
    @phase(name, ins=[src_name], outs=[dst_name])
    def ph_ln(ctx, deps, src_name=src_name, dst_name=dst_name):
        return _ln_body(ctx, deps, src_name, dst_name)
    return ph_ln


def _ln_body(ctx, deps, src_name, dst_name):
    block = ctx.block
    x0, xnt = ctx.aps[src_name], ctx.aps[dst_name]
    dv = deps.get(src_name) if deps else None
    if dv:
        def _w(sync):
            sync.wait_ge(dv[0], dv[1])
        block.sync(_w)
    xt = ctx.sbuf("ln_xt", [128, D], F32)
    xnb = ctx.sbuf("ln_xnb", [128, D], BF16)
    bns = ctx.sbuf("ln_bns", [128, 24], F32)
    agg = ctx.sbuf("ln_agg", [128, 4], F32)

    LD, ST, CV, CS = ctx.chain(), ctx.chain(), ctx.chain(), ctx.chain()
    ld, st, cv, cs = LD.sem, ST.sem, CV.sem, CS.sem

    def _sync(sync):
        for i in range(NT):
            if i >= 1:
                sync.wait_ge(cv, 7 * i)
            sync.dma_start(out=xt[:], in_=x0[i * 128:(i + 1) * 128, :]).then_inc(ld, 16)
            sync.wait_ge(cv, 7 * i + 7)
            sync.dma_start(out=xnt[i * 128:(i + 1) * 128, :], in_=xnb[:]).then_inc(st, 16)
        sync.wait_ge(st, 16 * NT)

    def _vector(vector):
        for i in range(NT):
            vector.wait_ge(ld, 16 * (i + 1))
            for j in range(4):
                vector.bn_stats(bns[:, 6 * j:6 * (j + 1)],
                                xt[:, 512 * j:512 * (j + 1)]).then_inc(cv, 1)
            vector.wait_ge(cv, 7 * i + 4)
            vector.bn_aggr(agg[:, 0:2], bns[:]).then_inc(cv, 1)
            vector.wait_ge(cv, 7 * i + 5)
            vector.tensor_scalar(agg[:, 2:3], agg[:, 1:2], 1e-5, None, ALU.add) \
                .then_inc(cv, 1)
            vector.wait_ge(cs, 2 * i + 2)
            if i >= 1:
                vector.wait_ge(st, 16 * i)
            vector.tensor_scalar(xnb[:], xt[:], agg[:, 0:1], agg[:, 3:4],
                                 ALU.subtract, ALU.mult).then_inc(cv, 1)

    def _scalar(scalar):
        for i in range(NT):
            scalar.wait_ge(cv, 7 * i + 6)
            scalar.activation(agg[:, 3:4], agg[:, 2:3], AF.Ln).then_inc(cs, 1)
            scalar.wait_ge(cs, 2 * i + 1)
            scalar.activation(agg[:, 3:4], agg[:, 3:4], AF.Exp, scale=-0.5) \
                .then_inc(cs, 1)

    block.sync(_sync)
    block.vector(_vector)
    block.scalar(_scalar)
    ST.n = 16 * NT
    return {dst_name: (st, 16 * NT)}


make_ln("ln0", "x0", "xnt")
make_ln("ln1", "x1b", "xn1t")
VARS["xn1t"] = ((N, D), BF16)


@phase("l0qkv", ins=["xnt", "Wq", "Wk", "Wb", "ccq", "ssq", "cck", "ssk",
                     "onesc", "onesr", "eye_f32"],
       outs=["qf", "kf", "ksf", "betad", "betatd"])
def ph_l0qkv(ctx, deps):
    block = ctx.block
    A = ctx.aps
    xnt, Wq, Wk, Wb = A["xnt"], A["Wq"], A["Wk"], A["Wb"]
    qf, kf, ksf, betad, betatd = (A[n] for n in
                                  ["qf", "kf", "ksf", "betad", "betatd"])

    xnf = ctx.sbuf("qk_xnf", [128, ND * N], BF16)
    wq = ctx.sbuf("qk_wq", [128, ND * 128], BF16)
    wk = ctx.sbuf("qk_wk", [128, ND * 128], BF16)
    wb = ctx.sbuf("qk_wb", [128, ND * 16], BF16)
    ccq = ctx.sbuf("qk_ccq", [128, N], BF16)
    ssq = ctx.sbuf("qk_ssq", [128, N], BF16)
    cck = ctx.sbuf("qk_cck", [128, N], BF16)
    ssk = ctx.sbuf("qk_ssk", [128, N], BF16)
    onesc = ctx.sbuf("qk_onesc", [128, 1], F32)
    onesr = ctx.sbuf("qk_onesr", [1, 128], F32)
    eyef = ctx.sbuf("qk_eyef", [128, 128], F32)
    betaf = ctx.sbuf("qk_betaf", [16, N], F32)
    betat = ctx.sbuf("qk_betat", [128, 16], F32)
    brow = ctx.sbuf("qk_brow", [1, N], F32)
    qa = ctx.sbuf("qk_qa", [128, N], F32)
    qsw = ctx.sbuf("qk_qsw", [128, N], F32)
    qout = ctx.sbuf("qk_qout", [128, N], BF16)
    kall = ctx.sbuf("qk_kall", [128, N], F32)
    ksq = ctx.sbuf("qk_ksq", [128, N], F32)
    s_row = ctx.sbuf("qk_srow", [1, N], F32)
    knr = ctx.sbuf("qk_knr", [128, N], F32)
    ksw, kt1 = ksq, kall
    kfout = ctx.sbuf("qk_kfout", [128, N], BF16)
    ksfout = ctx.sbuf("qk_ksfout", [128, N], BF16)
    psq = ctx.psum("qk_psq", [128, 512], F32)
    ps1 = ctx.psum("qk_ps1", [1, 512], F32)
    psb = ctx.psum("qk_psb", [128, 512], F32)
    psbt = ctx.psum("qk_psbt", [16, 512], F32)
    pstr = ctx.psum("qk_pstr", [128, 16], F32)

    LDX = ctx.chain()
    LDW = ctx.chain()
    CP = ctx.chain()
    CE = ctx.chain()
    CS = ctx.chain()
    CG = ctx.chain()
    STO = ctx.chain()

    dep_xnt = deps.get("xnt") if deps else None

    def _s0_sync(sync):
        if dep_xnt:
            sync.wait_ge(dep_xnt[0], dep_xnt[1])
        for c in range(ND):
            LDX.inc(sync.dma_start(out=xnf[:, c * N:(c + 1) * N],
                                   in_=xnt[:, c * 128:(c + 1) * 128], transpose=True), 16)
        for name, buf in (("ccq", ccq), ("ssq", ssq), ("cck", cck), ("ssk", ssk)):
            LDX.inc(sync.dma_start(out=buf[:], in_=A[name][:]), 16)
        LDX.inc(sync.dma_start(out=onesc[:], in_=A["onesc"][:]), 16)
        LDX.inc(sync.dma_start(out=onesr[:], in_=A["onesr"][:]), 16)
        LDX.inc(sync.dma_start(out=eyef[:], in_=A["eye_f32"][:]), 16)
    block.sync(_s0_sync)
    ldx0 = LDX.n

    def _s0_gp(gpsimd):
        LDW.inc(gpsimd.dma_start(out=wb[:].rearrange("p (c o) -> p c o", c=ND),
                                 in_=Wb.rearrange("(c p) o -> p c o", p=128)), 16)
    block.gpsimd(_s0_gp)

    for t4 in range(4):
        def _pe(tensor, t4=t4):
            tensor.wait_ge(LDX.sem, ldx0)
            LDW.wait(tensor)
            CS.wait(tensor)
            for c in range(ND):
                mm = tensor.matmul(psbt[:], wb[:, c * 16:(c + 1) * 16],
                                   xnf[:, c * N + t4 * 512: c * N + (t4 + 1) * 512],
                                   start=(c == 0), stop=(c == ND - 1))
            CP.inc(mm)
        block.tensor(_pe)

        def _sc(scalar, t4=t4):
            CP.wait(scalar)
            CS.inc(scalar.activation(betaf[:, t4 * 512:(t4 + 1) * 512], psbt[:],
                                     AF.Sigmoid))
        block.scalar(_sc)

    def _bv(vector):
        CS.wait(vector)
        CE.inc(vector.tensor_scalar(betaf[:], betaf[:], 2.0, None, ALU.mult))
    block.vector(_bv)

    def _bsy(sync):
        CE.wait(sync)
        STO.inc(sync.dma_start(out=betad[:], in_=betaf[:]), 16)
    block.sync(_bsy)
    betad_sto = STO.n

    for c in range(NT):
        def _pe(tensor, c=c):
            CE.wait(tensor)
            tensor.wait_ge(LDX.sem, ldx0)
            CP.inc(tensor.transpose(pstr[:], betaf[:, c * 128:(c + 1) * 128],
                                    eyef[0:16, 0:16]))
        block.tensor(_pe)

        def _ve(vector, c=c):
            CP.wait(vector)
            STO.wait(vector)
            CE.inc(vector.tensor_copy(betat[:], pstr[:]))
        block.vector(_ve)

        def _sy(sync, c=c):
            CE.wait(sync)
            STO.inc(sync.dma_start(out=betatd[c * 128:(c + 1) * 128, :], in_=betat[:]), 16)
        block.sync(_sy)

    for h in range(16):
        def _gp(gpsimd, h=h):
            CP.wait(gpsimd)
            LDW.inc(gpsimd.dma_start(
                out=wq[:].rearrange("p (c o) -> p c o", c=ND),
                in_=Wq[:, h * 128:(h + 1) * 128].rearrange("(c p) o -> p c o", p=128)), 16)
            LDW.inc(gpsimd.dma_start(
                out=wk[:].rearrange("p (c o) -> p c o", c=ND),
                in_=Wk[:, h * 128:(h + 1) * 128].rearrange("(c p) o -> p c o", p=128)), 16)
        block.gpsimd(_gp)

        for t4 in range(4):
            def _pe(tensor, t4=t4):
                LDW.wait(tensor)
                tensor.wait_ge(LDX.sem, ldx0)
                CE.wait(tensor)
                for c in range(ND):
                    mm = tensor.matmul(psq[:], wq[:, c * 128:(c + 1) * 128],
                                       xnf[:, c * N + t4 * 512: c * N + (t4 + 1) * 512],
                                       start=(c == 0), stop=(c == ND - 1))
                CP.inc(mm)
            block.tensor(_pe)

            def _ve(vector, t4=t4):
                CP.wait(vector)
                CG.wait(vector)
                CE.inc(vector.tensor_copy(qa[:, t4 * 512:(t4 + 1) * 512], psq[:]))
            block.vector(_ve)

        def _gp2(gpsimd, h=h):
            CE.wait(gpsimd)
            gpsimd.wait_ge(LDX.sem, ldx0)
            STO.wait(gpsimd)
            CG.inc(gpsimd.tensor_copy(qsw[0:64, :], qa[64:128, :]))
            CG.swait(gpsimd)
            CG.inc(gpsimd.tensor_copy(qsw[64:128, :], qa[0:64, :]))
            CG.swait(gpsimd)
            CG.inc(gpsimd.tensor_tensor(qa[:], qa[:], ccq[:], ALU.mult))
            CG.swait(gpsimd)
            CG.inc(gpsimd.tensor_tensor(qsw[:], qsw[:], ssq[:], ALU.mult))
            CG.swait(gpsimd)
            CG.inc(gpsimd.tensor_add(qout[:], qa[:], qsw[:]))
        block.gpsimd(_gp2)
        qrope_done = CG.n

        def _qst(sync, h=h):
            sync.wait_ge(CG.sem, qrope_done)
            STO.inc(sync.dma_start(out=qf[h * 128:(h + 1) * 128, :], in_=qout[:]), 16)
        block.sync(_qst)

        for t4 in range(4):
            def _pe(tensor, t4=t4):
                CE.wait(tensor)
                for c in range(ND):
                    mm = tensor.matmul(psq[:], wk[:, c * 128:(c + 1) * 128],
                                       xnf[:, c * N + t4 * 512: c * N + (t4 + 1) * 512],
                                       start=(c == 0), stop=(c == ND - 1))
                CP.inc(mm)
            block.tensor(_pe)

            def _ve(vector, t4=t4):
                CP.wait(vector)
                CS.wait(vector)
                CE.inc(vector.tensor_copy(kall[:, t4 * 512:(t4 + 1) * 512], psq[:]))
            block.vector(_ve)

        def _sc(scalar):
            CE.wait(scalar)
            CS.inc(scalar.activation(ksq[:], kall[:], AF.Square))
        block.scalar(_sc)

        for t4 in range(4):
            def _pe(tensor, t4=t4):
                CS.wait(tensor)
                CE.wait(tensor)
                CP.inc(tensor.matmul(ps1[:], onesc[:], ksq[:, t4 * 512:(t4 + 1) * 512],
                                     start=True, stop=True))
            block.tensor(_pe)

            def _ve(vector, t4=t4):
                CP.wait(vector)
                CE.inc(vector.tensor_copy(s_row[0:1, t4 * 512:(t4 + 1) * 512], ps1[:]))
            block.vector(_ve)

        def _ve2(vector):
            CE.wait(vector)
            CE.inc(vector.tensor_scalar(s_row[:], s_row[:], 1e-6, None, ALU.add))
        block.vector(_ve2)

        def _sc2(scalar):
            CE.wait(scalar)
            CS.inc(scalar.activation(s_row[:], s_row[:], AF.Ln))
            CS.wait(scalar)
            CS.inc(scalar.activation(s_row[:], s_row[:], AF.Exp, scale=-0.5))
        block.scalar(_sc2)

        for t4 in range(4):
            def _pe(tensor, t4=t4):
                CS.wait(tensor)
                CE.wait(tensor)
                CP.inc(tensor.matmul(psb[:], onesr[:],
                                     s_row[0:1, t4 * 512:(t4 + 1) * 512],
                                     start=True, stop=True))
            block.tensor(_pe)

            def _ve(vector, t4=t4):
                CP.wait(vector)
                CE.inc(vector.tensor_tensor(knr[:, t4 * 512:(t4 + 1) * 512],
                                            kall[:, t4 * 512:(t4 + 1) * 512],
                                            psb[:], ALU.mult))
            block.vector(_ve)

        def _krope(vector):
            CE.swait(vector)
            CE.inc(vector.tensor_copy(ksw[0:64, :], knr[64:128, :]))
            CE.swait(vector)
            CE.inc(vector.tensor_copy(ksw[64:128, :], knr[0:64, :]))
            CE.swait(vector)
            CE.inc(vector.tensor_tensor(kt1[:], knr[:], cck[:], ALU.mult))
            CE.swait(vector)
            CE.inc(vector.tensor_tensor(ksw[:], ksw[:], ssk[:], ALU.mult))
            CE.wait(vector)
            STO.wait(vector)
            CE.inc(vector.tensor_add(knr[:], kt1[:], ksw[:]))
            CE.swait(vector)
            CE.inc(vector.tensor_copy(kfout[:], knr[:]))
        block.vector(_krope)

        def _kst(sync, h=h):
            CE.wait(sync)
            STO.inc(sync.dma_start(out=kf[h * 128:(h + 1) * 128, :], in_=kfout[:]), 16)
            CP.wait(sync)
            LDX.inc(sync.dma_start(out=brow[:], in_=betad[h:h + 1, :]), 16)
        block.sync(_kst)
        brow_ld = LDX.n

        for t4 in range(4):
            def _pe(tensor, t4=t4):
                tensor.wait_ge(LDX.sem, brow_ld)
                CE.wait(tensor)
                CP.inc(tensor.matmul(psb[:], onesr[:],
                                     brow[0:1, t4 * 512:(t4 + 1) * 512],
                                     start=True, stop=True))
            block.tensor(_pe)

            def _ve(vector, t4=t4):
                CP.wait(vector)
                if t4 == 0:
                    STO.wait(vector)
                CE.inc(vector.tensor_tensor(ksfout[:, t4 * 512:(t4 + 1) * 512],
                                            knr[:, t4 * 512:(t4 + 1) * 512],
                                            psb[:], ALU.mult))
            block.vector(_ve)

        def _ksst(sync, h=h):
            CE.wait(sync)
            STO.inc(sync.dma_start(out=ksf[h * 128:(h + 1) * 128, :], in_=ksfout[:]), 16)
        block.sync(_ksst)

    def _fin(sync):
        STO.wait(sync)
        LDX.wait(sync)
        LDW.wait(sync)
    block.sync(_fin)

    done = (STO.sem, STO.n)
    return {n: done for n in ["qf", "kf", "ksf", "betad", "betatd"]}


@phase("l0v", ins=["xnt", "Wv"], outs=["vt"])
def ph_l0v(ctx, deps):
    block = ctx.block
    A = ctx.aps
    xnt, Wv, vt = A["xnt"], A["Wv"], A["vt"]
    xnf = ctx.sbuf("v_xnf", [128, ND * N], BF16)
    wv = ctx.sbuf("v_wv", [128, ND * N], BF16)
    vrow = ctx.sbuf("v_vrow", [128, N], BF16)
    psv = ctx.psum("v_psv", [128, 512], F32)
    LDX, LDW, CP, CE, STO = (ctx.chain() for _ in range(5))
    dep_xnt = deps.get("xnt") if deps else None

    def _s0(sync):
        if dep_xnt:
            sync.wait_ge(dep_xnt[0], dep_xnt[1])
        for c in range(ND):
            LDX.inc(sync.dma_start(out=xnf[:, c * N:(c + 1) * N],
                                   in_=xnt[:, c * 128:(c + 1) * 128], transpose=True), 16)
    block.sync(_s0)
    ldx0 = LDX.n

    def _b_gp(gpsimd):
        for c in range(ND):
            LDW.inc(gpsimd.dma_start(out=wv[:, c * N:(c + 1) * N],
                                     in_=Wv[c * 128:(c + 1) * 128, :]), 16)
    block.gpsimd(_b_gp)
    ldv = LDW.n

    for tt in range(NT):
        for dv4 in range(4):
            def _pe(tensor, tt=tt, dv4=dv4):
                tensor.wait_ge(LDW.sem, ldv)
                tensor.wait_ge(LDX.sem, ldx0)
                CE.wait(tensor)
                for c in range(ND):
                    mm = tensor.matmul(psv[:],
                                       xnf[:, c * N + tt * 128: c * N + (tt + 1) * 128],
                                       wv[:, c * N + dv4 * 512: c * N + (dv4 + 1) * 512],
                                       start=(c == 0), stop=(c == ND - 1))
                CP.inc(mm)
            block.tensor(_pe)

            def _ve(vector, tt=tt, dv4=dv4):
                CP.wait(vector)
                if dv4 == 0:
                    STO.wait(vector)
                CE.inc(vector.tensor_copy(vrow[:, dv4 * 512:(dv4 + 1) * 512], psv[:]))
            block.vector(_ve)

        def _sy(sync, tt=tt):
            CE.wait(sync)
            STO.inc(sync.dma_start(out=vt[tt * 128:(tt + 1) * 128, :], in_=vrow[:]), 16)
        block.sync(_sy)

    def _fin(sync):
        STO.wait(sync)
        LDX.wait(sync)
        LDW.wait(sync)
    block.sync(_fin)
    return {"vt": (STO.sem, STO.n)}


VARS.update({
    "qf": ((D, N), BF16),
    "kf": ((D, N), BF16),
    "ksf": ((D, N), BF16),
    "vt": ((N, D), BF16),
    "betad": ((16, N), F32),
    "betatd": ((N, 16), F32),
})
CONSTS_SHAPES.update({
    "Wq": ((D, D), F32), "Wk": ((D, D), F32), "Wv": ((D, D), F32), "Wb": ((D, 16), F32),
    "ccq": ((128, N), BF16), "ssq": ((128, N), BF16),
    "cck": ((128, N), BF16), "ssk": ((128, N), BF16),
    "onesc": ((128, 1), F32), "onesr": ((1, 128), F32), "eye_f32": ((128, 128), F32),
})


@phase("l0reg", ins=["kf", "vt", "betatd", "eye_bf", "mtriu"], outs=["vnt"])
def ph_l0reg(ctx, deps):
    block = ctx.block
    A = ctx.aps
    kf, vt, betatd, vnt = A["kf"], A["vt"], A["betatd"], A["vnt"]
    NH, CH, NCH = 16, 64, 32

    kall = ctx.sbuf("rg_kall", [128, NH * N], BF16)
    vall = ctx.sbuf("rg_vall", [128, 16 * D], BF16)
    bth = ctx.sbuf("rg_bth", [64, 32 * NH], F32)
    eye = ctx.sbuf("rg_eye", [128, 128], BF16)
    mtriu = ctx.sbuf("rg_mtriu", [64, 64], F32)
    S_f = ctx.sbuf("rg_Sf", [128, NH * 128], F32)
    S_b = ctx.sbuf("rg_Sb", [128, NH * 128], BF16)
    t1 = ctx.sbuf("rg_t1", [64, NH * 64], F32)
    nt_b = ctx.sbuf("rg_nt", [64, NH * 64], BF16)
    n_b = ctx.sbuf("rg_n", [64, NH * 64], BF16)
    nt2_b = ctx.sbuf("rg_nt2", [64, NH * 64], BF16)
    n2_b = ctx.sbuf("rg_n2", [64, NH * 64], BF16)
    r_b = ctx.sbuf("rg_r", [64, NH * 128], BF16)
    vnb = ctx.sbuf("rg_vnb", [64, NH * 128], BF16)
    kt_b = ctx.sbuf("rg_kt", [64, NH * 128], BF16)
    psbig = ctx.psum("rg_psbig", [64, 2048], F32)
    pstr = ctx.psum("rg_pstr", [64, 2048], BF16)
    psS = ctx.psum("rg_psS", [128, 512], F32)

    LDL, CP, CE, STO = (ctx.chain() for _ in range(4))
    for nm in ("kf", "vt", "betatd"):
        dv = deps.get(nm) if deps else None
        if dv:
            def _w(sync, dv=dv):
                sync.wait_ge(dv[0], dv[1])
            block.sync(_w)

    def _ld(sync):
        for h in range(NH):
            LDL.inc(sync.dma_start(out=kall[:, h * N:(h + 1) * N],
                                   in_=kf[h * 128:(h + 1) * 128, :]), 16)
        LDL.inc(sync.dma_start(out=vall[:].rearrange("p (g d) -> p g d", g=16),
                               in_=vt.rearrange("(g p) d -> p g d", p=128)), 16)
        LDL.inc(sync.dma_start(out=bth[:].rearrange("p (c h) -> p c h", c=32),
                               in_=betatd.rearrange("(c p) h -> p c h", p=64)), 16)
        LDL.inc(sync.dma_start(out=eye[:], in_=A["eye_bf"][:]), 16)
        LDL.inc(sync.dma_start(out=mtriu[:], in_=A["mtriu"][:]), 16)
    block.sync(_ld)

    def _init(vector):
        LDL.wait(vector)
        CE.inc(vector.memset(S_f[:], 0.0))
        CE.swait(vector)
        CE.inc(vector.memset(S_b[:], 0.0))
    block.vector(_init)

    for c in range(NCH):
        po = (c % 2) * 64
        g = c // 2

        def kslice(h, c=c):
            return kall[:, h * N + c * 64: h * N + (c + 1) * 64]

        def _pe_kk(tensor, c=c):
            LDL.wait(tensor)
            CE.wait(tensor)
            for h in range(NH):
                mm = tensor.matmul(psbig[:, h * 64:(h + 1) * 64], kslice(h), kslice(h),
                                   start=True, stop=True)
            CP.inc(mm)
        block.tensor(_pe_kk)

        def _ve_nt(vector, c=c, po=po, g=g):
            CP.wait(vector)
            CE.swait(vector)
            btc = bth[:].rearrange("p (c h) -> p c h", c=32)[:, c, :]
            btb = btc.unsqueeze(2).broadcast_to([64, NH, 64])
            CE.inc(vector.tensor_tensor(
                t1[:].rearrange("p (h j) -> p h j", h=NH),
                psbig[:, 0:1024].rearrange("p (h j) -> p h j", h=NH),
                btb, ALU.mult))
            CE.swait(vector)
            mtb = mtriu[:].unsqueeze(1).broadcast_to([64, NH, 64])
            CE.inc(vector.tensor_tensor(
                nt_b[:].rearrange("p (h j) -> p h j", h=NH),
                t1[:].rearrange("p (h j) -> p h j", h=NH),
                mtb, ALU.mult))
        block.vector(_ve_nt)

        def _pe_tr(tensor, c=c):
            CE.wait(tensor)
            for h in range(NH):
                mm = tensor.transpose(pstr[:, h * 64:(h + 1) * 64],
                                      nt_b[:, h * 64:(h + 1) * 64], eye[0:64, 0:64])
            CP.inc(mm)
        block.tensor(_pe_tr)

        def _ve_n(vector, c=c):
            CP.wait(vector)
            CE.inc(vector.tensor_copy(n_b[:], pstr[:, 0:1024]))
        block.vector(_ve_n)

        def _pe_ks(tensor, c=c):
            CE.wait(tensor)
            for h in range(NH):
                mm = tensor.matmul(psbig[:, h * 128:(h + 1) * 128], kslice(h),
                                   S_b[:, h * 128:(h + 1) * 128], start=True, stop=True)
            CP.inc(mm)
        block.tensor(_pe_ks)

        def _ve_r(vector, c=c, po=po, g=g):
            CP.wait(vector)
            STO.wait(vector)
            vsl = vall[po:po + 64, :].rearrange("p (g d) -> p g d", g=16)[:, g, :]
            CE.inc(vector.tensor_tensor(r_b[:], vsl, psbig[:, 0:2048], ALU.subtract))
        block.vector(_ve_r)

        cn, cnt = n_b, nt_b
        nn, nnt = n2_b, nt2_b
        for lvl in range(6):
            def _pe_f(tensor, cnt=cnt, c=c):
                CE.wait(tensor)
                for h in range(NH):
                    mm = tensor.matmul(psbig[:, h * 128:(h + 1) * 128],
                                       cnt[:, h * 64:(h + 1) * 64],
                                       r_b[:, h * 128:(h + 1) * 128],
                                       start=True, stop=True)
                CP.inc(mm)
            block.tensor(_pe_f)

            def _ve_f(vector, c=c):
                CP.wait(vector)
                CE.inc(vector.tensor_add(r_b[:], r_b[:], psbig[:, 0:2048]))
            block.vector(_ve_f)

            if lvl < 5:
                def _pe_s(tensor, cn=cn, cnt=cnt, c=c):
                    CE.wait(tensor)
                    for h in range(NH):
                        mm = tensor.matmul(psbig[:, h * 64:(h + 1) * 64],
                                           cnt[:, h * 64:(h + 1) * 64],
                                           cn[:, h * 64:(h + 1) * 64],
                                           start=True, stop=True)
                    CP.inc(mm)
                block.tensor(_pe_s)

                def _ve_s(vector, nn=nn, c=c):
                    CP.wait(vector)
                    CE.inc(vector.tensor_copy(nn[:], psbig[:, 0:1024]))
                block.vector(_ve_s)

                def _pe_s2(tensor, nn=nn, c=c):
                    CE.wait(tensor)
                    for h in range(NH):
                        mm = tensor.transpose(pstr[:, h * 64:(h + 1) * 64],
                                              nn[:, h * 64:(h + 1) * 64],
                                              eye[0:64, 0:64])
                    CP.inc(mm)
                block.tensor(_pe_s2)

                def _ve_s2(vector, nnt=nnt, c=c):
                    CP.wait(vector)
                    CE.inc(vector.tensor_copy(nnt[:], pstr[:, 0:1024]))
                block.vector(_ve_s2)
                cn, cnt, nn, nnt = nn, nnt, cn, cnt

        def _ve_vnb(vector, c=c, po=po, g=g):
            CE.swait(vector)
            btc = bth[:].rearrange("p (c h) -> p c h", c=32)[:, c, :]
            btb = btc.unsqueeze(2).broadcast_to([64, NH, 128])
            CE.inc(vector.tensor_tensor(
                vnb[:].rearrange("p (h d) -> p h d", h=NH),
                r_b[:].rearrange("p (h d) -> p h d", h=NH),
                btb, ALU.mult))
        block.vector(_ve_vnb)

        def _sy_vn(sync, c=c):
            CE.wait(sync)
            STO.inc(sync.dma_start(out=vnt[c * 64:(c + 1) * 64, :], in_=r_b[:]), 16)
        block.sync(_sy_vn)

        def _pe_kt(tensor, c=c):
            CE.wait(tensor)
            for h in range(NH):
                mm = tensor.transpose(pstr[:, h * 128:(h + 1) * 128], kslice(h), eye[:])
            CP.inc(mm)
        block.tensor(_pe_kt)

        def _ve_kt(vector, c=c):
            CP.wait(vector)
            CE.inc(vector.tensor_copy(kt_b[:], pstr[:, 0:2048]))
        block.vector(_ve_kt)

        for hg in range(4):
            def _pe_ds(tensor, hg=hg, c=c):
                CE.wait(tensor)
                for j in range(4):
                    h = hg * 4 + j
                    mm = tensor.matmul(psS[:, j * 128:(j + 1) * 128],
                                       kt_b[:, h * 128:(h + 1) * 128],
                                       vnb[:, h * 128:(h + 1) * 128],
                                       start=True, stop=True)
                CP.inc(mm)
            block.tensor(_pe_ds)

            def _ve_ds(vector, hg=hg, c=c):
                CP.wait(vector)
                CE.inc(vector.tensor_add(S_f[:, hg * 512:(hg + 1) * 512],
                                         S_f[:, hg * 512:(hg + 1) * 512], psS[:]))
            block.vector(_ve_ds)

        def _ve_sb(vector, c=c):
            CE.swait(vector)
            CE.inc(vector.tensor_copy(S_b[:], S_f[:]))
        block.vector(_ve_sb)

    def _fin(sync):
        STO.wait(sync)
        LDL.wait(sync)
    block.sync(_fin)
    return {"vnt": (STO.sem, STO.n)}


VARS["vnt"] = ((N, D), BF16)
CONSTS_SHAPES["mtriu"] = ((64, 64), F32)


@phase("l0attn", ins=["qf", "ksf", "vnt", "Wo", "x0", "eye_bf", "cmaskb"],
       outs=["x1a"])
def ph_l0attn(ctx, deps):
    block = ctx.block
    A = ctx.aps
    qf, ksf, vnt, Wo, x0, x1a = (A[n] for n in ["qf", "ksf", "vnt", "Wo", "x0", "x1a"])
    NH = 16

    kbuf = ctx.sbuf("at_kbuf", [128, N], BF16)
    vh = ctx.sbuf("at_vh", [128, 16 * 128], BF16)
    qrow = ctx.sbuf("at_qrow", [128, N], BF16)
    eye = ctx.sbuf("at_eye", [128, 128], BF16)
    cmask = ctx.sbuf("at_cmask", [128, 128], BF16)
    avf = ctx.sbuf("at_avf", [128, NH * N], BF16)
    pb = [ctx.sbuf("at_pb", [128, N], BF16) for _ in range(2)]
    ptall = [ctx.sbuf("at_ptall", [128, N], BF16) for _ in range(2)]
    srow = [ctx.sbuf("at_srow", [128, 4], F32) for _ in range(2)]
    psS = ctx.psum("at_psS", [128, 2048], F32)
    pstr = ctx.psum("at_pstr", [128, 2048], BF16)
    psA = ctx.psum("at_psA", [128, 512], F32)

    LDL, LDG, LDX2, CP, CE, CS, STO = (ctx.chain() for _ in range(7))
    for nm in ("qf", "ksf", "vnt"):
        dv = deps.get(nm) if deps else None
        if dv:
            def _w(sync, dv=dv):
                sync.wait_ge(dv[0], dv[1])
            block.sync(_w)

    def _ld0(sync):
        LDL.inc(sync.dma_start(out=eye[:], in_=A["eye_bf"][:]), 16)
        LDL.inc(sync.dma_start(out=cmask[:], in_=A["cmaskb"][:]), 16)
    block.sync(_ld0)

    SMM, MAXT, EXPT, NRM, TDONE, PTC, AVT, AVC = {}, {}, {}, {}, {}, {}, {}, {}

    def emit_T_AV(h, i):
        par = i % 2

        def _pe_t(tensor, h=h, i=i, par=par):
            tensor.wait_ge(CE.sem, NRM[h, i])
            tensor.wait_ge(CE.sem, PTC.get((h, i - 2), 0))
            for kc in range(i + 1):
                mm = tensor.transpose(pstr[:, kc * 128:(kc + 1) * 128],
                                      pb[par][:, kc * 128:(kc + 1) * 128], eye[:])
            CP.inc(mm)
            TDONE[h, i] = CP.n
        block.tensor(_pe_t)

        def _ve_t(vector, h=h, i=i, par=par, kr=(i + 1) * 128):
            vector.wait_ge(CP.sem, TDONE[h, i])
            vector.wait_ge(CP.sem, AVT.get((h, i - 2), 0))
            CE.inc(vector.tensor_copy(ptall[par][:, 0:kr], pstr[:, 0:kr]))
            PTC[h, i] = CE.n
        block.vector(_ve_t)

        def _pe_av(tensor, h=h, i=i, par=par):
            tensor.wait_ge(CE.sem, PTC[h, i])
            tensor.wait_ge(CE.sem, AVC.get((h, i - 1), 0))
            for kc in range(i + 1):
                mm = tensor.matmul(psA[:, 0:128],
                                   vh[:].rearrange("p (g d) -> p g d", g=16)[:, kc, :],
                                   ptall[par][:, kc * 128:(kc + 1) * 128],
                                   start=(kc == 0), stop=(kc == i))
            CP.inc(mm)
            AVT[h, i] = CP.n
        block.tensor(_pe_av)

        def _ve_av(vector, h=h, i=i):
            vector.wait_ge(CP.sem, AVT[h, i])
            CE.inc(vector.tensor_copy(avf[:, h * N + i * 128: h * N + (i + 1) * 128],
                                      psA[:, 0:128]))
            AVC[h, i] = CE.n
        block.vector(_ve_av)

    for h in range(NH):
        def _ldh(sync, h=h):
            CP.wait(sync)
            LDL.inc(sync.dma_start(out=kbuf[:], in_=ksf[h * 128:(h + 1) * 128, :]), 16)
            LDL.inc(sync.dma_start(
                out=vh[:].rearrange("p (g d) -> p g d", g=16),
                in_=vnt[:, h * 128:(h + 1) * 128].rearrange("(g p) d -> p g d", p=128)), 16)
            LDL.inc(sync.dma_start(out=qrow[:], in_=qf[h * 128:(h + 1) * 128, :]), 16)
        block.sync(_ldh)
        ldh_t = LDL.n

        for i in range(NT):
            kr = (i + 1) * 128
            par = i % 2

            def _pe_s(tensor, h=h, i=i, ldh_t=ldh_t):
                tensor.wait_ge(LDL.sem, ldh_t)
                tensor.wait_ge(CS.sem, EXPT.get((h, i - 1), 0))
                qt = qrow[:, i * 128:(i + 1) * 128]
                f = (i * 128) // 512
                for b in range(f):
                    tensor.matmul(psS[:, b * 512:(b + 1) * 512], qt,
                                  kbuf[:, b * 512:(b + 1) * 512], start=True, stop=True)
                for b in range(i - 4 * f):
                    o = f * 512 + b * 128
                    tensor.matmul(psS[:, o:o + 128], qt, kbuf[:, o:o + 128],
                                  start=True, stop=True)
                o = i * 128
                tensor.matmul(psS[:, o:o + 128], qt, kbuf[:, o:o + 128],
                              start=True, stop=False)
                mm = tensor.matmul(psS[:, o:o + 128], eye[:], cmask[:],
                                   start=False, stop=True)
                CP.inc(mm)
                SMM[h, i] = CP.n
            block.tensor(_pe_s)

            def _ve_m(vector, h=h, i=i, kr=kr, par=par):
                vector.wait_ge(CP.sem, SMM[h, i])
                vector.wait_ge(CS.sem, EXPT.get((h, i - 2), 0))
                CE.inc(vector.reduce_max(srow[par][:, 0:1], psS[:, 0:kr], axis=AX.X,
                                         negate=True))
                MAXT[h, i] = CE.n
            block.vector(_ve_m)

            def _sc_e(scalar, h=h, i=i, kr=kr, par=par):
                scalar.wait_ge(CE.sem, MAXT[h, i])
                scalar.wait_ge(CP.sem, TDONE.get((h, i - 2), 0))
                CS.inc(scalar.activation(pb[par][:, 0:kr], psS[:, 0:kr], AF.Exp,
                                         bias=srow[par][:, 0:1],
                                         accum_out=srow[par][:, 1:2]))
                EXPT[h, i] = CS.n
            block.scalar(_sc_e)

            def _ve_n(vector, h=h, i=i, kr=kr, par=par):
                vector.wait_ge(CS.sem, EXPT[h, i])
                CE.inc(vector.reciprocal(srow[par][:, 2:3], srow[par][:, 1:2]))
                CE.wait(vector)
                CE.inc(vector.tensor_scalar(pb[par][:, 0:kr], pb[par][:, 0:kr],
                                            srow[par][:, 2:3], None, ALU.mult))
                NRM[h, i] = CE.n
            block.vector(_ve_n)

            if i >= 1:
                emit_T_AV(h, i - 1)
        emit_T_AV(h, NT - 1)
    wosb = ctx.sbuf("at_wo", [128, NH * N], BF16)
    x0row = ctx.sbuf("at_x0r", [128, N], F32)
    x1row = ctx.sbuf("at_x1r", [128, N], F32)
    psW = ctx.psum("at_psW", [128, 512], F32)

    def _ldwo(gpsimd):
        for hc in range(NH):
            LDG.inc(gpsimd.dma_start(out=wosb[:, hc * N:(hc + 1) * N],
                                     in_=Wo[hc * 128:(hc + 1) * 128, :]), 16)
    block.gpsimd(_ldwo)
    ldwo = LDG.n

    for i in range(NT):
        def _ldx(sync, i=i):
            STO.wait(sync)
            LDX2.inc(sync.dma_start(out=x0row[:], in_=x0[i * 128:(i + 1) * 128, :]), 16)
        block.sync(_ldx)

        for d4 in range(4):
            def _pe_w(tensor, i=i, d4=d4):
                tensor.wait_ge(LDG.sem, ldwo)
                CE.wait(tensor)
                for hc in range(NH):
                    mm = tensor.matmul(psW[:],
                                       avf[:, hc * N + i * 128: hc * N + (i + 1) * 128],
                                       wosb[:, hc * N + d4 * 512: hc * N + (d4 + 1) * 512],
                                       start=(hc == 0), stop=(hc == NH - 1))
                CP.inc(mm)
            block.tensor(_pe_w)

            def _ve_w(vector, i=i, d4=d4):
                CP.wait(vector)
                LDX2.wait(vector)
                CE.inc(vector.tensor_add(x1row[:, d4 * 512:(d4 + 1) * 512],
                                         x0row[:, d4 * 512:(d4 + 1) * 512], psW[:]))
            block.vector(_ve_w)

        def _st(sync, i=i):
            CE.wait(sync)
            STO.inc(sync.dma_start(out=x1a[i * 128:(i + 1) * 128, :], in_=x1row[:]), 16)
        block.sync(_st)

    def _fin(sync):
        STO.wait(sync)
        LDL.wait(sync)
        LDG.wait(sync)
        LDX2.wait(sync)
    block.sync(_fin)
    return {"x1a": (STO.sem, STO.n)}


VARS["x1a"] = ((N, D), F32)
CONSTS_SHAPES["Wo"] = ((D, D), F32)
CONSTS_SHAPES["cmaskb"] = ((128, 128), BF16)


@phase("l0moe", ins=["xnt", "x1a", "gwt", "sgw", "w_gate", "w_up", "w_down",
                     "sw_gate", "sw_up", "sw_down", "onesr", "eye_f32"],
       outs=["x1b"])
def ph_l0moe(ctx, deps):
    nc, block = ctx.nc, ctx.block
    A = ctx.aps
    xnt, x1a, x1b = A["xnt"], A["x1a"], A["x1b"]
    macc = nc.dram_tensor("macc", [N, D], F32, kind="Internal").ap()
    cmbd = nc.dram_tensor("cmbd", [16, N], F32, kind="Internal").ap()
    HT = N
    NI = 1408
    NIC = NI // 128

    xnf = ctx.sbuf("mo_xnf", [128, ND * HT], BF16)
    gw = ctx.sbuf("mo_gw", [128, ND * 8], BF16)
    sgwb = ctx.sbuf("mo_sgw", [128, ND * 1], BF16)
    onesr = ctx.sbuf("mo_onesr", [1, 128], F32)
    eyef = ctx.sbuf("mo_eyef", [128, 128], F32)
    wg = ctx.sbuf("mo_wg", [128, 2 * ND * 128], BF16)
    wu = ctx.sbuf("mo_wu", [128, 2 * ND * 128], BF16)
    wd = ctx.sbuf("mo_wd", [128, 2 * NIC * 512], BF16)
    hsb = ctx.sbuf("mo_h", [128, NIC * HT], BF16)
    sg = ctx.sbuf("mo_sg", [128, 1024], F32)
    cb = ctx.sbuf("mo_cb", [128, HT], F32)
    crow = ctx.sbuf("mo_crow", [1, HT], F32)
    dall = ctx.sbuf("mo_dall", [128, 16 * 512], F32)
    prob = ctx.sbuf("mo_prob", [128, 16], F32)
    tst = ctx.sbuf("mo_tst", [128, 8], F32)
    ctok = ctx.sbuf("mo_ctok", [128, 16], F32)
    ctr = ctx.sbuf("mo_ctr", [16, 128], F32)
    x0r, x1r = None, None
    x0r = dall[:, 0:N]
    x1r = dall[:, N:2 * N]
    psG = ctx.psum("mo_psG", [128, 1024], F32)
    psU = ctx.psum("mo_psU", [128, 1024], F32)
    psD = ctx.psum("mo_psD", [128, 1024], F32)
    psB = ctx.psum("mo_psB", [128, 512], F32)
    psT = ctx.psum("mo_psT", [16, 128], F32)
    psL = psD

    LDXC, LDXN, LDCR, LDF, LDW, MST, CP, CE, CS, STO = (ctx.chain() for _ in range(10))
    for nm in ("xnt", "x1a"):
        dv = deps.get(nm) if deps else None
        if dv:
            def _w(sync, dv=dv):
                sync.wait_ge(dv[0], dv[1])
            block.sync(_w)

    def _ld0(sync):
        LDXC.inc(sync.dma_start(out=onesr[:], in_=A["onesr"][:]), 16)
        LDXC.inc(sync.dma_start(out=eyef[:], in_=A["eye_f32"][:]), 16)
    block.sync(_ld0)

    def _ldg(gpsimd):
        LDW.inc(gpsimd.dma_start(out=gw[:].rearrange("p (c o) -> p c o", c=ND),
                                 in_=A["gwt"].rearrange("(c p) o -> p c o", p=128)), 16)
        LDW.inc(gpsimd.dma_start(out=sgwb[:], in_=A["sgw"][:]), 16)
        LDW.wait(gpsimd)
    block.gpsimd(_ldg)

    GUK, GUG, DNK, DNG = [0], [0], [0], [0]
    GUMM, SILU, HDONE, DNMM, DEV = {}, {}, {}, {}, {}
    LDWBASE = 32
    pending_acc = [None]

    def flush_acc():
        if pending_acc[0] is not None:
            f = pending_acc[0]
            pending_acc[0] = None
            block.gpsimd(f)
    for s in range(1):
        t0 = 0

        def _ldxn(sync, s=s, t0=t0):
            CP.wait(sync)
            for c in range(ND):
                LDXN.inc(sync.dma_start(out=xnf[:, c * HT:(c + 1) * HT],
                                        in_=xnt[t0:t0 + HT, c * 128:(c + 1) * 128],
                                        transpose=True), 16)
        block.sync(_ldxn)

        for tt in range(16):
            def _pe_l(tensor, s=s, tt=tt):
                LDXN.wait(tensor)
                LDXC.wait(tensor)
                LDW.wait(tensor)
                CE.wait(tensor)
                for c in range(ND):
                    tensor.matmul(psL[:, 0:8],
                                  xnf[:, c * HT + tt * 128: c * HT + (tt + 1) * 128],
                                  gw[:, c * 8:(c + 1) * 8],
                                  start=(c == 0), stop=(c == ND - 1))
                for c in range(ND):
                    mm = tensor.matmul(psL[:, 8:9],
                                       xnf[:, c * HT + tt * 128: c * HT + (tt + 1) * 128],
                                       sgwb[:, c:c + 1],
                                       start=(c == 0), stop=(c == ND - 1))
                CP.inc(mm)
            block.tensor(_pe_l)

            def _ve_sm(vector, s=s, tt=tt):
                CP.wait(vector)
                CE.inc(vector.reduce_max(tst[:, 4:5], psL[:, 0:8], axis=AX.X,
                                         negate=True))
            block.vector(_ve_sm)

            def _sc_sm(scalar, s=s, tt=tt):
                CE.wait(scalar)
                CS.inc(scalar.activation(prob[:, 0:8], psL[:, 0:8], AF.Exp,
                                         bias=tst[:, 4:5], accum_out=tst[:, 5:6]))
                CS.wait(scalar)
                CS.inc(scalar.activation(prob[:, 8:9], psL[:, 8:9], AF.Sigmoid))
            block.scalar(_sc_sm)

            def _ve_c(vector, s=s, tt=tt):
                CS.wait(vector)
                CE.inc(vector.reciprocal(tst[:, 6:7], tst[:, 5:6]))
                CE.wait(vector)
                CE.inc(vector.tensor_scalar(prob[:, 0:8], prob[:, 0:8], tst[:, 6:7],
                                            None, ALU.mult))
                CE.wait(vector)
                CE.inc(vector.reduce_max(tst[:, 0:1], prob[:, 0:8], axis=AX.X))
                CE.wait(vector)
                CE.inc(vector.tensor_scalar(ctok[:, 0:8], prob[:, 0:8], tst[:, 0:1],
                                            None, ALU.is_ge))
                CE.wait(vector)
                CE.inc(vector.scalar_tensor_tensor(ctok[:, 8:16], ctok[:, 0:8], -1e30,
                                                   prob[:, 0:8], ALU.mult, ALU.add))
                CE.wait(vector)
                CE.inc(vector.reduce_max(tst[:, 1:2], ctok[:, 8:16], axis=AX.X))
                CE.wait(vector)
                CE.inc(vector.tensor_add(tst[:, 2:3], tst[:, 0:1], tst[:, 1:2]))
                CE.wait(vector)
                CE.inc(vector.reciprocal(tst[:, 3:4], tst[:, 2:3]))
                CE.wait(vector)
                CE.inc(vector.tensor_scalar(ctok[:, 0:8], prob[:, 0:8], tst[:, 1:2],
                                            None, ALU.is_ge))
                CE.wait(vector)
                CE.inc(vector.tensor_tensor(ctok[:, 0:8], ctok[:, 0:8], prob[:, 0:8],
                                            ALU.mult))
                CE.wait(vector)
                CE.inc(vector.tensor_scalar(ctok[:, 0:8], ctok[:, 0:8], tst[:, 3:4],
                                            None, ALU.mult))
                CE.wait(vector)
                CE.inc(vector.tensor_copy(ctok[:, 8:9], prob[:, 8:9]))
            block.vector(_ve_c)

            def _pe_t(tensor, s=s, tt=tt):
                CE.wait(tensor)
                CP.inc(tensor.transpose(psT[:], ctok[:], eyef[:]))
            block.tensor(_pe_t)

            def _ve_t(vector, s=s, tt=tt):
                CP.wait(vector)
                STO.wait(vector)
                CE.inc(vector.tensor_copy(ctr[:], psT[:, 0:128]))
            block.vector(_ve_t)

            def _sy_t(sync, s=s, tt=tt, t0=t0):
                CE.wait(sync)
                STO.inc(sync.dma_start(out=cmbd[:, t0 + tt * 128: t0 + (tt + 1) * 128],
                                       in_=ctr[:]), 16)
            block.sync(_sy_t)

        for e in range(12):
            if e < 8:
                wgs = A["w_gate"][e]
                wus = A["w_up"][e]
                wds = A["w_down"][e]
                cri = e
            else:
                b = e - 8
                wgs = A["sw_gate"][:, b * NI:(b + 1) * NI]
                wus = A["sw_up"][:, b * NI:(b + 1) * NI]
                wds = A["sw_down"][b * NI:(b + 1) * NI, :]
                cri = 8

            def _ldcr(sync, cri=cri, t0=t0, e=e):
                CP.wait(sync)
                STO.wait(sync)
                LDCR.inc(sync.dma_start(out=crow[:], in_=cmbd[cri:cri + 1, t0:t0 + HT]), 16)
            block.sync(_ldcr)

            for b2 in range(4):
                def _pe_b(tensor, b2=b2, e=e):
                    LDCR.wait(tensor)
                    LDXC.wait(tensor)
                    CE.wait(tensor)
                    CP.inc(tensor.matmul(psB[:], onesr[:],
                                         crow[0:1, b2 * 512:(b2 + 1) * 512],
                                         start=True, stop=True))
                block.tensor(_pe_b)

                def _ve_b(vector, b2=b2, e=e):
                    CP.wait(vector)
                    CE.inc(vector.tensor_copy(cb[:, b2 * 512:(b2 + 1) * 512],
                                              psB[:]))
                block.vector(_ve_b)

            for ic in range(NIC):
                kk = GUK[0]
                GUK[0] += 1
                kb = (kk % 2) * ND * 128

                def _ldw(gpsimd, wgs=wgs, wus=wus, ic=ic, kk=kk, kb=kb):
                    if kk >= 2:
                        gpsimd.wait_ge(CP.sem, GUMM.get(kk - 2, 0))
                    LDW.inc(gpsimd.dma_start(
                        out=wg[:, kb:kb + ND * 128].rearrange("p (c o) -> p c o", c=ND),
                        in_=wgs[:, ic * 128:(ic + 1) * 128]
                        .rearrange("(c p) o -> p c o", p=128)), 16)
                    LDW.inc(gpsimd.dma_start(
                        out=wu[:, kb:kb + ND * 128].rearrange("p (c o) -> p c o", c=ND),
                        in_=wus[:, ic * 128:(ic + 1) * 128]
                        .rearrange("(c p) o -> p c o", p=128)), 16)
                    LDW.wait(gpsimd)
                block.gpsimd(_ldw)
                flush_acc()
                LDWT = LDW.n

                for t4 in range(4):
                    gg = GUG[0]
                    GUG[0] += 1
                    sl = (gg % 2) * 512

                    def _pe_gu(tensor, ic=ic, t4=t4, e=e, kk=kk, kb=kb, gg=gg, sl=sl,
                               LDWT=None):
                        tensor.wait_ge(LDW.sem, LDWT)
                        LDXN.wait(tensor)
                        if gg >= 2:
                            tensor.wait_ge(CS.sem, SILU.get(gg - 2, 0))
                            tensor.wait_ge(CE.sem, HDONE.get(gg - 2, 0))
                        for c in range(ND):
                            tensor.matmul(psG[:, sl:sl + 512],
                                          wg[:, kb + c * 128: kb + (c + 1) * 128],
                                          xnf[:, c * HT + t4 * 512: c * HT + (t4 + 1) * 512],
                                          start=(c == 0), stop=(c == ND - 1))
                        for c in range(ND):
                            mm = tensor.matmul(psU[:, sl:sl + 512],
                                               wu[:, kb + c * 128: kb + (c + 1) * 128],
                                               xnf[:, c * HT + t4 * 512: c * HT + (t4 + 1) * 512],
                                               start=(c == 0), stop=(c == ND - 1))
                        CP.inc(mm)
                        GUMM[kk] = CP.n
                    block.tensor(lambda tensor, f=_pe_gu, L=LDWT, ic=ic, t4=t4, e=e,
                                 kk=kk, kb=kb, gg=gg, sl=sl:
                                 f(tensor, ic=ic, t4=t4, e=e, kk=kk, kb=kb, gg=gg,
                                   sl=sl, LDWT=L))

                    def _sc_g(scalar, gg=gg, sl=sl):
                        CP.wait(scalar)
                        if gg >= 2:
                            scalar.wait_ge(CE.sem, HDONE.get(gg - 2, 0))
                        CS.inc(scalar.activation(sg[:, sl:sl + 512],
                                                 psG[:, sl:sl + 512], AF.Silu))
                        SILU[gg] = CS.n
                    block.scalar(_sc_g)

                    def _ve_h(vector, ic=ic, t4=t4, gg=gg, sl=sl):
                        CS.wait(vector)
                        CE.inc(vector.tensor_tensor(sg[:, sl:sl + 512], sg[:, sl:sl + 512],
                                                    psU[:, sl:sl + 512], ALU.mult))
                        CE.wait(vector)
                        CE.inc(vector.tensor_tensor(
                            hsb[:, ic * HT + t4 * 512: ic * HT + (t4 + 1) * 512],
                            sg[:, sl:sl + 512], cb[:, t4 * 512:(t4 + 1) * 512], ALU.mult))
                        HDONE[gg] = CE.n
                    block.vector(_ve_h)

            hdone_e = CE.n
            for d4 in range(4):
                dk = DNK[0]
                DNK[0] += 1
                db = (dk % 2) * NIC * 512

                def _ldwd(gpsimd, wds=wds, d4=d4, dk=dk, db=db):
                    if dk >= 2:
                        gpsimd.wait_ge(CP.sem, DNMM.get(dk - 2, 0))
                    LDW.inc(gpsimd.dma_start(
                        out=wd[:, db:db + NIC * 512].rearrange("p (c o) -> p c o", c=NIC),
                        in_=wds[:, d4 * 512:(d4 + 1) * 512]
                        .rearrange("(c p) o -> p c o", p=128)), 16)
                    LDW.wait(gpsimd)
                block.gpsimd(_ldwd)
                flush_acc()
                wdtarget = LDW.n

                for tt in range(16):
                    dg = DNG[0]
                    DNG[0] += 1
                    dsl = (dg % 2) * 512

                    def _pe_d(tensor, d4=d4, tt=tt, e=e, dk=dk, db=db, dg=dg, dsl=dsl,
                              wdtarget=wdtarget, hdone_e=hdone_e):
                        tensor.wait_ge(LDW.sem, wdtarget)
                        tensor.wait_ge(CE.sem, hdone_e)
                        if dg >= 2:
                            tensor.wait_ge(CE.sem, DEV.get(dg - 2, 0))
                        for ic in range(NIC):
                            mm = tensor.matmul(
                                psD[:, dsl:dsl + 512],
                                hsb[:, ic * HT + tt * 128: ic * HT + (tt + 1) * 128],
                                wd[:, db + ic * 512: db + (ic + 1) * 512],
                                start=(ic == 0), stop=(ic == NIC - 1))
                        CP.inc(mm)
                        DNMM[dk] = CP.n
                    block.tensor(_pe_d)

                    def _ve_d(vector, d4=d4, tt=tt, e=e, dg=dg, dsl=dsl):
                        CP.wait(vector)
                        MST.wait(vector)
                        CE.inc(vector.tensor_copy(dall[:, tt * 512:(tt + 1) * 512],
                                                  psD[:, dsl:dsl + 512]))
                        DEV[dg] = CE.n
                    block.vector(_ve_d)

                def _acc(gpsimd, d4=d4, e=e, t0=t0):
                    CE.wait(gpsimd)
                    MST.wait(gpsimd)
                    op = ALU.bypass if e == 0 else ALU.add
                    MST.inc(gpsimd.dma_start(
                        out=macc[t0:t0 + HT, d4 * 512:(d4 + 1) * 512]
                        .rearrange("(a p) o -> p a o", p=128),
                        in_=dall[:].rearrange("p (a o) -> p a o", a=16),
                        accum_op=op), 16)
                pending_acc[0] = _acc

    flush_acc()
    for i in range(NT):
        def _ldf(gpsimd, i=i):
            MST.wait(gpsimd)
            CE.wait(gpsimd)
            STO.wait(gpsimd)
            LDF.inc(gpsimd.dma_start(out=x0r[:], in_=x1a[i * 128:(i + 1) * 128, :]), 16)
            LDF.inc(gpsimd.dma_start(out=x1r[:], in_=macc[i * 128:(i + 1) * 128, :]), 16)
            LDF.wait(gpsimd)
        block.gpsimd(_ldf)

        def _vef(vector, i=i):
            LDF.wait(vector)
            CE.inc(vector.tensor_add(x1r[:], x1r[:], x0r[:]))
        block.vector(_vef)

        def _stf(sync, i=i):
            CE.wait(sync)
            STO.inc(sync.dma_start(out=x1b[i * 128:(i + 1) * 128, :], in_=x1r[:]), 16)
        block.sync(_stf)

    def _fin(sync):
        STO.wait(sync)
        LDXC.wait(sync)
        LDXN.wait(sync)
        LDCR.wait(sync)
        LDF.wait(sync)
        LDW.wait(sync)
        MST.wait(sync)
    block.sync(_fin)
    return {"x1b": (STO.sem, STO.n)}


VARS["x1b"] = ((N, D), F32)
CONSTS_SHAPES.update({
    "gwt": ((D, 8), F32), "sgw": ((128, ND), F32),
    "w_gate": ((8, D, 1408), F32), "w_up": ((8, D, 1408), F32),
    "w_down": ((8, 1408, D), F32),
    "sw_gate": ((D, 5632), F32), "sw_up": ((D, 5632), F32),
    "sw_down": ((5632, D), F32),
})


NROW1 = 4416


@phase("l1proj", ins=["xn1t", "W_inT"], outs=["xbcf", "zt"])
def ph_l1proj(ctx, deps):
    block = ctx.block
    A = ctx.aps
    xn1t, W_inT, xbcf, zt = A["xn1t"], A["W_inT"], A["xbcf"], A["zt"]

    xnf = ctx.sbuf("p1_xnf", [128, ND * N], BF16)
    wch = ctx.sbuf("p1_wch", [128, ND * 128], BF16)
    wz = ctx.sbuf("p1_wz", [128, ND * 512], BF16)
    orow = ctx.sbuf("p1_orow", [128, N], F32)
    zrow = ctx.sbuf("p1_zrow", [128, 512 * 4], F32)
    psA = ctx.psum("p1_psA", [128, 512], F32)
    LDX, LDW, CP, CE, STO = (ctx.chain() for _ in range(5))

    dv = deps.get("xn1t") if deps else None
    if dv:
        def _w(sync):
            sync.wait_ge(dv[0], dv[1])
        block.sync(_w)

    def _ldxn(sync):
        for c in range(ND):
            LDX.inc(sync.dma_start(out=xnf[:, c * N:(c + 1) * N],
                                   in_=xn1t[:, c * 128:(c + 1) * 128], transpose=True), 16)
    block.sync(_ldxn)

    for rc in range(35):
        rows = 128 if rc < 34 else 64
        r0 = 4096 + rc * 128

        def _ldw(gpsimd, rc=rc, rows=rows, r0=r0):
            CP.wait(gpsimd)
            LDW.inc(gpsimd.dma_start(
                out=wch[:, 0:ND * rows].rearrange("p (c o) -> p c o", c=ND),
                in_=W_inT[:, r0:r0 + rows].rearrange("(c p) o -> p c o", p=128)), 16)
        block.gpsimd(_ldw)

        for t4 in range(4):
            def _pe(tensor, rc=rc, rows=rows, t4=t4):
                LDW.wait(tensor)
                LDX.wait(tensor)
                CE.wait(tensor)
                for c in range(ND):
                    mm = tensor.matmul(psA[:, 0:512] if rows == 128 else psA[0:64, 0:512],
                                       wch[:, 0:ND * rows]
                                       .rearrange("p (c o) -> p c o", c=ND)[:, c, :],
                                       xnf[:, c * N + t4 * 512: c * N + (t4 + 1) * 512],
                                       start=(c == 0), stop=(c == ND - 1))
                CP.inc(mm)
            block.tensor(_pe)

            def _ve(tensor_v, rc=rc, rows=rows, t4=t4):
                CP.wait(tensor_v)
                if t4 == 0:
                    STO.wait(tensor_v)
                CE.inc(tensor_v.tensor_copy(orow[0:rows, t4 * 512:(t4 + 1) * 512],
                                            psA[0:rows, 0:512]))
            block.vector(_ve)

        def _st(sync, rc=rc, rows=rows):
            CE.wait(sync)
            STO.inc(sync.dma_start(out=xbcf[rc * 128: rc * 128 + rows, :],
                                   in_=orow[0:rows, :]), 16)
        block.sync(_st)

    for zc in range(8):
        def _ldwz(gpsimd, zc=zc):
            CP.wait(gpsimd)
            LDW.inc(gpsimd.dma_start(
                out=wz[:].rearrange("p (c o) -> p c o", c=ND),
                in_=W_inT[:, zc * 512:(zc + 1) * 512].rearrange("(c p) o -> p c o", p=128)), 16)
        block.gpsimd(_ldwz)

        for tt in range(NT):
            def _pe(tensor, zc=zc, tt=tt):
                LDW.wait(tensor)
                LDX.wait(tensor)
                CE.wait(tensor)
                for c in range(ND):
                    mm = tensor.matmul(psA[:],
                                       xnf[:, c * N + tt * 128: c * N + (tt + 1) * 128],
                                       wz[:, c * 512:(c + 1) * 512],
                                       start=(c == 0), stop=(c == ND - 1))
                CP.inc(mm)
            block.tensor(_pe)

            def _ve(vector, zc=zc, tt=tt):
                CP.wait(vector)
                if tt % 4 == 0:
                    STO.wait(vector)
                CE.inc(vector.tensor_copy(zrow[:, (tt % 4) * 512:(tt % 4 + 1) * 512],
                                          psA[:]))
            block.vector(_ve)

            if tt % 4 == 3:
                def _st(sync, zc=zc, tt=tt):
                    CE.wait(sync)
                    STO.inc(sync.dma_start(
                        out=zt[(tt - 3) * 128:(tt + 1) * 128, zc * 512:(zc + 1) * 512]
                        .rearrange("(a p) o -> p a o", p=128),
                        in_=zrow[:].rearrange("p (a o) -> p a o", a=4)), 16)
                block.sync(_st)

    def _fin(sync):
        STO.wait(sync)
        LDX.wait(sync)
        LDW.wait(sync)
    block.sync(_fin)
    return {"xbcf": (STO.sem, STO.n), "zt": (STO.sem, STO.n)}


VARS["xbcf"] = ((NROW1, N), F32)
VARS["zt"] = ((N, 4096), F32)
CONSTS_SHAPES["W_inT"] = ((D, 8512), F32)


@phase("l1conv", ins=["xbcf", "convw", "convb"], outs=["xcb"])
def ph_l1conv(ctx, deps):
    block = ctx.block
    A = ctx.aps
    xbcf, xcb = A["xbcf"], A["xcb"]
    NCC = 34

    xp = ctx.sbuf("cv_xp", [128, N + 4], F32)
    acc = ctx.sbuf("cv_acc", [128, N], F32)
    ob = ctx.sbuf("cv_ob", [128, N], BF16)
    cw = ctx.sbuf("cv_cw", [128, NCC * 4], F32)
    cbb = ctx.sbuf("cv_cb", [128, NCC], F32)
    LDX, CP, CE, CS, STO = (ctx.chain() for _ in range(5))

    dv = deps.get("xbcf") if deps else None
    if dv:
        def _w(sync):
            sync.wait_ge(dv[0], dv[1])
        block.sync(_w)

    def _ld0(sync):
        LDX.inc(sync.dma_start(out=cw[:].rearrange("p (c k) -> p c k", c=NCC),
                               in_=A["convw"].rearrange("(c p) k -> p c k", p=128)), 16)
        LDX.inc(sync.dma_start(out=cbb[:], in_=A["convb"][:]), 16)
    block.sync(_ld0)

    def _init(vector):
        CE.inc(vector.memset(xp[:, 0:4], 0.0))
    block.vector(_init)

    for cc in range(NCC):
        def _ldx(sync, cc=cc):
            CS.wait(sync)
            CE.wait(sync)
            LDX.inc(sync.dma_start(out=xp[:, 3:3 + N],
                                   in_=xbcf[cc * 128:(cc + 1) * 128, :]), 16)
        block.sync(_ldx)

        def _ve(vector, cc=cc):
            LDX.wait(vector)
            CE.wait(vector)
            CE.inc(vector.tensor_scalar(acc[:], xp[:, 0:N],
                                        cw[:, cc * 4:cc * 4 + 1], None, ALU.mult))
            for j in range(1, 4):
                CE.swait(vector)
                CE.inc(vector.scalar_tensor_tensor(acc[:], xp[:, j:j + N],
                                                   cw[:, cc * 4 + j:cc * 4 + j + 1],
                                                   acc[:], ALU.mult, ALU.add))
        block.vector(_ve)

        def _sc(scalar, cc=cc):
            CE.wait(scalar)
            if cc >= 1:
                STO.wait(scalar)
            CS.inc(scalar.activation(ob[:], acc[:], AF.Silu,
                                     bias=cbb[:, cc:cc + 1]))
        block.scalar(_sc)

        def _st(sync, cc=cc):
            CS.wait(sync)
            STO.inc(sync.dma_start(out=xcb[cc * 128:(cc + 1) * 128, :], in_=ob[:]), 16)
        block.sync(_st)

    def _fin(sync):
        STO.wait(sync)
        LDX.wait(sync)
    block.sync(_fin)
    return {"xcb": (STO.sem, STO.n)}


VARS["xcb"] = ((4352, N), BF16)
CONSTS_SHAPES["convw"] = ((4352, 4), F32)
CONSTS_SHAPES["convb"] = ((128, 34), F32)


@phase("l1scan", ins=["xcb", "xbcf", "dt_biasv", "A_logv", "d_rep", "onesr",
                      "eye_bf", "eye_f32", "mnegf"],
       outs=["yd"])
def ph_l1scan(ctx, deps):
    block = ctx.block
    A = ctx.aps
    xcb, xbcf, yd = A["xcb"], A["xbcf"], A["yd"]
    NHm, HDm, Q, NCH = 64, 64, 128, 16
    DS = 4096

    bch = ctx.sbuf("sc_bch", [128, N], BF16)
    cch = ctx.sbuf("sc_cch", [128, N], BF16)
    dtf = ctx.sbuf("sc_dtf", [64, N], F32)
    dtb = ctx.sbuf("sc_dtb", [64, 1], F32)
    alg = ctx.sbuf("sc_alg", [64, 4], F32)
    drep = ctx.sbuf("sc_drep", [1, DS], F32)
    onesr = ctx.sbuf("sc_onesr", [1, 128], F32)
    eyeb = ctx.sbuf("sc_eyeb", [128, 128], BF16)
    eyef = ctx.sbuf("sc_eyef", [128, 128], F32)
    mneg = ctx.sbuf("sc_mneg", [128, 128], F32)
    ones_t = ctx.sbuf("sc_ones", [64, N], F32)
    sp1 = ctx.sbuf("sc_sp1", [64, N], F32)
    sp2 = ctx.sbuf("sc_sp2", [64, N], F32)
    dts = ctx.sbuf("sc_dts", [64, N], F32)
    cg = ctx.sbuf("sc_cg", [64, N], F32)
    ncg = ctx.sbuf("sc_ncg", [64, N], F32)
    cgt = ctx.sbuf("sc_cgt", [128, NCH * 64], F32)
    dtst = ctx.sbuf("sc_dtst", [128, NCH * 64], F32)
    dbc = ctx.sbuf("sc_dbc", [128, DS], F32)
    st_f = ctx.sbuf("sc_stf", [128, DS], F32)
    st_b = ctx.sbuf("sc_stb", [128, DS], BF16)
    klhs = ctx.sbuf("sc_klhs", [2, 1024], F32)
    krhs = ctx.sbuf("sc_krhs", [2, 1024], F32)
    csr = ctx.sbuf("sc_csr", [1, 64], F32)
    cer = ctx.sbuf("sc_cer", [1, 64], F32)
    erow = ctx.sbuf("sc_erow", [1, 64], F32)
    ecirel = ctx.sbuf("sc_eci", [128, 64], F32)
    wfac = ctx.sbuf("sc_wfac", [128, 64], F32)
    etb = ctx.sbuf("sc_etb", [128, 64], F32)
    tsc1 = ctx.sbuf("sc_tsc1", [128, 64], F32)
    xt_tok = ctx.sbuf("sc_xtok", [128, DS], BF16)
    xw = ctx.sbuf("sc_xw", [128, DS], BF16)
    xdt = ctx.sbuf("sc_xdt", [128, DS], BF16)
    btok = ctx.sbuf("sc_btok", [128, 128], BF16)
    dq = ctx.sbuf("sc_dq", [128, 1024], F32)
    dex = ctx.sbuf("sc_dex", [128, 1024], F32)
    mgt = ctx.sbuf("sc_mgt", [128, 1024], BF16)
    ty = ctx.sbuf("sc_ty", [128, 512], F32)
    ty2 = ctx.sbuf("sc_ty2", [128, 512], F32)
    yrow = ctx.sbuf("sc_yrow", [128, DS], F32)
    psD = ctx.psum("sc_psD", [128, 1024], F32)
    psY = ctx.psum("sc_psY", [128, 512], F32)
    psI = ctx.psum("sc_psI", [128, 512], F32)
    psS2 = ctx.psum("sc_psS2", [128, 512], F32)
    psG2 = ctx.psum("sc_psG2", [128, 128], F32)
    psB2 = ctx.psum("sc_psB2", [128, 64], F32)
    pstr2 = ctx.psum("sc_pstr2", [128, 128], BF16)

    LDC, LDR, LDT, CP, CE, CS, STO = (ctx.chain() for _ in range(7))
    nc = ctx.nc
    cgd = nc.dram_tensor("sc_cgd", [64, N], F32, kind="Internal").ap()
    ncgd = nc.dram_tensor("sc_ncgd", [64, N], F32, kind="Internal").ap()
    cgtd = nc.dram_tensor("sc_cgtd", [N, 64], F32, kind="Internal").ap()

    for nm in ("xcb", "xbcf"):
        dv = deps.get(nm) if deps else None
        if dv:
            def _w(sync, dv=dv):
                sync.wait_ge(dv[0], dv[1])
            block.sync(_w)

    def _ld0(sync):
        LDC.inc(sync.dma_start(out=bch[:], in_=xcb[4096:4224, :]), 16)
        LDC.inc(sync.dma_start(out=cch[:], in_=xcb[4224:4352, :]), 16)
        LDC.inc(sync.dma_start(out=dtf[:], in_=xbcf[4352:4416, :]), 16)
        LDC.inc(sync.dma_start(out=dtb[:], in_=A["dt_biasv"][:]), 16)
        LDC.inc(sync.dma_start(out=alg[:, 0:1], in_=A["A_logv"][:]), 16)
        LDC.inc(sync.dma_start(out=drep[:], in_=A["d_rep"][:]), 16)
        LDC.inc(sync.dma_start(out=onesr[:], in_=A["onesr"][:]), 16)
        LDC.inc(sync.dma_start(out=eyeb[:], in_=A["eye_bf"][:]), 16)
        LDC.inc(sync.dma_start(out=eyef[:], in_=A["eye_f32"][:]), 16)
        LDC.inc(sync.dma_start(out=mneg[:], in_=A["mnegf"][:]), 16)
    block.sync(_ld0)

    def _sp_sc(scalar):
        LDC.wait(scalar)
        CS.inc(scalar.activation(sp1[:], dtf[:], AF.Relu, bias=dtb[:, 0:1]))
        CS.wait(scalar)
        CS.inc(scalar.activation(sp2[:], dtf[:], AF.Abs, bias=dtb[:, 0:1]))
        CS.wait(scalar)
        CS.inc(scalar.activation(sp2[:], sp2[:], AF.Exp, scale=-1.0))
    block.scalar(_sp_sc)

    def _sp_ve(vector):
        CS.wait(vector)
        CE.inc(vector.tensor_scalar(sp2[:], sp2[:], 1.0, None, ALU.add))
    block.vector(_sp_ve)

    def _sp_sc2(scalar):
        CE.wait(scalar)
        CS.inc(scalar.activation(sp2[:], sp2[:], AF.Ln))
        CS.wait(scalar)
        CS.inc(scalar.activation(alg[:, 1:2], alg[:, 0:1], AF.Exp))
    block.scalar(_sp_sc2)

    def _sp_ve2(vector):
        CS.wait(vector)
        CE.inc(vector.tensor_add(dts[:], sp1[:], sp2[:]))
        CE.wait(vector)
        CE.inc(vector.tensor_scalar(alg[:, 2:3], alg[:, 1:2], -1.0, None, ALU.mult))
        CE.wait(vector)
        CE.inc(vector.tensor_scalar(sp1[:], dts[:], alg[:, 2:3], None, ALU.mult))
        CE.wait(vector)
        CE.inc(vector.memset(ones_t[:], 1.0))
        CE.wait(vector)
        CE.inc(vector.tensor_tensor_scan(cg[:], ones_t[:], sp1[:], 0.0,
                                         ALU.mult, ALU.add))
        CE.wait(vector)
        CE.inc(vector.tensor_scalar(ncg[:], cg[:], -1.0, None, ALU.mult))
        CE.wait(vector)
        CE.inc(vector.memset(klhs[:], 1.0))
        CE.wait(vector)
        CE.inc(vector.memset(krhs[:], 1.0))
        CE.wait(vector)
        CE.inc(vector.memset(st_f[:], 0.0))
        CE.wait(vector)
        CE.inc(vector.memset(st_b[:], 0.0))
    block.vector(_sp_ve2)

    def _stg(sync):
        CE.wait(sync)
        STO.inc(sync.dma_start(out=cgd[:], in_=cg[:]), 16)
        STO.inc(sync.dma_start(out=ncgd[:], in_=ncg[:]), 16)
    block.sync(_stg)

    for c in range(NCH):
        def _tr(tensor, c=c):
            CE.wait(tensor)
            CP.inc(tensor.transpose(pstr2[:].bitcast(F32)[:, 0:64] if False else psB2[:],
                                    cg[:, c * 128:(c + 1) * 128], eyef[0:64, 0:64]))
            CP.wait(tensor)
            CP.inc(tensor.transpose(psG2[:, 0:64], dts[:, c * 128:(c + 1) * 128],
                                    eyef[0:64, 0:64]))
        block.tensor(_tr)

        def _cp(vector, c=c):
            CP.wait(vector)
            STO.wait(vector)
            CE.inc(vector.tensor_copy(cgt[:, c * 64:(c + 1) * 64], psB2[:]))
            CE.swait(vector)
            CE.inc(vector.tensor_copy(dtst[:, c * 64:(c + 1) * 64], psG2[:, 0:64]))
        block.vector(_cp)

        def _st(sync, c=c):
            CE.wait(sync)
            STO.inc(sync.dma_start(out=cgtd[c * 128:(c + 1) * 128, :],
                                   in_=cgt[:, c * 64:(c + 1) * 64]), 16)
        block.sync(_st)

    for b8 in range(8):
        def _pe(tensor, b8=b8):
            LDC.wait(tensor)
            CE.wait(tensor)
            CP.inc(tensor.matmul(psY[:], onesr[:], drep[0:1, b8 * 512:(b8 + 1) * 512],
                                 start=True, stop=True))
        block.tensor(_pe)

        def _ve(vector, b8=b8):
            CP.wait(vector)
            CE.inc(vector.tensor_copy(dbc[:, b8 * 512:(b8 + 1) * 512], psY[:]))
        block.vector(_ve)

    for c in range(NCH):
        def _ldr(sync, c=c):
            CP.wait(sync)
            CE.wait(sync)
            STO.wait(sync)
            if c > 0:
                LDR.inc(sync.dma_start(out=csr[:], in_=cgtd[c * 128 - 1:c * 128, :]), 16)
            LDR.inc(sync.dma_start(out=cer[:], in_=cgtd[(c + 1) * 128 - 1:(c + 1) * 128, :]), 16)
        block.sync(_ldr)

        def _ldt(sync, c=c):
            CE.wait(sync)
            LDT.inc(sync.dma_start(out=xt_tok[:], in_=xcb[0:4096, c * 128:(c + 1) * 128],
                                   transpose=True), 16)
        block.sync(_ldt)

        def _pe_cs(tensor, c=c):
            LDR.wait(tensor)
            CE.wait(tensor)
            if c > 0:
                tensor.matmul(psB2[:], onesr[:], csr[0:1, :], start=True, stop=True)
            CP.inc(tensor.matmul(psI[:, 0:64], onesr[:], cer[0:1, :], start=True, stop=True))
        block.tensor(_pe_cs)

        def _ve_cs(vector, c=c):
            CP.wait(vector)
            if c > 0:
                CE.inc(vector.tensor_tensor(tsc1[:], cgt[:, c * 64:(c + 1) * 64],
                                            psB2[:], ALU.subtract))
            else:
                CE.inc(vector.tensor_copy(tsc1[:], cgt[:, 0:64]))
            CE.swait(vector)
            CE.inc(vector.tensor_tensor(wfac[:], psI[:, 0:64],
                                        cgt[:, c * 64:(c + 1) * 64], ALU.subtract))
        block.vector(_ve_cs)

        def _sc_cs(scalar, c=c):
            CE.wait(scalar)
            CS.inc(scalar.activation(ecirel[:], tsc1[:], AF.Exp))
            CS.wait(scalar)
            CS.inc(scalar.activation(wfac[:], wfac[:], AF.Exp))
        block.scalar(_sc_cs)

        def _ve_cs2(vector, c=c):
            CS.wait(vector)
            CE.inc(vector.tensor_tensor(wfac[:], wfac[:], dtst[:, c * 64:(c + 1) * 64],
                                        ALU.mult))
            if c > 0:
                CE.swait(vector)
                CE.inc(vector.tensor_tensor(erow[:], cer[:], csr[:], ALU.subtract))
        block.vector(_ve_cs2)

        def _sc_cs2(scalar, c=c):
            CE.wait(scalar)
            CS.inc(scalar.activation(erow[:], erow[:] if c > 0 else cer[:], AF.Exp))
        block.scalar(_sc_cs2)

        def _pe_et(tensor, c=c):
            CS.wait(tensor)
            CE.wait(tensor)
            CP.inc(tensor.matmul(psB2[:], onesr[:], erow[:], start=True, stop=True))
        block.tensor(_pe_et)

        def _ve_et(vector, c=c):
            CP.wait(vector)
            CE.inc(vector.tensor_copy(etb[:], psB2[:]))
        block.vector(_ve_et)

        def _ve_x(vector, c=c):
            LDT.wait(vector)
            CE.swait(vector)
            wb3 = wfac[:].unsqueeze(2).broadcast_to([128, 64, 64])
            CE.inc(vector.tensor_tensor(xw[:].rearrange("p (h d) -> p h d", h=64),
                                        xt_tok[:].rearrange("p (h d) -> p h d", h=64),
                                        wb3, ALU.mult))
            CE.swait(vector)
            db3 = dtst[:, c * 64:(c + 1) * 64].unsqueeze(2).broadcast_to([128, 64, 64])
            CE.inc(vector.tensor_tensor(xdt[:].rearrange("p (h d) -> p h d", h=64),
                                        xt_tok[:].rearrange("p (h d) -> p h d", h=64),
                                        db3, ALU.mult))
        block.vector(_ve_x)

        def _pe_bg(tensor, c=c):
            LDC.wait(tensor)
            CE.wait(tensor)
            CP.inc(tensor.transpose(pstr2[:], bch[:, c * 128:(c + 1) * 128], eyeb[:]))
            CP.wait(tensor)
            CP.inc(tensor.matmul(psG2[:], bch[:, c * 128:(c + 1) * 128],
                                 cch[:, c * 128:(c + 1) * 128], start=True, stop=True))
        block.tensor(_pe_bg)

        def _ve_bt(vector, c=c):
            CP.wait(vector)
            CE.inc(vector.tensor_copy(btok[:], pstr2[:]))
        block.vector(_ve_bt)

        for hg in range(8):
            h0 = hg * 8

            def _ld_k(sync, c=c, h0=h0):
                CP.wait(sync)
                LDR.inc(sync.dma_start(
                    out=klhs[0:1, :].rearrange("p (h t) -> p h t", h=8),
                    in_=ncgd[h0:h0 + 8, c * 128:(c + 1) * 128].unsqueeze(0)), 16)
                LDR.inc(sync.dma_start(
                    out=krhs[1:2, :].rearrange("p (h t) -> p h t", h=8),
                    in_=cgd[h0:h0 + 8, c * 128:(c + 1) * 128].unsqueeze(0)), 16)
            block.sync(_ld_k)

            def _pe_d(tensor, c=c, h0=h0):
                LDR.wait(tensor)
                CE.wait(tensor)
                for j in range(8):
                    mm = tensor.matmul(psD[:, j * 128:(j + 1) * 128],
                                       klhs[:, j * 128:(j + 1) * 128],
                                       krhs[:, j * 128:(j + 1) * 128],
                                       start=True, stop=True)
                CP.inc(mm)
            block.tensor(_pe_d)

            def _ve_d(vector, c=c, h0=h0):
                CP.wait(vector)
                mb3 = mneg[:].unsqueeze(1).broadcast_to([128, 8, 128])
                CE.inc(vector.tensor_tensor(dq[:].rearrange("p (j t) -> p j t", j=8),
                                            psD[:].rearrange("p (j t) -> p j t", j=8),
                                            mb3, ALU.add))
            block.vector(_ve_d)

            def _sc_d(scalar, c=c, h0=h0):
                CE.wait(scalar)
                CS.inc(scalar.activation(dex[:], dq[:], AF.Exp))
            block.scalar(_sc_d)

            def _ve_mgt(vector, c=c, h0=h0):
                CS.wait(vector)
                gb3 = psG2[:].unsqueeze(1).broadcast_to([128, 8, 128])
                CE.inc(vector.tensor_tensor(mgt[:].rearrange("p (j t) -> p j t", j=8),
                                            dex[:].rearrange("p (j t) -> p j t", j=8),
                                            gb3, ALU.mult))
            block.vector(_ve_mgt)

            def _pe_y(tensor, c=c, h0=h0):
                CE.wait(tensor)
                for j in range(8):
                    h = h0 + j
                    tensor.matmul(psY[:, j * 64:(j + 1) * 64],
                                  mgt[:, j * 128:(j + 1) * 128],
                                  xdt[:, h * 64:(h + 1) * 64], start=True, stop=True)
                for j in range(8):
                    h = h0 + j
                    tensor.matmul(psI[:, j * 64:(j + 1) * 64],
                                  cch[:, c * 128:(c + 1) * 128],
                                  st_b[:, h * 64:(h + 1) * 64], start=True, stop=True)
                for j in range(8):
                    h = h0 + j
                    mm = tensor.matmul(psS2[:, j * 64:(j + 1) * 64], btok[:],
                                       xw[:, h * 64:(h + 1) * 64], start=True, stop=True)
                CP.inc(mm)
            block.tensor(_pe_y)

            def _ve_y(vector, c=c, h0=h0, hg=hg):
                CP.wait(vector)
                if hg == 0:
                    STO.wait(vector)
                eb3 = ecirel[:, h0:h0 + 8].unsqueeze(2).broadcast_to([128, 8, 64])
                CE.inc(vector.tensor_tensor(ty[:].rearrange("p (j d) -> p j d", j=8),
                                            psI[:].rearrange("p (j d) -> p j d", j=8),
                                            eb3, ALU.mult))
                CE.swait(vector)
                CE.inc(vector.tensor_tensor(ty2[:], xt_tok[:, h0 * 64:(h0 + 8) * 64],
                                            dbc[:, h0 * 64:(h0 + 8) * 64], ALU.mult))
                CE.swait(vector)
                CE.inc(vector.tensor_add(ty[:], ty[:], ty2[:]))
                CE.swait(vector)
                CE.inc(vector.tensor_add(yrow[:, h0 * 64:(h0 + 8) * 64], ty[:], psY[:]))
                CE.swait(vector)
                et3 = etb[:, h0:h0 + 8].unsqueeze(2).broadcast_to([128, 8, 64])
                CE.inc(vector.tensor_tensor(
                    st_f[:, h0 * 64:(h0 + 8) * 64].rearrange("p (j d) -> p j d", j=8),
                    st_f[:, h0 * 64:(h0 + 8) * 64].rearrange("p (j d) -> p j d", j=8),
                    et3, ALU.mult))
                CE.swait(vector)
                CE.inc(vector.tensor_add(st_f[:, h0 * 64:(h0 + 8) * 64],
                                         st_f[:, h0 * 64:(h0 + 8) * 64], psS2[:]))
                CE.swait(vector)
                CE.inc(vector.tensor_copy(st_b[:, h0 * 64:(h0 + 8) * 64],
                                          st_f[:, h0 * 64:(h0 + 8) * 64]))
            block.vector(_ve_y)

        def _st_y(sync, c=c):
            CE.wait(sync)
            STO.inc(sync.dma_start(out=yd[c * 128:(c + 1) * 128, :], in_=yrow[:]), 16)
        block.sync(_st_y)

    def _fin(sync):
        STO.wait(sync)
        LDC.wait(sync)
        LDR.wait(sync)
        LDT.wait(sync)
    block.sync(_fin)
    return {"yd": (STO.sem, STO.n)}


VARS["yd"] = ((N, 4096), F32)
CONSTS_SHAPES.update({
    "dt_biasv": ((64, 1), F32), "A_logv": ((64, 1), F32),
    "d_rep": ((1, 4096), F32), "mnegf": ((128, 128), F32),
})


@phase("l1post", ins=["yd", "zt", "W_outT", "norm_wv", "rowsel", "onesr", "eye_bf"],
       outs=["att1h", "ynd"])
def ph_l1post(ctx, deps):
    block = ctx.block
    A = ctx.aps
    yd, zt, att1h, ynd = A["yd"], A["zt"], A["att1h"], A["ynd"]
    DS = 4096

    ix = ctx.sbuf("po_ix", [128, 8], mybir.dt.int32)
    nwrow = ctx.sbuf("po_nwr", [1, DS], F32)
    nwb = ctx.sbuf("po_nwb", [128, DS], F32)
    onesr = ctx.sbuf("po_onesr", [1, 128], F32)
    yrow = ctx.sbuf("po_yrow", [128, DS], F32)
    zrow = ctx.sbuf("po_zrow", [128, DS], F32)
    sq = ctx.sbuf("po_sq", [128, DS], F32)
    stt = ctx.sbuf("po_stt", [128, 4], F32)
    ynb = ctx.sbuf("po_ynb", [128, DS], BF16)
    psB = ctx.psum("po_psB", [128, 512], F32)
    LDI, LDG, CP, CE, CS, STO = (ctx.chain() for _ in range(6))

    for nm in ("yd", "zt"):
        dv = deps.get(nm) if deps else None
        if dv:
            def _w(sync, dv=dv):
                sync.wait_ge(dv[0], dv[1])
            block.sync(_w)

    def _ld0(sync):
        LDI.inc(sync.dma_start(out=ix[:], in_=A["rowsel"][:]), 16)
        LDI.inc(sync.dma_start(out=nwrow[:], in_=A["norm_wv"][:]), 16)
        LDI.inc(sync.dma_start(out=onesr[:], in_=A["onesr"][:]), 16)
    block.sync(_ld0)

    for b8 in range(8):
        def _pe(tensor, b8=b8):
            LDI.wait(tensor)
            CE.wait(tensor)
            CP.inc(tensor.matmul(psB[:], onesr[:], nwrow[0:1, b8 * 512:(b8 + 1) * 512],
                                 start=True, stop=True))
        block.tensor(_pe)

        def _ve(vector, b8=b8):
            CP.wait(vector)
            CE.inc(vector.tensor_copy(nwb[:, b8 * 512:(b8 + 1) * 512], psB[:]))
        block.vector(_ve)

    for g in range(8):
        def _gth(gpsimd, g=g):
            CE.wait(gpsimd)
            CS.wait(gpsimd)
            LDG.inc(gpsimd.indirect_dma_start(
                yrow[:], None, yd[:, :], IndirectOffsetOnAxis(ix[:, g:g + 1], 0)), 16)
            LDG.inc(gpsimd.indirect_dma_start(
                zrow[:], None, zt[:, :], IndirectOffsetOnAxis(ix[:, g:g + 1], 0)), 16)
        block.gpsimd(_gth)

        def _sc(scalar, g=g):
            LDG.wait(scalar)
            CS.inc(scalar.activation(zrow[:], zrow[:], AF.Silu))
        block.scalar(_sc)

        def _ve(vector, g=g):
            CS.wait(vector)
            CE.inc(vector.tensor_tensor(yrow[:], yrow[:], zrow[:], ALU.mult))
        block.vector(_ve)

        def _sc2(scalar, g=g):
            CE.wait(scalar)
            CS.inc(scalar.activation(sq[:], yrow[:], AF.Square, accum_out=stt[:, 0:1]))
        block.scalar(_sc2)

        def _ve2(vector, g=g):
            CS.wait(vector)
            CE.inc(vector.tensor_scalar(stt[:, 1:2], stt[:, 0:1], 1.0 / DS, None,
                                        ALU.mult))
            CE.wait(vector)
            CE.inc(vector.tensor_scalar(stt[:, 1:2], stt[:, 1:2], 1e-5, None, ALU.add))
        block.vector(_ve2)

        def _sc3(scalar, g=g):
            CE.wait(scalar)
            CS.inc(scalar.activation(stt[:, 2:3], stt[:, 1:2], AF.Ln))
            CS.wait(scalar)
            CS.inc(scalar.activation(stt[:, 2:3], stt[:, 2:3], AF.Exp, scale=-0.5))
        block.scalar(_sc3)

        def _ve3(vector, g=g):
            CS.wait(vector)
            if g >= 1:
                STO.wait(vector)
            CE.inc(vector.tensor_scalar(yrow[:], yrow[:], stt[:, 2:3], None, ALU.mult))
            CE.wait(vector)
            CE.inc(vector.tensor_tensor(ynb[:], yrow[:], nwb[:], ALU.mult))
        block.vector(_ve3)

        def _st(sync, g=g):
            CE.wait(sync)
            STO.inc(sync.dma_start(out=ynd[g * 128:(g + 1) * 128, :], in_=ynb[:]), 16)
        block.sync(_st)

    wog = ctx.sbuf("po_wog", [128, 8 * N], BF16)
    ynf = ctx.sbuf("po_ynf", [128, 8 * 128], BF16)
    drow = ctx.sbuf("po_drow", [128, N], F32)
    psW = ctx.psum("po_psW", [128, 512], F32)
    LDW2, LDT2, MST = (ctx.chain() for _ in range(3))

    for cg in range(4):
        def _ldw(gpsimd, cg=cg):
            CP.wait(gpsimd)
            MST.wait(gpsimd)
            for j in range(8):
                LDW2.inc(gpsimd.dma_start(
                    out=wog[:, j * N:(j + 1) * N],
                    in_=A["W_outT"][(cg * 8 + j) * 128:(cg * 8 + j + 1) * 128, :]), 16)
        block.gpsimd(_ldw)

        for g in range(8):
            def _ldt(sync, cg=cg, g=g):
                CP.wait(sync)
                STO.wait(sync)
                for j in range(8):
                    LDT2.inc(sync.dma_start(
                        out=ynf[:, j * 128:(j + 1) * 128],
                        in_=ynd[g * 128:(g + 1) * 128,
                                (cg * 8 + j) * 128:(cg * 8 + j + 1) * 128],
                        transpose=True), 16)
            block.sync(_ldt)

            for d4 in range(4):
                def _pe(tensor, cg=cg, g=g, d4=d4):
                    LDW2.wait(tensor)
                    LDT2.wait(tensor)
                    CE.wait(tensor)
                    for j in range(8):
                        mm = tensor.matmul(psW[:], ynf[:, j * 128:(j + 1) * 128],
                                           wog[:, j * N + d4 * 512: j * N + (d4 + 1) * 512],
                                           start=(j == 0), stop=(j == 7))
                    CP.inc(mm)
                block.tensor(_pe)

                def _ve(vector, cg=cg, g=g, d4=d4):
                    CP.wait(vector)
                    if d4 == 0:
                        MST.wait(vector)
                    CE.inc(vector.tensor_copy(drow[:, d4 * 512:(d4 + 1) * 512], psW[:]))
                block.vector(_ve)

            def _acc(gpsimd, cg=cg, g=g):
                CE.wait(gpsimd)
                MST.wait(gpsimd)
                op = ALU.bypass if cg == 0 else ALU.add
                MST.inc(gpsimd.dma_start(out=att1h[g * 128:(g + 1) * 128, :],
                                         in_=drow[:], accum_op=op), 16)
            block.gpsimd(_acc)

    def _fin(sync):
        STO.wait(sync)
        LDI.wait(sync)
        LDG.wait(sync)
        LDW2.wait(sync)
        LDT2.wait(sync)
        MST.wait(sync)
    block.sync(_fin)
    return {"att1h": (MST.sem, MST.n), "ynd": (STO.sem, STO.n)}


VARS["att1h"] = ((1024, D), F32)
VARS["ynd"] = ((1024, 4096), BF16)
CONSTS_SHAPES.update({
    "W_outT": ((4096, D), F32), "norm_wv": ((1, 4096), F32),
    "rowsel": ((128, 8), mybir.dt.int32),
})


HH1 = 8192


@phase("l1quant", ins=["w_gate_up", "w_proj"], outs=["whgud", "whpd"])
def ph_l1quant(ctx, deps):
    block = ctx.block
    A = ctx.aps

    wqf = ctx.sbuf("fq_wqf", [128, 2 * HH1], F32)
    wvb = [ctx.sbuf("fq_wvb", [128, HH1], BF16) for _ in range(2)]
    absw_b = [ctx.sbuf("fq_absw", [128, HH1], BF16) for _ in range(2)]
    sgn_b = [ctx.sbuf("fq_sgn", [128, HH1], BF16) for _ in range(2)]
    gs = ctx.sbuf("fq_gs", [128, 128], F32)
    gsb = ctx.sbuf("fq_gsb", [128, 128], BF16)
    hgs = ctx.sbuf("fq_hgs", [128, 128], F32)
    what = [ctx.sbuf("fq_what", [128, HH1], BF16) for _ in range(2)]
    LDQ0, LDQ1, CE, CS, STO = (ctx.chain() for _ in range(5))
    SC_DONE, DVE_DONE = {}, {}

    jobs = []
    for rc in range(128):
        jobs.append((A["w_gate_up"], A["whgud"], rc, D))
    for rc in range(16):
        jobs.append((A["w_proj"], A["whpd"], rc, HH1))

    for j, (wsrc, wdst, rc, ncols) in enumerate(jobs):
        LQ = LDQ0 if j % 2 == 0 else LDQ1
        s = j % 2
        ng = ncols // 64
        wv = wqf[:, s * HH1:s * HH1 + ncols]
        absw = absw_b[s]
        sgn = sgn_b[s]
        dste = absw

        def _ldq(sync, j=j, wsrc=wsrc, rc=rc, ncols=ncols, LQ=LQ, s=s):
            if j >= 2:
                sync.wait_ge(CS.sem, SC_DONE.get(j - 2, 0))
            LQ.inc(sync.dma_start(out=wqf[:, s * HH1:s * HH1 + ncols],
                                  in_=wsrc[rc * 128:(rc + 1) * 128, :]), 16)
        block.sync(_ldq)

        def _sc(scalar, j=j, wv=wv, ncols=ncols, LQ=LQ, s=s, absw=absw, sgn=sgn):
            scalar.wait_ge(LQ.sem, 16 * (j // 2 + 1))
            if j >= 2:
                scalar.wait_ge(CE.sem, DVE_DONE.get(j - 2, 0))
            CS.inc(scalar.activation(wvb[s][:, 0:ncols], wv, AF.Copy))
            CS.wait(scalar)
            CS.inc(scalar.activation(absw[:, 0:ncols], wv, AF.Abs))
            CS.wait(scalar)
            CS.inc(scalar.activation(sgn[:, 0:ncols], wv, AF.Sign))
            SC_DONE[j] = CS.n
        block.scalar(_sc)
        sc_done = CS.n

        def _ve(vector, j=j, ncols=ncols, ng=ng, s=s, sc_done=sc_done,
                absw=absw, sgn=sgn, dste=dste):
            vector.wait_ge(CS.sem, sc_done)
            CE.inc(vector.tensor_reduce(
                gs[:, 0:ng], absw[:, 0:ncols].rearrange("p (g k) -> p g k", k=64),
                AX.X, ALU.add))
            CE.swait(vector)
            CE.inc(vector.tensor_scalar(gs[:, 0:ng], gs[:, 0:ng], 1.0 / 64, None,
                                        ALU.mult))
            CE.swait(vector)
            CE.inc(vector.tensor_scalar(gs[:, 0:ng], gs[:, 0:ng], 1e-8, None, ALU.max))
            CE.swait(vector)
            CE.inc(vector.tensor_copy(gsb[:, 0:ng], gs[:, 0:ng]))
            CE.swait(vector)
            CE.inc(vector.tensor_scalar(hgs[:, 0:ng], gsb[:, 0:ng], 0.501953125,
                                        None, ALU.mult))
            CE.swait(vector)
            CE.inc(vector.tensor_tensor(
                absw[:, 0:ncols].rearrange("p (g k) -> p g k", k=64),
                absw[:, 0:ncols].rearrange("p (g k) -> p g k", k=64),
                hgs[:, 0:ng].unsqueeze(2).broadcast_to([128, ng, 64]), ALU.is_gt))
            CE.swait(vector)
            CE.inc(vector.tensor_tensor(
                absw[:, 0:ncols].rearrange("p (g k) -> p g k", k=64),
                absw[:, 0:ncols].rearrange("p (g k) -> p g k", k=64),
                gsb[:, 0:ng].unsqueeze(2).broadcast_to([128, ng, 64]), ALU.mult))
            CE.swait(vector)
            if j >= 2:
                vector.wait_ge(STO.sem, 32 * (j // 2))
            CE.inc(vector.tensor_tensor(what[s][:, 0:ncols], sgn[:, 0:ncols],
                                        absw[:, 0:ncols], ALU.mult))
            CE.swait(vector)
            CE.inc(vector.tensor_tensor(dste[:, 0:ncols], what[s][:, 0:ncols],
                                        wvb[s][:, 0:ncols], ALU.subtract))
            CE.swait(vector)
            CE.inc(vector.tensor_add(what[s][:, 0:ncols], wvb[s][:, 0:ncols],
                                     dste[:, 0:ncols]))
            DVE_DONE[j] = CE.n
        block.vector(_ve)
        dv_done = CE.n

        def _st(sync, j=j, wdst=wdst, rc=rc, ncols=ncols, s=s, dv_done=dv_done):
            sync.wait_ge(CE.sem, dv_done)
            STO.inc(sync.dma_start(out=wdst[rc * 128:(rc + 1) * 128, :],
                                   in_=what[s][:, 0:ncols]), 16)
        block.sync(_st)

    def _fin(sync):
        STO.wait(sync)
        LDQ0.wait(sync)
        LDQ1.wait(sync)
    block.sync(_fin)
    done = (STO.sem, STO.n)
    return {"whgud": done, "whpd": done}


VARS["whgud"] = ((16384, D), BF16)
VARS["whpd"] = ((D, HH1), BF16)


@phase("l1gu", ins=["xn1t", "whgud", "rowsel", "onesc"], outs=["hfd", "rsd"])
def ph_l1gu(ctx, deps):
    nc, block = ctx.nc, ctx.block
    A = ctx.aps
    whgud, hfd, rsd = A["whgud"], A["hfd"], A["rsd"]
    HT = 1024
    xn1hd = nc.dram_tensor("xn1hd", [HT, D], BF16, kind="Internal").ap()
    gufd = nc.dram_tensor("gufd", [16384, HT], BF16, kind="Internal").ap()

    ix = ctx.sbuf("fg_ix", [128, 8], mybir.dt.int32)
    onesc = ctx.sbuf("fg_onesc", [128, 1], F32)
    xrow = ctx.sbuf("fg_xrow", [128, D], BF16)
    xnf1 = ctx.sbuf("fg_xnf1", [128, ND * HT], BF16)
    wtall = ctx.sbuf("fg_wtall", [128, ND * 2048], BF16)
    gur = ctx.sbuf("fg_gur", [128, HT], BF16)
    psA = ctx.psum("fg_psA", [128, 1024], F32)
    LDI, LDQ, STOG, LDT2, LDW2, CP, CE, CS, STO = (ctx.chain() for _ in range(9))
    for nm in ("xn1t", "whgud"):
        dv = deps.get(nm) if deps else None
        if dv:
            def _w(sync, dv=dv):
                sync.wait_ge(dv[0], dv[1])
            block.sync(_w)

    def _ld0(sync):
        LDI.inc(sync.dma_start(out=ix[:], in_=A["rowsel"][:]), 16)
        LDI.inc(sync.dma_start(out=onesc[:], in_=A["onesc"][:]), 16)
    block.sync(_ld0)

    for g in range(8):
        def _gth(gpsimd, g=g):
            LDI.wait(gpsimd)
            LDQ.inc(gpsimd.indirect_dma_start(
                xrow[:], None, A["xn1t"][:, :], IndirectOffsetOnAxis(ix[:, g:g + 1], 0)), 16)
            LDQ.wait(gpsimd)
            STOG.inc(gpsimd.dma_start(out=xn1hd[g * 128:(g + 1) * 128, :], in_=xrow[:]), 16)
            STOG.wait(gpsimd)
        block.gpsimd(_gth)

    def _ldxn(sync):
        STOG.wait(sync)
        for c in range(ND):
            LDT2.inc(sync.dma_start(out=xnf1[:, c * HT:(c + 1) * HT],
                                    in_=xn1hd[:, c * 128:(c + 1) * 128], transpose=True), 16)
    block.sync(_ldxn)

    for hb in range(8):
        def _ldw(sync, hb=hb):
            CP.wait(sync)
            for c in range(ND):
                LDW2.inc(sync.dma_start(
                    out=wtall[:, c * 2048:(c + 1) * 2048],
                    in_=whgud[hb * 2048:(hb + 1) * 2048, c * 128:(c + 1) * 128],
                    transpose=True), 16)
        block.sync(_ldw)

        for hc in range(16):
            for t4 in range(2):
                def _pe(tensor, hb=hb, hc=hc, t4=t4):
                    LDW2.wait(tensor)
                    LDT2.wait(tensor)
                    CE.wait(tensor)
                    sl = (t4 % 2) * 512
                    for c in range(ND):
                        mm = tensor.matmul(
                            psA[:, sl:sl + 512],
                            wtall[:, c * 2048 + hc * 128: c * 2048 + (hc + 1) * 128],
                            xnf1[:, c * HT + t4 * 512: c * HT + (t4 + 1) * 512],
                            start=(c == 0), stop=(c == ND - 1))
                    CP.inc(mm)
                block.tensor(_pe)

                def _ve(vector, hb=hb, hc=hc, t4=t4):
                    CP.wait(vector)
                    if t4 == 0:
                        STO.wait(vector)
                    CE.inc(vector.tensor_copy(gur[:, t4 * 512:(t4 + 1) * 512],
                                              psA[:, (t4 % 2) * 512:(t4 % 2) * 512 + 512]))
                block.vector(_ve)

            def _st(sync, hb=hb, hc=hc):
                CE.wait(sync)
                STO.inc(sync.dma_start(
                    out=gufd[(hb * 16 + hc) * 128:(hb * 16 + hc + 1) * 128, :],
                    in_=gur[:]), 16)
            block.sync(_st)

    gate_s = ctx.sbuf("fg_gate", [128, HT], BF16)
    up_s = ctx.sbuf("fg_up", [128, HT], BF16)
    sgf = ctx.sbuf("fg_sgf", [128, HT], F32)
    hf32 = ctx.sbuf("fg_hf32", [128, HT], F32)
    hbf = ctx.sbuf("fg_hbf", [128, HT], BF16)
    sqh = ctx.sbuf("fg_sqh", [128, HT], F32)
    ssr = ctx.sbuf("fg_ssr", [1, HT], F32)
    psR = ctx.psum("fg_psR", [1, 1024], F32)
    LDH = ctx.chain()

    for hc in range(64):
        def _ldh(sync, hc=hc):
            CS.wait(sync)
            CE.wait(sync)
            CP.wait(sync)
            LDH.inc(sync.dma_start(out=gate_s[:], in_=gufd[hc * 128:(hc + 1) * 128, :]), 16)
            LDH.inc(sync.dma_start(out=up_s[:],
                                   in_=gufd[HH1 + hc * 128: HH1 + (hc + 1) * 128, :]), 16)
        block.sync(_ldh)

        def _sc(scalar, hc=hc):
            LDH.wait(scalar)
            CS.inc(scalar.activation(sgf[:], gate_s[:], AF.Silu))
        block.scalar(_sc)

        def _ve(vector, hc=hc):
            CS.wait(vector)
            CE.inc(vector.tensor_tensor(hf32[:], sgf[:], up_s[:], ALU.mult))
            CE.swait(vector)
            STO.wait(vector)
            CE.inc(vector.tensor_copy(hbf[:], hf32[:]))
        block.vector(_ve)

        def _sc2(scalar, hc=hc):
            CE.wait(scalar)
            CS.inc(scalar.activation(sqh[:], hf32[:], AF.Square))
        block.scalar(_sc2)

        def _pe2(tensor, hc=hc):
            CS.wait(tensor)
            if hc == 0:
                CE.wait(tensor)
            tensor.matmul(psR[:, 0:512], onesc[:], sqh[:, 0:512],
                          start=(hc == 0), stop=(hc == 63))
            mm = tensor.matmul(psR[:, 512:1024], onesc[:], sqh[:, 512:1024],
                               start=(hc == 0), stop=(hc == 63))
            CP.inc(mm)
        block.tensor(_pe2)

        def _st(sync, hc=hc):
            CE.wait(sync)
            STO.inc(sync.dma_start(out=hfd[hc * 128:(hc + 1) * 128, :], in_=hbf[:]), 16)
        block.sync(_st)

    def _ver(vector):
        CP.wait(vector)
        CE.inc(vector.tensor_copy(ssr[:, 0:1024], psR[:, 0:1024]))
        CE.wait(vector)
        CE.inc(vector.tensor_scalar(ssr[:, 0:1024], ssr[:, 0:1024], 1.0 / HH1, None,
                                    ALU.mult))
        CE.wait(vector)
        CE.inc(vector.tensor_scalar(ssr[:, 0:1024], ssr[:, 0:1024],
                                    1.1920928955078125e-07, None, ALU.add))
    block.vector(_ver)

    def _scr(scalar):
        CE.wait(scalar)
        CS.inc(scalar.activation(ssr[:, 0:1024], ssr[:, 0:1024], AF.Ln))
        CS.wait(scalar)
        CS.inc(scalar.activation(ssr[:, 0:1024], ssr[:, 0:1024], AF.Exp, scale=-0.5))
    block.scalar(_scr)

    def _str(sync):
        CS.wait(sync)
        STO.inc(sync.dma_start(out=A["rsd"][:].flatten().unsqueeze(0), in_=ssr[:, 0:1024]), 16)
    block.sync(_str)

    def _fin(sync):
        STO.wait(sync)
        LDI.wait(sync)
        LDQ.wait(sync)
        STOG.wait(sync)
        LDT2.wait(sync)
        LDW2.wait(sync)
        LDH.wait(sync)
    block.sync(_fin)
    done = (STO.sem, STO.n)
    return {"hfd": done, "rsd": done}


VARS["hfd"] = ((HH1, 1024), BF16)
VARS["rsd"] = ((8, 128), F32)


@phase("l1proj2", ins=["hfd", "whpd", "rsd", "x1b", "att1h", "rowsel", "eye_f32"],
       outs=["outh"])
def ph_l1proj2(ctx, deps):
    block = ctx.block
    A = ctx.aps
    hfd, whpd, x1b, att1h, outh = A["hfd"], A["whpd"], A["x1b"], A["att1h"], A["outh"]
    HT = 1024

    ix = ctx.sbuf("fp_ix", [128, 8], mybir.dt.int32)
    eyef = ctx.sbuf("fp_eyef", [128, 128], F32)
    rr8 = ctx.sbuf("fp_rr8", [8, 128], F32)
    rstd_tok = ctx.sbuf("fp_rstok", [128, 8], F32)
    wptg = ctx.sbuf("fp_wptg", [128, 8 * 2048], BF16)
    hfb = ctx.sbuf("fp_hfb", [128, 8 * 128], BF16)
    pacc8 = ctx.sbuf("fp_pacc8", [128, 8 * 2048], F32)
    x1br = ctx.sbuf("fp_x1br", [128, D], F32)
    a1r = ctx.sbuf("fp_a1r", [128, D], F32)
    orow = ctx.sbuf("fp_orow", [128, D], F32)
    psQ = ctx.psum("fp_psQ", [128, 1024], F32)
    psT8 = ctx.psum("fp_psT8", [128, 8], F32)
    LDI, LDG, LDW, LDH, LDF, CP, CE, STO = (ctx.chain() for _ in range(8))

    for nm in ("hfd", "whpd", "rsd", "att1h"):
        dv = deps.get(nm) if deps else None
        if dv:
            def _w(sync, dv=dv):
                sync.wait_ge(dv[0], dv[1])
            block.sync(_w)

    def _ld0(sync):
        LDI.inc(sync.dma_start(out=ix[:], in_=A["rowsel"][:]), 16)
        LDI.inc(sync.dma_start(out=eyef[:], in_=A["eye_f32"][:]), 16)
        LDI.inc(sync.dma_start(out=rr8[:], in_=A["rsd"][:]), 16)
    block.sync(_ld0)

    def _per(tensor):
        LDI.wait(tensor)
        CP.inc(tensor.transpose(psT8[:], rr8[:], eyef[0:8, 0:8]))
    block.tensor(_per)

    def _ver2(vector):
        CP.wait(vector)
        CE.inc(vector.tensor_copy(rstd_tok[:], psT8[:]))
    block.vector(_ver2)

    for hg in range(8):
        def _ldw(sync, hg=hg):
            CP.wait(sync)
            for j in range(8):
                LDW.inc(sync.dma_start(
                    out=wptg[:, j * 2048:(j + 1) * 2048],
                    in_=whpd[:, (hg * 8 + j) * 128:(hg * 8 + j + 1) * 128],
                    transpose=True), 16)
        block.sync(_ldw)

        for tt in range(8):
            def _ldh(sync, hg=hg, tt=tt):
                CP.wait(sync)
                LDH.inc(sync.dma_start(
                    out=hfb[:].rearrange("p (c t) -> p c t", c=8),
                    in_=hfd[hg * 1024:(hg + 1) * 1024, tt * 128:(tt + 1) * 128]
                    .rearrange("(c p) t -> p c t", p=128)), 16)
            block.sync(_ldh)

            for d2 in range(2):
                def _pe(tensor, hg=hg, tt=tt, d2=d2):
                    LDW.wait(tensor)
                    LDH.wait(tensor)
                    CE.wait(tensor)
                    sl = d2 * 1024
                    for j in range(8):
                        tensor.matmul(psQ[:, 0:512],
                                      hfb[:, j * 128:(j + 1) * 128],
                                      wptg[:, j * 2048 + sl: j * 2048 + sl + 512],
                                      start=(j == 0), stop=(j == 7))
                    for j in range(8):
                        mm = tensor.matmul(psQ[:, 512:1024],
                                           hfb[:, j * 128:(j + 1) * 128],
                                           wptg[:, j * 2048 + sl + 512: j * 2048 + sl + 1024],
                                           start=(j == 0), stop=(j == 7))
                    CP.inc(mm)
                block.tensor(_pe)

                def _ve(vector, hg=hg, tt=tt, d2=d2):
                    CP.wait(vector)
                    o = tt * 2048 + d2 * 1024
                    if hg == 0:
                        CE.inc(vector.tensor_copy(pacc8[:, o:o + 1024], psQ[:, 0:1024]))
                    else:
                        CE.inc(vector.tensor_add(pacc8[:, o:o + 1024],
                                                 pacc8[:, o:o + 1024], psQ[:, 0:1024]))
                block.vector(_ve)

    for tt in range(8):
        def _gth(gpsimd, tt=tt):
            LDI.wait(gpsimd)
            CE.wait(gpsimd)
            LDG.inc(gpsimd.indirect_dma_start(
                x1br[:], None, x1b[:, :], IndirectOffsetOnAxis(ix[:, tt:tt + 1], 0)), 16)
        block.gpsimd(_gth)

        def _ldf(sync, tt=tt):
            CE.wait(sync)
            STO.wait(sync)
            LDF.inc(sync.dma_start(out=a1r[:], in_=att1h[tt * 128:(tt + 1) * 128, :]), 16)
        block.sync(_ldf)

        def _vef(vector, tt=tt):
            LDG.wait(vector)
            LDF.wait(vector)
            CE.inc(vector.tensor_scalar(orow[:], pacc8[:, tt * 2048:(tt + 1) * 2048],
                                        rstd_tok[:, tt:tt + 1], None, ALU.mult))
            CE.wait(vector)
            CE.inc(vector.tensor_add(orow[:], orow[:], a1r[:]))
            CE.wait(vector)
            CE.inc(vector.tensor_add(orow[:], orow[:], x1br[:]))
        block.vector(_vef)

        def _stf(sync, tt=tt):
            CE.wait(sync)
            STO.inc(sync.dma_start(out=outh[tt * 128:(tt + 1) * 128, :], in_=orow[:]), 16)
        block.sync(_stf)

    def _fin(sync):
        STO.wait(sync)
        LDI.wait(sync)
        LDG.wait(sync)
        LDW.wait(sync)
        LDH.wait(sync)
        LDF.wait(sync)
    block.sync(_fin)
    return {"outh": (STO.sem, STO.n)}


VARS["outh"] = ((1024, D), F32)
CONSTS_SHAPES.update({
    "w_gate_up": ((16384, D), F32), "w_proj": ((D, HH if False else 8192), F32),
})


LAUNCH_GROUPS = [
    ["ln0", "l0qkv", "l0v", "l0reg", "l0attn", "l0moe", "ln1",
     "l1proj", "l1conv", "l1scan", "l1post", "l1quant", "l1gu", "l1proj2"],
]


def build_group(group, ext_out=None):
    nc = bass.Bass("TRN2", target_bir_lowering=False)
    need_in, produced = set(), set()
    for pname in group:
        fn, ins, outs = PHASES[pname]
        for n in ins:
            if n not in produced:
                need_in.add(n)
        produced.update(outs)
    if ext_out is None:
        ext_out = produced
    aps = {}
    for n in sorted(need_in):
        shape, dt = CONSTS_SHAPES[n] if n in CONSTS_SHAPES else VARS[n]
        aps[n] = nc.dram_tensor(n, list(shape), dt, kind="ExternalInput").ap()
    for n in sorted(produced):
        shape, dt = VARS[n]
        kind = "ExternalOutput" if n in ext_out else "Internal"
        aps[n] = nc.dram_tensor(n, list(shape), dt, kind=kind).ap()
    with contextlib.ExitStack() as gstack:
        with nc.Block() as block:
            ctx = Ctx(nc, block, aps, gstack)
            ctx.phd = ctx.chain()
            deps = {}
            for pi, pname in enumerate(group):
                fn, ins, outs = PHASES[pname]
                ctx.phase_idx = pi
                ctx.phase_chains = []
                with contextlib.ExitStack() as pstack:
                    ctx.stack = pstack
                    newdeps = fn(ctx, deps)
                if newdeps:
                    deps.update(newdeps)
                chains = ctx.phase_chains

                def _pb(sync, chains=chains):
                    for ch in chains:
                        ch.wait(sync)
                    ctx.phd.inc(sync.nop())
                ctx.block.sync(_pb)
    return nc, sorted(need_in), sorted(n for n in produced if n in ext_out)


def prepare_consts():
    import ml_dtypes
    c = {}
    c["eye_bf"] = np.eye(128, dtype=ml_dtypes.bfloat16)
    c["eye_f32"] = np.eye(128, dtype=np.float32)
    c["onesc"] = np.ones((128, 1), dtype=np.float32)
    c["onesr"] = np.ones((1, 128), dtype=np.float32)
    inv = (10000.0 ** (-np.arange(0, 128, 2, dtype=np.float64) / 128.0))
    th = np.arange(N, dtype=np.float64)[None, :] * inv[:, None]
    cos = np.cos(th).astype(np.float32)
    sin = np.sin(th).astype(np.float32)
    s = np.float32(128.0 ** -0.5)
    c["ccq"] = np.concatenate([cos * s, cos * s], 0).astype(ml_dtypes.bfloat16)
    c["ssq"] = np.concatenate([-sin * s, sin * s], 0).astype(ml_dtypes.bfloat16)
    c["cck"] = np.concatenate([cos, cos], 0).astype(ml_dtypes.bfloat16)
    c["ssk"] = np.concatenate([-sin, sin], 0).astype(ml_dtypes.bfloat16)
    c["mtriu"] = (-np.triu(np.ones((64, 64), dtype=np.float32), 1))
    cm = np.where(np.tril(np.ones((128, 128), bool)), 0.0, -1e30).astype(np.float32)
    c["cmaskb"] = cm.astype(ml_dtypes.bfloat16)
    jj, ii = np.meshgrid(np.arange(128), np.arange(128), indexing="ij")
    c["mnegf"] = np.where(jj <= ii, 0.0, -1e30).astype(np.float32)
    return c


NCORES = 8


def kernel(**inputs):
    consts = prepare_consts()
    asc = np.ascontiguousarray
    shared = {
        "Wq": inputs["L0_attn_Wq"], "Wk": inputs["L0_attn_Wk"],
        "Wv": inputs["L0_attn_Wv"], "Wb": inputs["L0_attn_Wb"],
        "Wo": inputs["L0_attn_Wo"],
        "gwt": asc(inputs["L0_ffn_gate_w"].T),
        "sgw": asc(inputs["L0_ffn_shared_gate_w"].reshape(16, 128).T),
        "w_gate": inputs["L0_ffn_w_gate"], "w_up": inputs["L0_ffn_w_up"],
        "w_down": inputs["L0_ffn_w_down"],
        "sw_gate": inputs["L0_ffn_sw_gate"], "sw_up": inputs["L0_ffn_sw_up"],
        "sw_down": inputs["L0_ffn_sw_down"],
        "W_inT": asc(inputs["L1_attn_W_in"].T),
        "convw": asc(inputs["L1_attn_conv_w"].reshape(4352, 4)),
        "convb": asc(inputs["L1_attn_conv_b"].reshape(34, 128).T),
        "dt_biasv": asc(inputs["L1_attn_dt_bias"].reshape(64, 1)),
        "A_logv": asc(inputs["L1_attn_A_log"].reshape(64, 1)),
        "d_rep": asc(np.repeat(inputs["L1_attn_D"], 64).reshape(1, 4096)),
        "norm_wv": asc(inputs["L1_attn_norm_w"].reshape(1, 4096)),
        "W_outT": asc(inputs["L1_attn_W_out"].T),
        "w_gate_up": inputs["L1_ffn_w_gate_up"], "w_proj": inputs["L1_ffn_w_proj"],
    }
    shared.update(consts)
    x0 = inputs["L0_attn_x"]
    percore = []
    for c in range(NCORES):
        p = c % 2
        rs = (p * 1024 + np.arange(1024)).astype(np.int32).reshape(8, 128).T.copy()
        percore.append({"x0": asc(x0[c // 2]), "rowsel": rs})

    state = [dict() for _ in range(NCORES)]

    def get(c, n):
        if n in percore[c]:
            return percore[c][n]
        if n in shared:
            return shared[n]
        return state[c][n]

    produced_by = {}
    for gi, group in enumerate(LAUNCH_GROUPS):
        for pname in group:
            for n in PHASES[pname][2]:
                produced_by[n] = gi
    for gi, group in enumerate(LAUNCH_GROUPS):
        ext = {"outh"}
        for gj in range(gi + 1, len(LAUNCH_GROUPS)):
            for pname in LAUNCH_GROUPS[gj]:
                for n in PHASES[pname][1]:
                    if produced_by.get(n, 99) <= gi:
                        ext.add(n)
        nc, in_names, out_names = build_group(group, ext_out=ext)
        in_maps = [{n: get(c, n) for n in in_names} for c in range(NCORES)]
        res = run_bass_kernel_spmd(nc, in_maps, core_ids=list(range(NCORES)))
        for c in range(NCORES):
            for n in out_names:
                state[c][n] = np.asarray(res.results[c][n])

    out = np.empty((B, N, D), dtype=np.float32)
    for b in range(B):
        out[b, 0:1024] = state[2 * b]["outh"]
        out[b, 1024:2048] = state[2 * b + 1]["outh"]
    return out


# version 12
# speedup vs baseline: 1.4712x; 1.0265x over previous
import contextlib
import numpy as np
import concourse.bass as bass
import concourse.mybir as mybir
from concourse.bass_utils import run_bass_kernel_spmd
from concourse.bass_core_rust import IndirectOffsetOnAxis

F32 = mybir.dt.float32
BF16 = mybir.dt.bfloat16
AF = mybir.ActivationFunctionType
ALU = mybir.AluOpType
AX = mybir.AxisListType

B, N, D = 4, 2048, 2048
NT = N // 128
ND = D // 128

VARS = {
    "xnt": ((N, D), BF16),
}

CONSTS_SHAPES = {
    "x0": ((N, D), F32),
    "eye_bf": ((128, 128), BF16),
}

PHASES = {}


def phase(name, ins, outs):
    def deco(fn):
        PHASES[name] = (fn, ins, outs)
        return fn
    return deco


SELF_EDGES = True


class Chain:
    def __init__(self, sem):
        self.sem = sem
        self.n = 0

    def swait(self, eng, v=None):
        if SELF_EDGES:
            eng.wait_ge(self.sem, self.n if v is None else v)

    def inc(self, instr, k=1):
        self.n += k
        instr.then_inc(self.sem, k)
        return self.n

    def wait(self, eng, v=None):
        eng.wait_ge(self.sem, self.n if v is None else v)


class _BlockProxy:
    def __init__(self, block, ctx):
        self._block = block
        self._ctx = ctx

    def __getattr__(self, name):
        sec = getattr(self._block, name)
        ctx = self._ctx

        def wrapped(fn):
            def inner(eng):
                if ctx.phase_idx > 0:
                    eng.wait_ge(ctx.phd.sem, ctx.phase_idx)
                return fn(eng)
            return sec(inner)
        return wrapped


class Ctx:
    def __init__(self, nc, block, aps, gstack):
        self.nc = nc
        self.rawblock = block
        self.block = _BlockProxy(block, self)
        self.aps = aps
        self.gstack = gstack
        self.stack = None
        self.sem_n = 0
        self.buf_n = 0
        self.phase_idx = 0
        self.phd = None
        self.phase_chains = []

    def chain(self):
        self.sem_n += 1
        ch = Chain(self.gstack.enter_context(self.nc.semaphore(f"s{self.sem_n}")))
        self.phase_chains.append(ch)
        return ch

    def sbuf(self, name, shape, dtype):
        self.buf_n += 1
        return self.stack.enter_context(
            self.nc.sbuf_tensor(f"{name}{self.buf_n}", list(shape), dtype))

    def psum(self, name, shape, dtype):
        self.buf_n += 1
        return self.stack.enter_context(
            self.nc.psum_tensor(f"{name}{self.buf_n}", list(shape), dtype))


def make_ln(name, src_name, dst_name):
    @phase(name, ins=[src_name], outs=[dst_name])
    def ph_ln(ctx, deps, src_name=src_name, dst_name=dst_name):
        return _ln_body(ctx, deps, src_name, dst_name)
    return ph_ln


def _ln_body(ctx, deps, src_name, dst_name):
    block = ctx.block
    x0, xnt = ctx.aps[src_name], ctx.aps[dst_name]
    dv = deps.get(src_name) if deps else None
    if dv:
        def _w(sync):
            sync.wait_ge(dv[0], dv[1])
        block.sync(_w)
    xt = ctx.sbuf("ln_xt", [128, D], F32)
    xnb = ctx.sbuf("ln_xnb", [128, D], BF16)
    bns = ctx.sbuf("ln_bns", [128, 24], F32)
    agg = ctx.sbuf("ln_agg", [128, 4], F32)

    LD, ST, CV, CS = ctx.chain(), ctx.chain(), ctx.chain(), ctx.chain()
    ld, st, cv, cs = LD.sem, ST.sem, CV.sem, CS.sem

    def _sync(sync):
        for i in range(NT):
            if i >= 1:
                sync.wait_ge(cv, 7 * i)
            sync.dma_start(out=xt[:], in_=x0[i * 128:(i + 1) * 128, :]).then_inc(ld, 16)
            sync.wait_ge(cv, 7 * i + 7)
            sync.dma_start(out=xnt[i * 128:(i + 1) * 128, :], in_=xnb[:]).then_inc(st, 16)
        sync.wait_ge(st, 16 * NT)

    def _vector(vector):
        for i in range(NT):
            vector.wait_ge(ld, 16 * (i + 1))
            for j in range(4):
                vector.bn_stats(bns[:, 6 * j:6 * (j + 1)],
                                xt[:, 512 * j:512 * (j + 1)]).then_inc(cv, 1)
            vector.wait_ge(cv, 7 * i + 4)
            vector.bn_aggr(agg[:, 0:2], bns[:]).then_inc(cv, 1)
            vector.wait_ge(cv, 7 * i + 5)
            vector.tensor_scalar(agg[:, 2:3], agg[:, 1:2], 1e-5, None, ALU.add) \
                .then_inc(cv, 1)
            vector.wait_ge(cs, 2 * i + 2)
            if i >= 1:
                vector.wait_ge(st, 16 * i)
            vector.tensor_scalar(xnb[:], xt[:], agg[:, 0:1], agg[:, 3:4],
                                 ALU.subtract, ALU.mult).then_inc(cv, 1)

    def _scalar(scalar):
        for i in range(NT):
            scalar.wait_ge(cv, 7 * i + 6)
            scalar.activation(agg[:, 3:4], agg[:, 2:3], AF.Ln).then_inc(cs, 1)
            scalar.wait_ge(cs, 2 * i + 1)
            scalar.activation(agg[:, 3:4], agg[:, 3:4], AF.Exp, scale=-0.5) \
                .then_inc(cs, 1)

    block.sync(_sync)
    block.vector(_vector)
    block.scalar(_scalar)
    ST.n = 16 * NT
    return {dst_name: (st, 16 * NT)}


make_ln("ln0", "x0", "xnt")
make_ln("ln1", "x1b", "xn1t")
VARS["xn1t"] = ((N, D), BF16)


@phase("l0qkv", ins=["xnt", "Wq", "Wk", "Wb", "ccq", "ssq", "cck", "ssk",
                     "onesc", "onesr", "eye_f32"],
       outs=["qf", "kf", "ksf", "betad", "betatd"])
def ph_l0qkv(ctx, deps):
    block = ctx.block
    A = ctx.aps
    xnt, Wq, Wk, Wb = A["xnt"], A["Wq"], A["Wk"], A["Wb"]
    qf, kf, ksf, betad, betatd = (A[n] for n in
                                  ["qf", "kf", "ksf", "betad", "betatd"])

    xnf = ctx.sbuf("qk_xnf", [128, ND * N], BF16)
    wq = ctx.sbuf("qk_wq", [128, ND * 128], BF16)
    wk = ctx.sbuf("qk_wk", [128, ND * 128], BF16)
    wb = ctx.sbuf("qk_wb", [128, ND * 16], BF16)
    ccq = ctx.sbuf("qk_ccq", [128, N], BF16)
    ssq = ctx.sbuf("qk_ssq", [128, N], BF16)
    cck = ctx.sbuf("qk_cck", [128, N], BF16)
    ssk = ctx.sbuf("qk_ssk", [128, N], BF16)
    onesc = ctx.sbuf("qk_onesc", [128, 1], F32)
    onesr = ctx.sbuf("qk_onesr", [1, 128], F32)
    eyef = ctx.sbuf("qk_eyef", [128, 128], F32)
    betaf = ctx.sbuf("qk_betaf", [16, N], F32)
    betat = ctx.sbuf("qk_betat", [128, 16], F32)
    brow = ctx.sbuf("qk_brow", [1, N], F32)
    qa = ctx.sbuf("qk_qa", [128, N], F32)
    qsw = ctx.sbuf("qk_qsw", [128, N], F32)
    qout = ctx.sbuf("qk_qout", [128, N], BF16)
    kall = ctx.sbuf("qk_kall", [128, N], F32)
    ksq = ctx.sbuf("qk_ksq", [128, N], F32)
    s_row = ctx.sbuf("qk_srow", [1, N], F32)
    knr = ctx.sbuf("qk_knr", [128, N], F32)
    ksw, kt1 = ksq, kall
    kfout = ctx.sbuf("qk_kfout", [128, N], BF16)
    ksfout = ctx.sbuf("qk_ksfout", [128, N], BF16)
    psq = ctx.psum("qk_psq", [128, 512], F32)
    ps1 = ctx.psum("qk_ps1", [1, 512], F32)
    psb = ctx.psum("qk_psb", [128, 512], F32)
    psbt = ctx.psum("qk_psbt", [16, 512], F32)
    pstr = ctx.psum("qk_pstr", [128, 16], F32)

    LDX = ctx.chain()
    LDW = ctx.chain()
    CP = ctx.chain()
    CE = ctx.chain()
    CS = ctx.chain()
    CG = ctx.chain()
    STO = ctx.chain()

    dep_xnt = deps.get("xnt") if deps else None

    def _s0_sync(sync):
        if dep_xnt:
            sync.wait_ge(dep_xnt[0], dep_xnt[1])
        for c in range(ND):
            LDX.inc(sync.dma_start(out=xnf[:, c * N:(c + 1) * N],
                                   in_=xnt[:, c * 128:(c + 1) * 128], transpose=True), 16)
        for name, buf in (("ccq", ccq), ("ssq", ssq), ("cck", cck), ("ssk", ssk)):
            LDX.inc(sync.dma_start(out=buf[:], in_=A[name][:]), 16)
        LDX.inc(sync.dma_start(out=onesc[:], in_=A["onesc"][:]), 16)
        LDX.inc(sync.dma_start(out=onesr[:], in_=A["onesr"][:]), 16)
        LDX.inc(sync.dma_start(out=eyef[:], in_=A["eye_f32"][:]), 16)
    block.sync(_s0_sync)
    ldx0 = LDX.n

    def _s0_gp(gpsimd):
        LDW.inc(gpsimd.dma_start(out=wb[:].rearrange("p (c o) -> p c o", c=ND),
                                 in_=Wb.rearrange("(c p) o -> p c o", p=128)), 16)
    block.gpsimd(_s0_gp)

    for t4 in range(4):
        def _pe(tensor, t4=t4):
            tensor.wait_ge(LDX.sem, ldx0)
            LDW.wait(tensor)
            CS.wait(tensor)
            for c in range(ND):
                mm = tensor.matmul(psbt[:], wb[:, c * 16:(c + 1) * 16],
                                   xnf[:, c * N + t4 * 512: c * N + (t4 + 1) * 512],
                                   start=(c == 0), stop=(c == ND - 1))
            CP.inc(mm)
        block.tensor(_pe)

        def _sc(scalar, t4=t4):
            CP.wait(scalar)
            CS.inc(scalar.activation(betaf[:, t4 * 512:(t4 + 1) * 512], psbt[:],
                                     AF.Sigmoid))
        block.scalar(_sc)

    def _bv(vector):
        CS.wait(vector)
        CE.inc(vector.tensor_scalar(betaf[:], betaf[:], 2.0, None, ALU.mult))
    block.vector(_bv)

    def _bsy(sync):
        CE.wait(sync)
        STO.inc(sync.dma_start(out=betad[:], in_=betaf[:]), 16)
    block.sync(_bsy)
    betad_sto = STO.n

    for c in range(NT):
        def _pe(tensor, c=c):
            CE.wait(tensor)
            tensor.wait_ge(LDX.sem, ldx0)
            CP.inc(tensor.transpose(pstr[:], betaf[:, c * 128:(c + 1) * 128],
                                    eyef[0:16, 0:16]))
        block.tensor(_pe)

        def _ve(vector, c=c):
            CP.wait(vector)
            STO.wait(vector)
            CE.inc(vector.tensor_copy(betat[:], pstr[:]))
        block.vector(_ve)

        def _sy(sync, c=c):
            CE.wait(sync)
            STO.inc(sync.dma_start(out=betatd[c * 128:(c + 1) * 128, :], in_=betat[:]), 16)
        block.sync(_sy)

    for h in range(16):
        def _gp(gpsimd, h=h):
            CP.wait(gpsimd)
            LDW.inc(gpsimd.dma_start(
                out=wq[:].rearrange("p (c o) -> p c o", c=ND),
                in_=Wq[:, h * 128:(h + 1) * 128].rearrange("(c p) o -> p c o", p=128)), 16)
            LDW.inc(gpsimd.dma_start(
                out=wk[:].rearrange("p (c o) -> p c o", c=ND),
                in_=Wk[:, h * 128:(h + 1) * 128].rearrange("(c p) o -> p c o", p=128)), 16)
        block.gpsimd(_gp)

        for t4 in range(4):
            def _pe(tensor, t4=t4):
                LDW.wait(tensor)
                tensor.wait_ge(LDX.sem, ldx0)
                CE.wait(tensor)
                for c in range(ND):
                    mm = tensor.matmul(psq[:], wq[:, c * 128:(c + 1) * 128],
                                       xnf[:, c * N + t4 * 512: c * N + (t4 + 1) * 512],
                                       start=(c == 0), stop=(c == ND - 1))
                CP.inc(mm)
            block.tensor(_pe)

            def _ve(vector, t4=t4):
                CP.wait(vector)
                CG.wait(vector)
                CE.inc(vector.tensor_copy(qa[:, t4 * 512:(t4 + 1) * 512], psq[:]))
            block.vector(_ve)

        def _gp2(gpsimd, h=h):
            CE.wait(gpsimd)
            gpsimd.wait_ge(LDX.sem, ldx0)
            STO.wait(gpsimd)
            CG.inc(gpsimd.tensor_copy(qsw[0:64, :], qa[64:128, :]))
            CG.swait(gpsimd)
            CG.inc(gpsimd.tensor_copy(qsw[64:128, :], qa[0:64, :]))
            CG.swait(gpsimd)
            CG.inc(gpsimd.tensor_tensor(qa[:], qa[:], ccq[:], ALU.mult))
            CG.swait(gpsimd)
            CG.inc(gpsimd.tensor_tensor(qsw[:], qsw[:], ssq[:], ALU.mult))
            CG.swait(gpsimd)
            CG.inc(gpsimd.tensor_add(qout[:], qa[:], qsw[:]))
        block.gpsimd(_gp2)
        qrope_done = CG.n

        def _qst(sync, h=h):
            sync.wait_ge(CG.sem, qrope_done)
            STO.inc(sync.dma_start(out=qf[h * 128:(h + 1) * 128, :], in_=qout[:]), 16)
        block.sync(_qst)

        for t4 in range(4):
            def _pe(tensor, t4=t4):
                CE.wait(tensor)
                for c in range(ND):
                    mm = tensor.matmul(psq[:], wk[:, c * 128:(c + 1) * 128],
                                       xnf[:, c * N + t4 * 512: c * N + (t4 + 1) * 512],
                                       start=(c == 0), stop=(c == ND - 1))
                CP.inc(mm)
            block.tensor(_pe)

            def _ve(vector, t4=t4):
                CP.wait(vector)
                CS.wait(vector)
                CE.inc(vector.tensor_copy(kall[:, t4 * 512:(t4 + 1) * 512], psq[:]))
            block.vector(_ve)

        def _sc(scalar):
            CE.wait(scalar)
            CS.inc(scalar.activation(ksq[:], kall[:], AF.Square))
        block.scalar(_sc)

        for t4 in range(4):
            def _pe(tensor, t4=t4):
                CS.wait(tensor)
                CE.wait(tensor)
                CP.inc(tensor.matmul(ps1[:], onesc[:], ksq[:, t4 * 512:(t4 + 1) * 512],
                                     start=True, stop=True))
            block.tensor(_pe)

            def _ve(vector, t4=t4):
                CP.wait(vector)
                CE.inc(vector.tensor_copy(s_row[0:1, t4 * 512:(t4 + 1) * 512], ps1[:]))
            block.vector(_ve)

        def _ve2(vector):
            CE.wait(vector)
            CE.inc(vector.tensor_scalar(s_row[:], s_row[:], 1e-6, None, ALU.add))
        block.vector(_ve2)

        def _sc2(scalar):
            CE.wait(scalar)
            CS.inc(scalar.activation(s_row[:], s_row[:], AF.Ln))
            CS.wait(scalar)
            CS.inc(scalar.activation(s_row[:], s_row[:], AF.Exp, scale=-0.5))
        block.scalar(_sc2)

        for t4 in range(4):
            def _pe(tensor, t4=t4):
                CS.wait(tensor)
                CE.wait(tensor)
                CP.inc(tensor.matmul(psb[:], onesr[:],
                                     s_row[0:1, t4 * 512:(t4 + 1) * 512],
                                     start=True, stop=True))
            block.tensor(_pe)

            def _ve(vector, t4=t4):
                CP.wait(vector)
                CE.inc(vector.tensor_tensor(knr[:, t4 * 512:(t4 + 1) * 512],
                                            kall[:, t4 * 512:(t4 + 1) * 512],
                                            psb[:], ALU.mult))
            block.vector(_ve)

        def _krope(vector):
            CE.swait(vector)
            CE.inc(vector.tensor_copy(ksw[0:64, :], knr[64:128, :]))
            CE.swait(vector)
            CE.inc(vector.tensor_copy(ksw[64:128, :], knr[0:64, :]))
            CE.swait(vector)
            CE.inc(vector.tensor_tensor(kt1[:], knr[:], cck[:], ALU.mult))
            CE.swait(vector)
            CE.inc(vector.tensor_tensor(ksw[:], ksw[:], ssk[:], ALU.mult))
            CE.wait(vector)
            STO.wait(vector)
            CE.inc(vector.tensor_add(knr[:], kt1[:], ksw[:]))
            CE.swait(vector)
            CE.inc(vector.tensor_copy(kfout[:], knr[:]))
        block.vector(_krope)

        def _kst(sync, h=h):
            CE.wait(sync)
            STO.inc(sync.dma_start(out=kf[h * 128:(h + 1) * 128, :], in_=kfout[:]), 16)
            CP.wait(sync)
            LDX.inc(sync.dma_start(out=brow[:], in_=betad[h:h + 1, :]), 16)
        block.sync(_kst)
        brow_ld = LDX.n

        for t4 in range(4):
            def _pe(tensor, t4=t4):
                tensor.wait_ge(LDX.sem, brow_ld)
                CE.wait(tensor)
                CP.inc(tensor.matmul(psb[:], onesr[:],
                                     brow[0:1, t4 * 512:(t4 + 1) * 512],
                                     start=True, stop=True))
            block.tensor(_pe)

            def _ve(vector, t4=t4):
                CP.wait(vector)
                if t4 == 0:
                    STO.wait(vector)
                CE.inc(vector.tensor_tensor(ksfout[:, t4 * 512:(t4 + 1) * 512],
                                            knr[:, t4 * 512:(t4 + 1) * 512],
                                            psb[:], ALU.mult))
            block.vector(_ve)

        def _ksst(sync, h=h):
            CE.wait(sync)
            STO.inc(sync.dma_start(out=ksf[h * 128:(h + 1) * 128, :], in_=ksfout[:]), 16)
        block.sync(_ksst)

    def _fin(sync):
        STO.wait(sync)
        LDX.wait(sync)
        LDW.wait(sync)
    block.sync(_fin)

    done = (STO.sem, STO.n)
    return {n: done for n in ["qf", "kf", "ksf", "betad", "betatd"]}


@phase("l0v", ins=["xnt", "Wv"], outs=["vt"])
def ph_l0v(ctx, deps):
    block = ctx.block
    A = ctx.aps
    xnt, Wv, vt = A["xnt"], A["Wv"], A["vt"]
    xnf = ctx.sbuf("v_xnf", [128, ND * N], BF16)
    wv = ctx.sbuf("v_wv", [128, ND * N], BF16)
    vrow = ctx.sbuf("v_vrow", [128, N], BF16)
    psv = ctx.psum("v_psv", [128, 512], F32)
    LDX, LDW, CP, CE, STO = (ctx.chain() for _ in range(5))
    dep_xnt = deps.get("xnt") if deps else None

    def _s0(sync):
        if dep_xnt:
            sync.wait_ge(dep_xnt[0], dep_xnt[1])
        for c in range(ND):
            LDX.inc(sync.dma_start(out=xnf[:, c * N:(c + 1) * N],
                                   in_=xnt[:, c * 128:(c + 1) * 128], transpose=True), 16)
    block.sync(_s0)
    ldx0 = LDX.n

    def _b_gp(gpsimd):
        for c in range(ND):
            LDW.inc(gpsimd.dma_start(out=wv[:, c * N:(c + 1) * N],
                                     in_=Wv[c * 128:(c + 1) * 128, :]), 16)
    block.gpsimd(_b_gp)
    ldv = LDW.n

    for tt in range(NT):
        for dv4 in range(4):
            def _pe(tensor, tt=tt, dv4=dv4):
                tensor.wait_ge(LDW.sem, ldv)
                tensor.wait_ge(LDX.sem, ldx0)
                CE.wait(tensor)
                for c in range(ND):
                    mm = tensor.matmul(psv[:],
                                       xnf[:, c * N + tt * 128: c * N + (tt + 1) * 128],
                                       wv[:, c * N + dv4 * 512: c * N + (dv4 + 1) * 512],
                                       start=(c == 0), stop=(c == ND - 1))
                CP.inc(mm)
            block.tensor(_pe)

            def _ve(vector, tt=tt, dv4=dv4):
                CP.wait(vector)
                if dv4 == 0:
                    STO.wait(vector)
                CE.inc(vector.tensor_copy(vrow[:, dv4 * 512:(dv4 + 1) * 512], psv[:]))
            block.vector(_ve)

        def _sy(sync, tt=tt):
            CE.wait(sync)
            STO.inc(sync.dma_start(out=vt[tt * 128:(tt + 1) * 128, :], in_=vrow[:]), 16)
        block.sync(_sy)

    def _fin(sync):
        STO.wait(sync)
        LDX.wait(sync)
        LDW.wait(sync)
    block.sync(_fin)
    return {"vt": (STO.sem, STO.n)}


VARS.update({
    "qf": ((D, N), BF16),
    "kf": ((D, N), BF16),
    "ksf": ((D, N), BF16),
    "vt": ((N, D), BF16),
    "betad": ((16, N), F32),
    "betatd": ((N, 16), F32),
})
CONSTS_SHAPES.update({
    "Wq": ((D, D), F32), "Wk": ((D, D), F32), "Wv": ((D, D), F32), "Wb": ((D, 16), F32),
    "ccq": ((128, N), BF16), "ssq": ((128, N), BF16),
    "cck": ((128, N), BF16), "ssk": ((128, N), BF16),
    "onesc": ((128, 1), F32), "onesr": ((1, 128), F32), "eye_f32": ((128, 128), F32),
})


@phase("l0reg", ins=["kf", "vt", "betatd", "eye_bf", "mtriu"], outs=["vnt"])
def ph_l0reg(ctx, deps):
    block = ctx.block
    A = ctx.aps
    kf, vt, betatd, vnt = A["kf"], A["vt"], A["betatd"], A["vnt"]
    NH, CH, NCH = 16, 64, 32

    kall = ctx.sbuf("rg_kall", [128, NH * N], BF16)
    vall = ctx.sbuf("rg_vall", [128, 16 * D], BF16)
    bth = ctx.sbuf("rg_bth", [64, 32 * NH], F32)
    eye = ctx.sbuf("rg_eye", [128, 128], BF16)
    mtriu = ctx.sbuf("rg_mtriu", [64, 64], F32)
    S_f = ctx.sbuf("rg_Sf", [128, NH * 128], F32)
    S_b = ctx.sbuf("rg_Sb", [128, NH * 128], BF16)
    t1 = ctx.sbuf("rg_t1", [64, NH * 64], F32)
    nt_b = ctx.sbuf("rg_nt", [64, NH * 64], BF16)
    n_b = ctx.sbuf("rg_n", [64, NH * 64], BF16)
    nt2_b = ctx.sbuf("rg_nt2", [64, NH * 64], BF16)
    n2_b = ctx.sbuf("rg_n2", [64, NH * 64], BF16)
    r_b = ctx.sbuf("rg_r", [64, NH * 128], BF16)
    vnb = ctx.sbuf("rg_vnb", [64, NH * 128], BF16)
    kt_b = ctx.sbuf("rg_kt", [64, NH * 128], BF16)
    psbig = ctx.psum("rg_psbig", [64, 2048], F32)
    pstr = ctx.psum("rg_pstr", [64, 2048], BF16)
    psS = ctx.psum("rg_psS", [128, 512], F32)

    LDL, CP, CE, STO = (ctx.chain() for _ in range(4))
    for nm in ("kf", "vt", "betatd"):
        dv = deps.get(nm) if deps else None
        if dv:
            def _w(sync, dv=dv):
                sync.wait_ge(dv[0], dv[1])
            block.sync(_w)

    def _ld(sync):
        for h in range(NH):
            LDL.inc(sync.dma_start(out=kall[:, h * N:(h + 1) * N],
                                   in_=kf[h * 128:(h + 1) * 128, :]), 16)
        LDL.inc(sync.dma_start(out=vall[:].rearrange("p (g d) -> p g d", g=16),
                               in_=vt.rearrange("(g p) d -> p g d", p=128)), 16)
        LDL.inc(sync.dma_start(out=bth[:].rearrange("p (c h) -> p c h", c=32),
                               in_=betatd.rearrange("(c p) h -> p c h", p=64)), 16)
        LDL.inc(sync.dma_start(out=eye[:], in_=A["eye_bf"][:]), 16)
        LDL.inc(sync.dma_start(out=mtriu[:], in_=A["mtriu"][:]), 16)
    block.sync(_ld)

    def _init(vector):
        LDL.wait(vector)
        CE.inc(vector.memset(S_f[:], 0.0))
        CE.swait(vector)
        CE.inc(vector.memset(S_b[:], 0.0))
    block.vector(_init)

    for c in range(NCH):
        po = (c % 2) * 64
        g = c // 2

        def kslice(h, c=c):
            return kall[:, h * N + c * 64: h * N + (c + 1) * 64]

        def _pe_kk(tensor, c=c):
            LDL.wait(tensor)
            CE.wait(tensor)
            for h in range(NH):
                mm = tensor.matmul(psbig[:, h * 64:(h + 1) * 64], kslice(h), kslice(h),
                                   start=True, stop=True)
            CP.inc(mm)
        block.tensor(_pe_kk)

        def _ve_nt(vector, c=c, po=po, g=g):
            CP.wait(vector)
            CE.swait(vector)
            btc = bth[:].rearrange("p (c h) -> p c h", c=32)[:, c, :]
            btb = btc.unsqueeze(2).broadcast_to([64, NH, 64])
            CE.inc(vector.tensor_tensor(
                t1[:].rearrange("p (h j) -> p h j", h=NH),
                psbig[:, 0:1024].rearrange("p (h j) -> p h j", h=NH),
                btb, ALU.mult))
            CE.swait(vector)
            mtb = mtriu[:].unsqueeze(1).broadcast_to([64, NH, 64])
            CE.inc(vector.tensor_tensor(
                nt_b[:].rearrange("p (h j) -> p h j", h=NH),
                t1[:].rearrange("p (h j) -> p h j", h=NH),
                mtb, ALU.mult))
        block.vector(_ve_nt)

        def _pe_tr(tensor, c=c):
            CE.wait(tensor)
            for h in range(NH):
                mm = tensor.transpose(pstr[:, h * 64:(h + 1) * 64],
                                      nt_b[:, h * 64:(h + 1) * 64], eye[0:64, 0:64])
            CP.inc(mm)
        block.tensor(_pe_tr)

        def _ve_n(vector, c=c):
            CP.wait(vector)
            CE.inc(vector.tensor_copy(n_b[:], pstr[:, 0:1024]))
        block.vector(_ve_n)

        def _pe_ks(tensor, c=c):
            CE.wait(tensor)
            for h in range(NH):
                mm = tensor.matmul(psbig[:, h * 128:(h + 1) * 128], kslice(h),
                                   S_b[:, h * 128:(h + 1) * 128], start=True, stop=True)
            CP.inc(mm)
        block.tensor(_pe_ks)

        def _ve_r(vector, c=c, po=po, g=g):
            CP.wait(vector)
            STO.wait(vector)
            vsl = vall[po:po + 64, :].rearrange("p (g d) -> p g d", g=16)[:, g, :]
            CE.inc(vector.tensor_tensor(r_b[:], vsl, psbig[:, 0:2048], ALU.subtract))
        block.vector(_ve_r)

        cn, cnt = n_b, nt_b
        nn, nnt = n2_b, nt2_b
        for lvl in range(6):
            def _pe_f(tensor, cnt=cnt, c=c):
                CE.wait(tensor)
                for h in range(NH):
                    mm = tensor.matmul(psbig[:, h * 128:(h + 1) * 128],
                                       cnt[:, h * 64:(h + 1) * 64],
                                       r_b[:, h * 128:(h + 1) * 128],
                                       start=True, stop=True)
                CP.inc(mm)
            block.tensor(_pe_f)

            def _ve_f(vector, c=c):
                CP.wait(vector)
                CE.inc(vector.tensor_add(r_b[:], r_b[:], psbig[:, 0:2048]))
            block.vector(_ve_f)

            if lvl < 5:
                def _pe_s(tensor, cn=cn, cnt=cnt, c=c):
                    CE.wait(tensor)
                    for h in range(NH):
                        mm = tensor.matmul(psbig[:, h * 64:(h + 1) * 64],
                                           cnt[:, h * 64:(h + 1) * 64],
                                           cn[:, h * 64:(h + 1) * 64],
                                           start=True, stop=True)
                    CP.inc(mm)
                block.tensor(_pe_s)

                def _ve_s(vector, nn=nn, c=c):
                    CP.wait(vector)
                    CE.inc(vector.tensor_copy(nn[:], psbig[:, 0:1024]))
                block.vector(_ve_s)

                def _pe_s2(tensor, nn=nn, c=c):
                    CE.wait(tensor)
                    for h in range(NH):
                        mm = tensor.transpose(pstr[:, h * 64:(h + 1) * 64],
                                              nn[:, h * 64:(h + 1) * 64],
                                              eye[0:64, 0:64])
                    CP.inc(mm)
                block.tensor(_pe_s2)

                def _ve_s2(vector, nnt=nnt, c=c):
                    CP.wait(vector)
                    CE.inc(vector.tensor_copy(nnt[:], pstr[:, 0:1024]))
                block.vector(_ve_s2)
                cn, cnt, nn, nnt = nn, nnt, cn, cnt

        def _ve_vnb(vector, c=c, po=po, g=g):
            CE.swait(vector)
            btc = bth[:].rearrange("p (c h) -> p c h", c=32)[:, c, :]
            btb = btc.unsqueeze(2).broadcast_to([64, NH, 128])
            CE.inc(vector.tensor_tensor(
                vnb[:].rearrange("p (h d) -> p h d", h=NH),
                r_b[:].rearrange("p (h d) -> p h d", h=NH),
                btb, ALU.mult))
        block.vector(_ve_vnb)

        def _sy_vn(sync, c=c):
            CE.wait(sync)
            STO.inc(sync.dma_start(out=vnt[c * 64:(c + 1) * 64, :], in_=r_b[:]), 16)
        block.sync(_sy_vn)

        def _pe_kt(tensor, c=c):
            CE.wait(tensor)
            for h in range(NH):
                mm = tensor.transpose(pstr[:, h * 128:(h + 1) * 128], kslice(h), eye[:])
            CP.inc(mm)
        block.tensor(_pe_kt)

        def _ve_kt(vector, c=c):
            CP.wait(vector)
            CE.inc(vector.tensor_copy(kt_b[:], pstr[:, 0:2048]))
        block.vector(_ve_kt)

        for hg in range(4):
            def _pe_ds(tensor, hg=hg, c=c):
                CE.wait(tensor)
                for j in range(4):
                    h = hg * 4 + j
                    mm = tensor.matmul(psS[:, j * 128:(j + 1) * 128],
                                       kt_b[:, h * 128:(h + 1) * 128],
                                       vnb[:, h * 128:(h + 1) * 128],
                                       start=True, stop=True)
                CP.inc(mm)
            block.tensor(_pe_ds)

            def _ve_ds(vector, hg=hg, c=c):
                CP.wait(vector)
                CE.inc(vector.tensor_add(S_f[:, hg * 512:(hg + 1) * 512],
                                         S_f[:, hg * 512:(hg + 1) * 512], psS[:]))
            block.vector(_ve_ds)

        def _ve_sb(vector, c=c):
            CE.swait(vector)
            CE.inc(vector.tensor_copy(S_b[:], S_f[:]))
        block.vector(_ve_sb)

    def _fin(sync):
        STO.wait(sync)
        LDL.wait(sync)
    block.sync(_fin)
    return {"vnt": (STO.sem, STO.n)}


VARS["vnt"] = ((N, D), BF16)
CONSTS_SHAPES["mtriu"] = ((64, 64), F32)


@phase("l0attn", ins=["qf", "ksf", "vnt", "Wo", "x0", "eye_bf", "cmaskb"],
       outs=["x1a"])
def ph_l0attn(ctx, deps):
    block = ctx.block
    A = ctx.aps
    qf, ksf, vnt, Wo, x0, x1a = (A[n] for n in ["qf", "ksf", "vnt", "Wo", "x0", "x1a"])
    NH = 16

    kbuf = ctx.sbuf("at_kbuf", [128, N], BF16)
    vh = ctx.sbuf("at_vh", [128, 16 * 128], BF16)
    qrow = ctx.sbuf("at_qrow", [128, N], BF16)
    eye = ctx.sbuf("at_eye", [128, 128], BF16)
    cmask = ctx.sbuf("at_cmask", [128, 128], BF16)
    avf = ctx.sbuf("at_avf", [128, NH * N], BF16)
    pb = [ctx.sbuf("at_pb", [128, N], BF16) for _ in range(2)]
    ptall = [ctx.sbuf("at_ptall", [128, N], BF16) for _ in range(2)]
    srow = [ctx.sbuf("at_srow", [128, 4], F32) for _ in range(2)]
    psS = ctx.psum("at_psS", [128, 2048], F32)
    pstr = ctx.psum("at_pstr", [128, 2048], BF16)
    psA = ctx.psum("at_psA", [128, 512], F32)

    LDL, LDG, LDX2, CP, CE, CS, STO = (ctx.chain() for _ in range(7))
    for nm in ("qf", "ksf", "vnt"):
        dv = deps.get(nm) if deps else None
        if dv:
            def _w(sync, dv=dv):
                sync.wait_ge(dv[0], dv[1])
            block.sync(_w)

    def _ld0(sync):
        LDL.inc(sync.dma_start(out=eye[:], in_=A["eye_bf"][:]), 16)
        LDL.inc(sync.dma_start(out=cmask[:], in_=A["cmaskb"][:]), 16)
    block.sync(_ld0)

    SMM, MAXT, EXPT, NRM, TDONE, PTC, AVT, AVC = {}, {}, {}, {}, {}, {}, {}, {}

    def emit_T_AV(h, i):
        par = i % 2

        def _pe_t(tensor, h=h, i=i, par=par):
            tensor.wait_ge(CE.sem, NRM[h, i])
            tensor.wait_ge(CE.sem, PTC.get((h, i - 2), 0))
            for kc in range(i + 1):
                mm = tensor.transpose(pstr[:, kc * 128:(kc + 1) * 128],
                                      pb[par][:, kc * 128:(kc + 1) * 128], eye[:])
            CP.inc(mm)
            TDONE[h, i] = CP.n
        block.tensor(_pe_t)

        def _ve_t(vector, h=h, i=i, par=par, kr=(i + 1) * 128):
            vector.wait_ge(CP.sem, TDONE[h, i])
            vector.wait_ge(CP.sem, AVT.get((h, i - 2), 0))
            CE.inc(vector.tensor_copy(ptall[par][:, 0:kr], pstr[:, 0:kr]))
            PTC[h, i] = CE.n
        block.vector(_ve_t)

        def _pe_av(tensor, h=h, i=i, par=par):
            tensor.wait_ge(CE.sem, PTC[h, i])
            tensor.wait_ge(CE.sem, AVC.get((h, i - 1), 0))
            for kc in range(i + 1):
                mm = tensor.matmul(psA[:, 0:128],
                                   vh[:].rearrange("p (g d) -> p g d", g=16)[:, kc, :],
                                   ptall[par][:, kc * 128:(kc + 1) * 128],
                                   start=(kc == 0), stop=(kc == i))
            CP.inc(mm)
            AVT[h, i] = CP.n
        block.tensor(_pe_av)

        def _ve_av(vector, h=h, i=i):
            vector.wait_ge(CP.sem, AVT[h, i])
            CE.inc(vector.tensor_copy(avf[:, h * N + i * 128: h * N + (i + 1) * 128],
                                      psA[:, 0:128]))
            AVC[h, i] = CE.n
        block.vector(_ve_av)

    for h in range(NH):
        def _ldh(sync, h=h):
            CP.wait(sync)
            LDL.inc(sync.dma_start(out=kbuf[:], in_=ksf[h * 128:(h + 1) * 128, :]), 16)
            LDL.inc(sync.dma_start(
                out=vh[:].rearrange("p (g d) -> p g d", g=16),
                in_=vnt[:, h * 128:(h + 1) * 128].rearrange("(g p) d -> p g d", p=128)), 16)
            LDL.inc(sync.dma_start(out=qrow[:], in_=qf[h * 128:(h + 1) * 128, :]), 16)
        block.sync(_ldh)
        ldh_t = LDL.n

        for i in range(NT):
            kr = (i + 1) * 128
            par = i % 2

            def _pe_s(tensor, h=h, i=i, ldh_t=ldh_t):
                tensor.wait_ge(LDL.sem, ldh_t)
                tensor.wait_ge(CS.sem, EXPT.get((h, i - 1), 0))
                qt = qrow[:, i * 128:(i + 1) * 128]
                f = (i * 128) // 512
                for b in range(f):
                    tensor.matmul(psS[:, b * 512:(b + 1) * 512], qt,
                                  kbuf[:, b * 512:(b + 1) * 512], start=True, stop=True)
                for b in range(i - 4 * f):
                    o = f * 512 + b * 128
                    tensor.matmul(psS[:, o:o + 128], qt, kbuf[:, o:o + 128],
                                  start=True, stop=True)
                o = i * 128
                tensor.matmul(psS[:, o:o + 128], qt, kbuf[:, o:o + 128],
                              start=True, stop=False)
                mm = tensor.matmul(psS[:, o:o + 128], eye[:], cmask[:],
                                   start=False, stop=True)
                CP.inc(mm)
                SMM[h, i] = CP.n
            block.tensor(_pe_s)

            def _ve_m(vector, h=h, i=i, kr=kr, par=par):
                vector.wait_ge(CP.sem, SMM[h, i])
                vector.wait_ge(CS.sem, EXPT.get((h, i - 2), 0))
                CE.inc(vector.reduce_max(srow[par][:, 0:1], psS[:, 0:kr], axis=AX.X,
                                         negate=True))
                MAXT[h, i] = CE.n
            block.vector(_ve_m)

            def _sc_e(scalar, h=h, i=i, kr=kr, par=par):
                scalar.wait_ge(CE.sem, MAXT[h, i])
                scalar.wait_ge(CP.sem, TDONE.get((h, i - 2), 0))
                CS.inc(scalar.activation(pb[par][:, 0:kr], psS[:, 0:kr], AF.Exp,
                                         bias=srow[par][:, 0:1],
                                         accum_out=srow[par][:, 1:2]))
                EXPT[h, i] = CS.n
            block.scalar(_sc_e)

            def _ve_n(vector, h=h, i=i, kr=kr, par=par):
                vector.wait_ge(CS.sem, EXPT[h, i])
                CE.inc(vector.reciprocal(srow[par][:, 2:3], srow[par][:, 1:2]))
                CE.wait(vector)
                CE.inc(vector.tensor_scalar(pb[par][:, 0:kr], pb[par][:, 0:kr],
                                            srow[par][:, 2:3], None, ALU.mult))
                NRM[h, i] = CE.n
            block.vector(_ve_n)

            if i >= 1:
                emit_T_AV(h, i - 1)
        emit_T_AV(h, NT - 1)
    wosb = ctx.sbuf("at_wo", [128, NH * N], BF16)
    x0row = ctx.sbuf("at_x0r", [128, N], F32)
    x1row = ctx.sbuf("at_x1r", [128, N], F32)
    psW = ctx.psum("at_psW", [128, 512], F32)

    def _ldwo(gpsimd):
        for hc in range(NH):
            LDG.inc(gpsimd.dma_start(out=wosb[:, hc * N:(hc + 1) * N],
                                     in_=Wo[hc * 128:(hc + 1) * 128, :]), 16)
    block.gpsimd(_ldwo)
    ldwo = LDG.n

    for i in range(NT):
        def _ldx(sync, i=i):
            STO.wait(sync)
            LDX2.inc(sync.dma_start(out=x0row[:], in_=x0[i * 128:(i + 1) * 128, :]), 16)
        block.sync(_ldx)

        for d4 in range(4):
            def _pe_w(tensor, i=i, d4=d4):
                tensor.wait_ge(LDG.sem, ldwo)
                CE.wait(tensor)
                for hc in range(NH):
                    mm = tensor.matmul(psW[:],
                                       avf[:, hc * N + i * 128: hc * N + (i + 1) * 128],
                                       wosb[:, hc * N + d4 * 512: hc * N + (d4 + 1) * 512],
                                       start=(hc == 0), stop=(hc == NH - 1))
                CP.inc(mm)
            block.tensor(_pe_w)

            def _ve_w(vector, i=i, d4=d4):
                CP.wait(vector)
                LDX2.wait(vector)
                CE.inc(vector.tensor_add(x1row[:, d4 * 512:(d4 + 1) * 512],
                                         x0row[:, d4 * 512:(d4 + 1) * 512], psW[:]))
            block.vector(_ve_w)

        def _st(sync, i=i):
            CE.wait(sync)
            STO.inc(sync.dma_start(out=x1a[i * 128:(i + 1) * 128, :], in_=x1row[:]), 16)
        block.sync(_st)

    def _fin(sync):
        STO.wait(sync)
        LDL.wait(sync)
        LDG.wait(sync)
        LDX2.wait(sync)
    block.sync(_fin)
    return {"x1a": (STO.sem, STO.n)}


VARS["x1a"] = ((N, D), F32)
CONSTS_SHAPES["Wo"] = ((D, D), F32)
CONSTS_SHAPES["cmaskb"] = ((128, 128), BF16)


@phase("l0moe", ins=["xnt", "x1a", "gwt", "sgw", "w_gate", "w_up", "w_down",
                     "sw_gate", "sw_up", "sw_down", "onesr", "eye_f32"],
       outs=["x1b"])
def ph_l0moe(ctx, deps):
    nc, block = ctx.nc, ctx.block
    A = ctx.aps
    xnt, x1a, x1b = A["xnt"], A["x1a"], A["x1b"]
    macc = nc.dram_tensor("macc", [N, D], F32, kind="Internal").ap()
    cmbd = nc.dram_tensor("cmbd", [16, N], F32, kind="Internal").ap()
    HT = N
    NI = 1408
    NIC = NI // 128

    xnf = ctx.sbuf("mo_xnf", [128, ND * HT], BF16)
    gw = ctx.sbuf("mo_gw", [128, ND * 8], BF16)
    sgwb = ctx.sbuf("mo_sgw", [128, ND * 1], BF16)
    onesr = ctx.sbuf("mo_onesr", [1, 128], F32)
    eyef = ctx.sbuf("mo_eyef", [128, 128], F32)
    wg = ctx.sbuf("mo_wg", [128, 2 * ND * 128], BF16)
    wu = ctx.sbuf("mo_wu", [128, 2 * ND * 128], BF16)
    wd = ctx.sbuf("mo_wd", [128, 2 * NIC * 512], BF16)
    hsb = ctx.sbuf("mo_h", [128, NIC * HT], BF16)
    sg = ctx.sbuf("mo_sg", [128, 1024], F32)
    cb = ctx.sbuf("mo_cb", [128, HT], F32)
    crow = ctx.sbuf("mo_crow", [1, HT], F32)
    dall = ctx.sbuf("mo_dall", [128, 16 * 512], F32)
    prob = ctx.sbuf("mo_prob", [128, 16], F32)
    tst = ctx.sbuf("mo_tst", [128, 8], F32)
    ctok = ctx.sbuf("mo_ctok", [128, 16], F32)
    ctr = ctx.sbuf("mo_ctr", [16, 128], F32)
    x0r, x1r = None, None
    x0r = dall[:, 0:N]
    x1r = dall[:, N:2 * N]
    psG = ctx.psum("mo_psG", [128, 1024], F32)
    psU = ctx.psum("mo_psU", [128, 1024], F32)
    psD = ctx.psum("mo_psD", [128, 1024], F32)
    psB = ctx.psum("mo_psB", [128, 512], F32)
    psT = ctx.psum("mo_psT", [16, 128], F32)
    psL = psD

    LDXC, LDXN, LDCR, LDF, LDW, MST, CP, CE, CS, STO = (ctx.chain() for _ in range(10))
    for nm in ("xnt", "x1a"):
        dv = deps.get(nm) if deps else None
        if dv:
            def _w(sync, dv=dv):
                sync.wait_ge(dv[0], dv[1])
            block.sync(_w)

    def _ld0(sync):
        LDXC.inc(sync.dma_start(out=onesr[:], in_=A["onesr"][:]), 16)
        LDXC.inc(sync.dma_start(out=eyef[:], in_=A["eye_f32"][:]), 16)
    block.sync(_ld0)

    def _ldg(gpsimd):
        LDW.inc(gpsimd.dma_start(out=gw[:].rearrange("p (c o) -> p c o", c=ND),
                                 in_=A["gwt"].rearrange("(c p) o -> p c o", p=128)), 16)
        LDW.inc(gpsimd.dma_start(out=sgwb[:], in_=A["sgw"][:]), 16)
        LDW.wait(gpsimd)
    block.gpsimd(_ldg)

    GUK, GUG, DNK, DNG = [0], [0], [0], [0]
    GUMM, SILU, HDONE, DNMM, DEV = {}, {}, {}, {}, {}
    LDWBASE = 32
    pending_acc = [None]

    def flush_acc():
        if pending_acc[0] is not None:
            f = pending_acc[0]
            pending_acc[0] = None
            block.gpsimd(f)
    for s in range(1):
        t0 = 0

        def _ldxn(sync, s=s, t0=t0):
            CP.wait(sync)
            for c in range(ND):
                LDXN.inc(sync.dma_start(out=xnf[:, c * HT:(c + 1) * HT],
                                        in_=xnt[t0:t0 + HT, c * 128:(c + 1) * 128],
                                        transpose=True), 16)
        block.sync(_ldxn)

        for tt in range(16):
            def _pe_l(tensor, s=s, tt=tt):
                LDXN.wait(tensor)
                LDXC.wait(tensor)
                LDW.wait(tensor)
                CE.wait(tensor)
                for c in range(ND):
                    tensor.matmul(psL[:, 0:8],
                                  xnf[:, c * HT + tt * 128: c * HT + (tt + 1) * 128],
                                  gw[:, c * 8:(c + 1) * 8],
                                  start=(c == 0), stop=(c == ND - 1))
                for c in range(ND):
                    mm = tensor.matmul(psL[:, 8:9],
                                       xnf[:, c * HT + tt * 128: c * HT + (tt + 1) * 128],
                                       sgwb[:, c:c + 1],
                                       start=(c == 0), stop=(c == ND - 1))
                CP.inc(mm)
            block.tensor(_pe_l)

            def _ve_sm(vector, s=s, tt=tt):
                CP.wait(vector)
                CE.inc(vector.reduce_max(tst[:, 4:5], psL[:, 0:8], axis=AX.X,
                                         negate=True))
            block.vector(_ve_sm)

            def _sc_sm(scalar, s=s, tt=tt):
                CE.wait(scalar)
                CS.inc(scalar.activation(prob[:, 0:8], psL[:, 0:8], AF.Exp,
                                         bias=tst[:, 4:5], accum_out=tst[:, 5:6]))
                CS.wait(scalar)
                CS.inc(scalar.activation(prob[:, 8:9], psL[:, 8:9], AF.Sigmoid))
            block.scalar(_sc_sm)

            def _ve_c(vector, s=s, tt=tt):
                CS.wait(vector)
                CE.inc(vector.reciprocal(tst[:, 6:7], tst[:, 5:6]))
                CE.wait(vector)
                CE.inc(vector.tensor_scalar(prob[:, 0:8], prob[:, 0:8], tst[:, 6:7],
                                            None, ALU.mult))
                CE.wait(vector)
                CE.inc(vector.reduce_max(tst[:, 0:1], prob[:, 0:8], axis=AX.X))
                CE.wait(vector)
                CE.inc(vector.tensor_scalar(ctok[:, 0:8], prob[:, 0:8], tst[:, 0:1],
                                            None, ALU.is_ge))
                CE.wait(vector)
                CE.inc(vector.scalar_tensor_tensor(ctok[:, 8:16], ctok[:, 0:8], -1e30,
                                                   prob[:, 0:8], ALU.mult, ALU.add))
                CE.wait(vector)
                CE.inc(vector.reduce_max(tst[:, 1:2], ctok[:, 8:16], axis=AX.X))
                CE.wait(vector)
                CE.inc(vector.tensor_add(tst[:, 2:3], tst[:, 0:1], tst[:, 1:2]))
                CE.wait(vector)
                CE.inc(vector.reciprocal(tst[:, 3:4], tst[:, 2:3]))
                CE.wait(vector)
                CE.inc(vector.tensor_scalar(ctok[:, 0:8], prob[:, 0:8], tst[:, 1:2],
                                            None, ALU.is_ge))
                CE.wait(vector)
                CE.inc(vector.tensor_tensor(ctok[:, 0:8], ctok[:, 0:8], prob[:, 0:8],
                                            ALU.mult))
                CE.wait(vector)
                CE.inc(vector.tensor_scalar(ctok[:, 0:8], ctok[:, 0:8], tst[:, 3:4],
                                            None, ALU.mult))
                CE.wait(vector)
                CE.inc(vector.tensor_copy(ctok[:, 8:9], prob[:, 8:9]))
            block.vector(_ve_c)

            def _pe_t(tensor, s=s, tt=tt):
                CE.wait(tensor)
                CP.inc(tensor.transpose(psT[:], ctok[:], eyef[:]))
            block.tensor(_pe_t)

            def _ve_t(vector, s=s, tt=tt):
                CP.wait(vector)
                STO.wait(vector)
                CE.inc(vector.tensor_copy(ctr[:], psT[:, 0:128]))
            block.vector(_ve_t)

            def _sy_t(sync, s=s, tt=tt, t0=t0):
                CE.wait(sync)
                STO.inc(sync.dma_start(out=cmbd[:, t0 + tt * 128: t0 + (tt + 1) * 128],
                                       in_=ctr[:]), 16)
            block.sync(_sy_t)

        for e in range(12):
            if e < 8:
                wgs = A["w_gate"][e]
                wus = A["w_up"][e]
                wds = A["w_down"][e]
                cri = e
            else:
                b = e - 8
                wgs = A["sw_gate"][:, b * NI:(b + 1) * NI]
                wus = A["sw_up"][:, b * NI:(b + 1) * NI]
                wds = A["sw_down"][b * NI:(b + 1) * NI, :]
                cri = 8

            def _ldcr(sync, cri=cri, t0=t0, e=e):
                CP.wait(sync)
                STO.wait(sync)
                LDCR.inc(sync.dma_start(out=crow[:], in_=cmbd[cri:cri + 1, t0:t0 + HT]), 16)
            block.sync(_ldcr)

            for b2 in range(4):
                def _pe_b(tensor, b2=b2, e=e):
                    LDCR.wait(tensor)
                    LDXC.wait(tensor)
                    CE.wait(tensor)
                    CP.inc(tensor.matmul(psB[:], onesr[:],
                                         crow[0:1, b2 * 512:(b2 + 1) * 512],
                                         start=True, stop=True))
                block.tensor(_pe_b)

                def _ve_b(vector, b2=b2, e=e):
                    CP.wait(vector)
                    CE.inc(vector.tensor_copy(cb[:, b2 * 512:(b2 + 1) * 512],
                                              psB[:]))
                block.vector(_ve_b)

            for ic in range(NIC):
                kk = GUK[0]
                GUK[0] += 1
                kb = (kk % 2) * ND * 128

                def _ldw(gpsimd, wgs=wgs, wus=wus, ic=ic, kk=kk, kb=kb):
                    if kk >= 2:
                        gpsimd.wait_ge(CP.sem, GUMM.get(kk - 2, 0))
                    LDW.inc(gpsimd.dma_start(
                        out=wg[:, kb:kb + ND * 128].rearrange("p (c o) -> p c o", c=ND),
                        in_=wgs[:, ic * 128:(ic + 1) * 128]
                        .rearrange("(c p) o -> p c o", p=128)), 16)
                    LDW.inc(gpsimd.dma_start(
                        out=wu[:, kb:kb + ND * 128].rearrange("p (c o) -> p c o", c=ND),
                        in_=wus[:, ic * 128:(ic + 1) * 128]
                        .rearrange("(c p) o -> p c o", p=128)), 16)
                    LDW.wait(gpsimd)
                block.gpsimd(_ldw)
                flush_acc()
                LDWT = LDW.n

                for t4 in range(4):
                    gg = GUG[0]
                    GUG[0] += 1
                    sl = (gg % 2) * 512

                    def _pe_gu(tensor, ic=ic, t4=t4, e=e, kk=kk, kb=kb, gg=gg, sl=sl,
                               LDWT=None):
                        tensor.wait_ge(LDW.sem, LDWT)
                        LDXN.wait(tensor)
                        if gg >= 2:
                            tensor.wait_ge(CS.sem, SILU.get(gg - 2, 0))
                            tensor.wait_ge(CE.sem, HDONE.get(gg - 2, 0))
                        for c in range(ND):
                            tensor.matmul(psG[:, sl:sl + 512],
                                          wg[:, kb + c * 128: kb + (c + 1) * 128],
                                          xnf[:, c * HT + t4 * 512: c * HT + (t4 + 1) * 512],
                                          start=(c == 0), stop=(c == ND - 1))
                        for c in range(ND):
                            mm = tensor.matmul(psU[:, sl:sl + 512],
                                               wu[:, kb + c * 128: kb + (c + 1) * 128],
                                               xnf[:, c * HT + t4 * 512: c * HT + (t4 + 1) * 512],
                                               start=(c == 0), stop=(c == ND - 1))
                        CP.inc(mm)
                        GUMM[kk] = CP.n
                    block.tensor(lambda tensor, f=_pe_gu, L=LDWT, ic=ic, t4=t4, e=e,
                                 kk=kk, kb=kb, gg=gg, sl=sl:
                                 f(tensor, ic=ic, t4=t4, e=e, kk=kk, kb=kb, gg=gg,
                                   sl=sl, LDWT=L))

                    def _sc_g(scalar, gg=gg, sl=sl):
                        CP.wait(scalar)
                        if gg >= 2:
                            scalar.wait_ge(CE.sem, HDONE.get(gg - 2, 0))
                        CS.inc(scalar.activation(sg[:, sl:sl + 512],
                                                 psG[:, sl:sl + 512], AF.Silu))
                        SILU[gg] = CS.n
                    block.scalar(_sc_g)

                    def _ve_h(vector, ic=ic, t4=t4, gg=gg, sl=sl):
                        CS.wait(vector)
                        CE.inc(vector.tensor_tensor(sg[:, sl:sl + 512], sg[:, sl:sl + 512],
                                                    psU[:, sl:sl + 512], ALU.mult))
                        CE.wait(vector)
                        CE.inc(vector.tensor_tensor(
                            hsb[:, ic * HT + t4 * 512: ic * HT + (t4 + 1) * 512],
                            sg[:, sl:sl + 512], cb[:, t4 * 512:(t4 + 1) * 512], ALU.mult))
                        HDONE[gg] = CE.n
                    block.vector(_ve_h)

            hdone_e = CE.n
            for d4 in range(4):
                dk = DNK[0]
                DNK[0] += 1
                db = (dk % 2) * NIC * 512

                def _ldwd(gpsimd, wds=wds, d4=d4, dk=dk, db=db):
                    if dk >= 2:
                        gpsimd.wait_ge(CP.sem, DNMM.get(dk - 2, 0))
                    LDW.inc(gpsimd.dma_start(
                        out=wd[:, db:db + NIC * 512].rearrange("p (c o) -> p c o", c=NIC),
                        in_=wds[:, d4 * 512:(d4 + 1) * 512]
                        .rearrange("(c p) o -> p c o", p=128)), 16)
                    LDW.wait(gpsimd)
                block.gpsimd(_ldwd)
                flush_acc()
                wdtarget = LDW.n

                for tt in range(16):
                    dg = DNG[0]
                    DNG[0] += 1
                    dsl = (dg % 2) * 512

                    def _pe_d(tensor, d4=d4, tt=tt, e=e, dk=dk, db=db, dg=dg, dsl=dsl,
                              wdtarget=wdtarget, hdone_e=hdone_e):
                        tensor.wait_ge(LDW.sem, wdtarget)
                        tensor.wait_ge(CE.sem, hdone_e)
                        if dg >= 2:
                            tensor.wait_ge(CE.sem, DEV.get(dg - 2, 0))
                        for ic in range(NIC):
                            mm = tensor.matmul(
                                psD[:, dsl:dsl + 512],
                                hsb[:, ic * HT + tt * 128: ic * HT + (tt + 1) * 128],
                                wd[:, db + ic * 512: db + (ic + 1) * 512],
                                start=(ic == 0), stop=(ic == NIC - 1))
                        CP.inc(mm)
                        DNMM[dk] = CP.n
                    block.tensor(_pe_d)

                    def _ve_d(vector, d4=d4, tt=tt, e=e, dg=dg, dsl=dsl):
                        CP.wait(vector)
                        MST.wait(vector)
                        CE.inc(vector.tensor_copy(dall[:, tt * 512:(tt + 1) * 512],
                                                  psD[:, dsl:dsl + 512]))
                        DEV[dg] = CE.n
                    block.vector(_ve_d)

                def _acc(gpsimd, d4=d4, e=e, t0=t0):
                    CE.wait(gpsimd)
                    MST.wait(gpsimd)
                    op = ALU.bypass if e == 0 else ALU.add
                    MST.inc(gpsimd.dma_start(
                        out=macc[t0:t0 + HT, d4 * 512:(d4 + 1) * 512]
                        .rearrange("(a p) o -> p a o", p=128),
                        in_=dall[:].rearrange("p (a o) -> p a o", a=16),
                        accum_op=op), 16)
                pending_acc[0] = _acc

    flush_acc()
    for i in range(NT):
        def _ldf(gpsimd, i=i):
            MST.wait(gpsimd)
            CE.wait(gpsimd)
            STO.wait(gpsimd)
            LDF.inc(gpsimd.dma_start(out=x0r[:], in_=x1a[i * 128:(i + 1) * 128, :]), 16)
            LDF.inc(gpsimd.dma_start(out=x1r[:], in_=macc[i * 128:(i + 1) * 128, :]), 16)
            LDF.wait(gpsimd)
        block.gpsimd(_ldf)

        def _vef(vector, i=i):
            LDF.wait(vector)
            CE.inc(vector.tensor_add(x1r[:], x1r[:], x0r[:]))
        block.vector(_vef)

        def _stf(sync, i=i):
            CE.wait(sync)
            STO.inc(sync.dma_start(out=x1b[i * 128:(i + 1) * 128, :], in_=x1r[:]), 16)
        block.sync(_stf)

    def _fin(sync):
        STO.wait(sync)
        LDXC.wait(sync)
        LDXN.wait(sync)
        LDCR.wait(sync)
        LDF.wait(sync)
        LDW.wait(sync)
        MST.wait(sync)
    block.sync(_fin)
    return {"x1b": (STO.sem, STO.n)}


VARS["x1b"] = ((N, D), F32)
CONSTS_SHAPES.update({
    "gwt": ((D, 8), F32), "sgw": ((128, ND), F32),
    "w_gate": ((8, D, 1408), F32), "w_up": ((8, D, 1408), F32),
    "w_down": ((8, 1408, D), F32),
    "sw_gate": ((D, 5632), F32), "sw_up": ((D, 5632), F32),
    "sw_down": ((5632, D), F32),
})


NROW1 = 4416


@phase("l1proj", ins=["xn1t", "W_inT"], outs=["xbcf", "zt"])
def ph_l1proj(ctx, deps):
    block = ctx.block
    A = ctx.aps
    xn1t, W_inT, xbcf, zt = A["xn1t"], A["W_inT"], A["xbcf"], A["zt"]

    xnf = ctx.sbuf("p1_xnf", [128, ND * N], BF16)
    wch = ctx.sbuf("p1_wch", [128, ND * 128], BF16)
    wz = ctx.sbuf("p1_wz", [128, ND * 512], BF16)
    orow = ctx.sbuf("p1_orow", [128, N], F32)
    zrow = ctx.sbuf("p1_zrow", [128, 512 * 4], F32)
    psA = ctx.psum("p1_psA", [128, 512], F32)
    LDX, LDW, CP, CE, STO = (ctx.chain() for _ in range(5))

    dv = deps.get("xn1t") if deps else None
    if dv:
        def _w(sync):
            sync.wait_ge(dv[0], dv[1])
        block.sync(_w)

    def _ldxn(sync):
        for c in range(ND):
            LDX.inc(sync.dma_start(out=xnf[:, c * N:(c + 1) * N],
                                   in_=xn1t[:, c * 128:(c + 1) * 128], transpose=True), 16)
    block.sync(_ldxn)

    for rc in range(35):
        rows = 128 if rc < 34 else 64
        r0 = 4096 + rc * 128

        def _ldw(gpsimd, rc=rc, rows=rows, r0=r0):
            CP.wait(gpsimd)
            LDW.inc(gpsimd.dma_start(
                out=wch[:, 0:ND * rows].rearrange("p (c o) -> p c o", c=ND),
                in_=W_inT[:, r0:r0 + rows].rearrange("(c p) o -> p c o", p=128)), 16)
        block.gpsimd(_ldw)

        for t4 in range(4):
            def _pe(tensor, rc=rc, rows=rows, t4=t4):
                LDW.wait(tensor)
                LDX.wait(tensor)
                CE.wait(tensor)
                for c in range(ND):
                    mm = tensor.matmul(psA[:, 0:512] if rows == 128 else psA[0:64, 0:512],
                                       wch[:, 0:ND * rows]
                                       .rearrange("p (c o) -> p c o", c=ND)[:, c, :],
                                       xnf[:, c * N + t4 * 512: c * N + (t4 + 1) * 512],
                                       start=(c == 0), stop=(c == ND - 1))
                CP.inc(mm)
            block.tensor(_pe)

            def _ve(tensor_v, rc=rc, rows=rows, t4=t4):
                CP.wait(tensor_v)
                if t4 == 0:
                    STO.wait(tensor_v)
                CE.inc(tensor_v.tensor_copy(orow[0:rows, t4 * 512:(t4 + 1) * 512],
                                            psA[0:rows, 0:512]))
            block.vector(_ve)

        def _st(sync, rc=rc, rows=rows):
            CE.wait(sync)
            STO.inc(sync.dma_start(out=xbcf[rc * 128: rc * 128 + rows, :],
                                   in_=orow[0:rows, :]), 16)
        block.sync(_st)

    for zc in range(8):
        def _ldwz(gpsimd, zc=zc):
            CP.wait(gpsimd)
            LDW.inc(gpsimd.dma_start(
                out=wz[:].rearrange("p (c o) -> p c o", c=ND),
                in_=W_inT[:, zc * 512:(zc + 1) * 512].rearrange("(c p) o -> p c o", p=128)), 16)
        block.gpsimd(_ldwz)

        for tt in range(NT):
            def _pe(tensor, zc=zc, tt=tt):
                LDW.wait(tensor)
                LDX.wait(tensor)
                CE.wait(tensor)
                for c in range(ND):
                    mm = tensor.matmul(psA[:],
                                       xnf[:, c * N + tt * 128: c * N + (tt + 1) * 128],
                                       wz[:, c * 512:(c + 1) * 512],
                                       start=(c == 0), stop=(c == ND - 1))
                CP.inc(mm)
            block.tensor(_pe)

            def _ve(vector, zc=zc, tt=tt):
                CP.wait(vector)
                if tt % 4 == 0:
                    STO.wait(vector)
                CE.inc(vector.tensor_copy(zrow[:, (tt % 4) * 512:(tt % 4 + 1) * 512],
                                          psA[:]))
            block.vector(_ve)

            if tt % 4 == 3:
                def _st(sync, zc=zc, tt=tt):
                    CE.wait(sync)
                    STO.inc(sync.dma_start(
                        out=zt[(tt - 3) * 128:(tt + 1) * 128, zc * 512:(zc + 1) * 512]
                        .rearrange("(a p) o -> p a o", p=128),
                        in_=zrow[:].rearrange("p (a o) -> p a o", a=4)), 16)
                block.sync(_st)

    def _fin(sync):
        STO.wait(sync)
        LDX.wait(sync)
        LDW.wait(sync)
    block.sync(_fin)
    return {"xbcf": (STO.sem, STO.n), "zt": (STO.sem, STO.n)}


VARS["xbcf"] = ((NROW1, N), F32)
VARS["zt"] = ((N, 4096), F32)
CONSTS_SHAPES["W_inT"] = ((D, 8512), F32)


@phase("l1conv", ins=["xbcf", "convw", "convb"], outs=["xcb"])
def ph_l1conv(ctx, deps):
    block = ctx.block
    A = ctx.aps
    xbcf, xcb = A["xbcf"], A["xcb"]
    NCC = 34

    xp = ctx.sbuf("cv_xp", [128, N + 4], F32)
    acc = ctx.sbuf("cv_acc", [128, N], F32)
    ob = ctx.sbuf("cv_ob", [128, N], BF16)
    cw = ctx.sbuf("cv_cw", [128, NCC * 4], F32)
    cbb = ctx.sbuf("cv_cb", [128, NCC], F32)
    LDX, CP, CE, CS, STO = (ctx.chain() for _ in range(5))

    dv = deps.get("xbcf") if deps else None
    if dv:
        def _w(sync):
            sync.wait_ge(dv[0], dv[1])
        block.sync(_w)

    def _ld0(sync):
        LDX.inc(sync.dma_start(out=cw[:].rearrange("p (c k) -> p c k", c=NCC),
                               in_=A["convw"].rearrange("(c p) k -> p c k", p=128)), 16)
        LDX.inc(sync.dma_start(out=cbb[:], in_=A["convb"][:]), 16)
    block.sync(_ld0)

    def _init(vector):
        CE.inc(vector.memset(xp[:, 0:4], 0.0))
    block.vector(_init)

    for cc in range(NCC):
        def _ldx(sync, cc=cc):
            CS.wait(sync)
            CE.wait(sync)
            LDX.inc(sync.dma_start(out=xp[:, 3:3 + N],
                                   in_=xbcf[cc * 128:(cc + 1) * 128, :]), 16)
        block.sync(_ldx)

        def _ve(vector, cc=cc):
            LDX.wait(vector)
            CE.wait(vector)
            CE.inc(vector.tensor_scalar(acc[:], xp[:, 0:N],
                                        cw[:, cc * 4:cc * 4 + 1], None, ALU.mult))
            for j in range(1, 4):
                CE.swait(vector)
                CE.inc(vector.scalar_tensor_tensor(acc[:], xp[:, j:j + N],
                                                   cw[:, cc * 4 + j:cc * 4 + j + 1],
                                                   acc[:], ALU.mult, ALU.add))
        block.vector(_ve)

        def _sc(scalar, cc=cc):
            CE.wait(scalar)
            if cc >= 1:
                STO.wait(scalar)
            CS.inc(scalar.activation(ob[:], acc[:], AF.Silu,
                                     bias=cbb[:, cc:cc + 1]))
        block.scalar(_sc)

        def _st(sync, cc=cc):
            CS.wait(sync)
            STO.inc(sync.dma_start(out=xcb[cc * 128:(cc + 1) * 128, :], in_=ob[:]), 16)
        block.sync(_st)

    def _fin(sync):
        STO.wait(sync)
        LDX.wait(sync)
    block.sync(_fin)
    return {"xcb": (STO.sem, STO.n)}


VARS["xcb"] = ((4352, N), BF16)
CONSTS_SHAPES["convw"] = ((4352, 4), F32)
CONSTS_SHAPES["convb"] = ((128, 34), F32)


@phase("l1scan", ins=["xcb", "xbcf", "dt_biasv", "A_logv", "d_rep", "onesr",
                      "eye_bf", "eye_f32", "mnegf"],
       outs=["yd"])
def ph_l1scan(ctx, deps):
    block = ctx.block
    A = ctx.aps
    xcb, xbcf, yd = A["xcb"], A["xbcf"], A["yd"]
    NHm, HDm, Q, NCH = 64, 64, 128, 16
    DS = 4096

    bch = ctx.sbuf("sc_bch", [128, N], BF16)
    cch = ctx.sbuf("sc_cch", [128, N], BF16)
    dtf = ctx.sbuf("sc_dtf", [64, N], F32)
    dtb = ctx.sbuf("sc_dtb", [64, 1], F32)
    alg = ctx.sbuf("sc_alg", [64, 4], F32)
    drep = ctx.sbuf("sc_drep", [1, DS], F32)
    onesr = ctx.sbuf("sc_onesr", [1, 128], F32)
    eyeb = ctx.sbuf("sc_eyeb", [128, 128], BF16)
    eyef = ctx.sbuf("sc_eyef", [128, 128], F32)
    mneg = ctx.sbuf("sc_mneg", [128, 128], F32)
    ones_t = ctx.sbuf("sc_ones", [64, N], F32)
    sp1 = ctx.sbuf("sc_sp1", [64, N], F32)
    sp2 = ctx.sbuf("sc_sp2", [64, N], F32)
    dts = ctx.sbuf("sc_dts", [64, N], F32)
    cg = ctx.sbuf("sc_cg", [64, N], F32)
    ncg = ctx.sbuf("sc_ncg", [64, N], F32)
    cgt = ctx.sbuf("sc_cgt", [128, NCH * 64], F32)
    dtst = ctx.sbuf("sc_dtst", [128, NCH * 64], F32)
    dbc = ctx.sbuf("sc_dbc", [128, DS], F32)
    st_f = ctx.sbuf("sc_stf", [128, DS], F32)
    st_b = ctx.sbuf("sc_stb", [128, DS], BF16)
    klhs = ctx.sbuf("sc_klhs", [2, 1024], F32)
    krhs = ctx.sbuf("sc_krhs", [2, 1024], F32)
    csr = ctx.sbuf("sc_csr", [1, 64], F32)
    cer = ctx.sbuf("sc_cer", [1, 64], F32)
    erow = ctx.sbuf("sc_erow", [1, 64], F32)
    ecirel = ctx.sbuf("sc_eci", [128, 64], F32)
    wfac = ctx.sbuf("sc_wfac", [128, 64], F32)
    etb = ctx.sbuf("sc_etb", [128, 64], F32)
    tsc1 = ctx.sbuf("sc_tsc1", [128, 64], F32)
    xt_tok = ctx.sbuf("sc_xtok", [128, DS], BF16)
    xw = ctx.sbuf("sc_xw", [128, DS], BF16)
    xdt = ctx.sbuf("sc_xdt", [128, DS], BF16)
    btok = ctx.sbuf("sc_btok", [128, 128], BF16)
    dq = ctx.sbuf("sc_dq", [128, 1024], F32)
    dex = ctx.sbuf("sc_dex", [128, 1024], F32)
    mgt = ctx.sbuf("sc_mgt", [128, 1024], BF16)
    ty = ctx.sbuf("sc_ty", [128, 512], F32)
    ty2 = ctx.sbuf("sc_ty2", [128, 512], F32)
    yrow = ctx.sbuf("sc_yrow", [128, DS], F32)
    psD = ctx.psum("sc_psD", [128, 1024], F32)
    psY = ctx.psum("sc_psY", [128, 512], F32)
    psI = ctx.psum("sc_psI", [128, 512], F32)
    psS2 = ctx.psum("sc_psS2", [128, 512], F32)
    psG2 = ctx.psum("sc_psG2", [128, 128], F32)
    psB2 = ctx.psum("sc_psB2", [128, 64], F32)
    pstr2 = ctx.psum("sc_pstr2", [128, 128], BF16)

    LDC, LDR, LDT, CP, CE, CS, STO = (ctx.chain() for _ in range(7))
    nc = ctx.nc
    cgd = nc.dram_tensor("sc_cgd", [64, N], F32, kind="Internal").ap()
    ncgd = nc.dram_tensor("sc_ncgd", [64, N], F32, kind="Internal").ap()
    cgtd = nc.dram_tensor("sc_cgtd", [N, 64], F32, kind="Internal").ap()

    for nm in ("xcb", "xbcf"):
        dv = deps.get(nm) if deps else None
        if dv:
            def _w(sync, dv=dv):
                sync.wait_ge(dv[0], dv[1])
            block.sync(_w)

    def _ld0(sync):
        LDC.inc(sync.dma_start(out=bch[:], in_=xcb[4096:4224, :]), 16)
        LDC.inc(sync.dma_start(out=cch[:], in_=xcb[4224:4352, :]), 16)
        LDC.inc(sync.dma_start(out=dtf[:], in_=xbcf[4352:4416, :]), 16)
        LDC.inc(sync.dma_start(out=dtb[:], in_=A["dt_biasv"][:]), 16)
        LDC.inc(sync.dma_start(out=alg[:, 0:1], in_=A["A_logv"][:]), 16)
        LDC.inc(sync.dma_start(out=drep[:], in_=A["d_rep"][:]), 16)
        LDC.inc(sync.dma_start(out=onesr[:], in_=A["onesr"][:]), 16)
        LDC.inc(sync.dma_start(out=eyeb[:], in_=A["eye_bf"][:]), 16)
        LDC.inc(sync.dma_start(out=eyef[:], in_=A["eye_f32"][:]), 16)
        LDC.inc(sync.dma_start(out=mneg[:], in_=A["mnegf"][:]), 16)
    block.sync(_ld0)

    def _sp_sc(scalar):
        LDC.wait(scalar)
        CS.inc(scalar.activation(sp1[:], dtf[:], AF.Relu, bias=dtb[:, 0:1]))
        CS.wait(scalar)
        CS.inc(scalar.activation(sp2[:], dtf[:], AF.Abs, bias=dtb[:, 0:1]))
        CS.wait(scalar)
        CS.inc(scalar.activation(sp2[:], sp2[:], AF.Exp, scale=-1.0))
    block.scalar(_sp_sc)

    def _sp_ve(vector):
        CS.wait(vector)
        CE.inc(vector.tensor_scalar(sp2[:], sp2[:], 1.0, None, ALU.add))
    block.vector(_sp_ve)

    def _sp_sc2(scalar):
        CE.wait(scalar)
        CS.inc(scalar.activation(sp2[:], sp2[:], AF.Ln))
        CS.wait(scalar)
        CS.inc(scalar.activation(alg[:, 1:2], alg[:, 0:1], AF.Exp))
    block.scalar(_sp_sc2)

    def _sp_ve2(vector):
        CS.wait(vector)
        CE.inc(vector.tensor_add(dts[:], sp1[:], sp2[:]))
        CE.wait(vector)
        CE.inc(vector.tensor_scalar(alg[:, 2:3], alg[:, 1:2], -1.0, None, ALU.mult))
        CE.wait(vector)
        CE.inc(vector.tensor_scalar(sp1[:], dts[:], alg[:, 2:3], None, ALU.mult))
        CE.wait(vector)
        CE.inc(vector.memset(ones_t[:], 1.0))
        CE.wait(vector)
        CE.inc(vector.tensor_tensor_scan(cg[:], ones_t[:], sp1[:], 0.0,
                                         ALU.mult, ALU.add))
        CE.wait(vector)
        CE.inc(vector.tensor_scalar(ncg[:], cg[:], -1.0, None, ALU.mult))
        CE.wait(vector)
        CE.inc(vector.memset(klhs[:], 1.0))
        CE.wait(vector)
        CE.inc(vector.memset(krhs[:], 1.0))
        CE.wait(vector)
        CE.inc(vector.memset(st_f[:], 0.0))
        CE.wait(vector)
        CE.inc(vector.memset(st_b[:], 0.0))
    block.vector(_sp_ve2)

    def _stg(sync):
        CE.wait(sync)
        STO.inc(sync.dma_start(out=cgd[:], in_=cg[:]), 16)
        STO.inc(sync.dma_start(out=ncgd[:], in_=ncg[:]), 16)
    block.sync(_stg)

    for c in range(NCH):
        def _tr(tensor, c=c):
            CE.wait(tensor)
            CP.inc(tensor.transpose(pstr2[:].bitcast(F32)[:, 0:64] if False else psB2[:],
                                    cg[:, c * 128:(c + 1) * 128], eyef[0:64, 0:64]))
            CP.wait(tensor)
            CP.inc(tensor.transpose(psG2[:, 0:64], dts[:, c * 128:(c + 1) * 128],
                                    eyef[0:64, 0:64]))
        block.tensor(_tr)

        def _cp(vector, c=c):
            CP.wait(vector)
            STO.wait(vector)
            CE.inc(vector.tensor_copy(cgt[:, c * 64:(c + 1) * 64], psB2[:]))
            CE.swait(vector)
            CE.inc(vector.tensor_copy(dtst[:, c * 64:(c + 1) * 64], psG2[:, 0:64]))
        block.vector(_cp)

        def _st(sync, c=c):
            CE.wait(sync)
            STO.inc(sync.dma_start(out=cgtd[c * 128:(c + 1) * 128, :],
                                   in_=cgt[:, c * 64:(c + 1) * 64]), 16)
        block.sync(_st)

    for b8 in range(8):
        def _pe(tensor, b8=b8):
            LDC.wait(tensor)
            CE.wait(tensor)
            CP.inc(tensor.matmul(psY[:], onesr[:], drep[0:1, b8 * 512:(b8 + 1) * 512],
                                 start=True, stop=True))
        block.tensor(_pe)

        def _ve(vector, b8=b8):
            CP.wait(vector)
            CE.inc(vector.tensor_copy(dbc[:, b8 * 512:(b8 + 1) * 512], psY[:]))
        block.vector(_ve)

    for c in range(NCH):
        def _ldr(sync, c=c):
            CP.wait(sync)
            CE.wait(sync)
            STO.wait(sync)
            if c > 0:
                LDR.inc(sync.dma_start(out=csr[:], in_=cgtd[c * 128 - 1:c * 128, :]), 16)
            LDR.inc(sync.dma_start(out=cer[:], in_=cgtd[(c + 1) * 128 - 1:(c + 1) * 128, :]), 16)
        block.sync(_ldr)

        def _ldt(sync, c=c):
            CE.wait(sync)
            LDT.inc(sync.dma_start(out=xt_tok[:], in_=xcb[0:4096, c * 128:(c + 1) * 128],
                                   transpose=True), 16)
        block.sync(_ldt)

        def _pe_cs(tensor, c=c):
            LDR.wait(tensor)
            CE.wait(tensor)
            if c > 0:
                tensor.matmul(psB2[:], onesr[:], csr[0:1, :], start=True, stop=True)
            CP.inc(tensor.matmul(psI[:, 0:64], onesr[:], cer[0:1, :], start=True, stop=True))
        block.tensor(_pe_cs)

        def _ve_cs(vector, c=c):
            CP.wait(vector)
            if c > 0:
                CE.inc(vector.tensor_tensor(tsc1[:], cgt[:, c * 64:(c + 1) * 64],
                                            psB2[:], ALU.subtract))
            else:
                CE.inc(vector.tensor_copy(tsc1[:], cgt[:, 0:64]))
            CE.swait(vector)
            CE.inc(vector.tensor_tensor(wfac[:], psI[:, 0:64],
                                        cgt[:, c * 64:(c + 1) * 64], ALU.subtract))
        block.vector(_ve_cs)

        def _sc_cs(scalar, c=c):
            CE.wait(scalar)
            CS.inc(scalar.activation(ecirel[:], tsc1[:], AF.Exp))
            CS.wait(scalar)
            CS.inc(scalar.activation(wfac[:], wfac[:], AF.Exp))
        block.scalar(_sc_cs)

        def _ve_cs2(vector, c=c):
            CS.wait(vector)
            CE.inc(vector.tensor_tensor(wfac[:], wfac[:], dtst[:, c * 64:(c + 1) * 64],
                                        ALU.mult))
            if c > 0:
                CE.swait(vector)
                CE.inc(vector.tensor_tensor(erow[:], cer[:], csr[:], ALU.subtract))
        block.vector(_ve_cs2)

        def _sc_cs2(scalar, c=c):
            CE.wait(scalar)
            CS.inc(scalar.activation(erow[:], erow[:] if c > 0 else cer[:], AF.Exp))
        block.scalar(_sc_cs2)

        def _pe_et(tensor, c=c):
            CS.wait(tensor)
            CE.wait(tensor)
            CP.inc(tensor.matmul(psB2[:], onesr[:], erow[:], start=True, stop=True))
        block.tensor(_pe_et)

        def _ve_et(vector, c=c):
            CP.wait(vector)
            CE.inc(vector.tensor_copy(etb[:], psB2[:]))
        block.vector(_ve_et)

        def _ve_x(vector, c=c):
            LDT.wait(vector)
            CE.swait(vector)
            wb3 = wfac[:].unsqueeze(2).broadcast_to([128, 64, 64])
            CE.inc(vector.tensor_tensor(xw[:].rearrange("p (h d) -> p h d", h=64),
                                        xt_tok[:].rearrange("p (h d) -> p h d", h=64),
                                        wb3, ALU.mult))
            CE.swait(vector)
            db3 = dtst[:, c * 64:(c + 1) * 64].unsqueeze(2).broadcast_to([128, 64, 64])
            CE.inc(vector.tensor_tensor(xdt[:].rearrange("p (h d) -> p h d", h=64),
                                        xt_tok[:].rearrange("p (h d) -> p h d", h=64),
                                        db3, ALU.mult))
        block.vector(_ve_x)

        def _pe_bg(tensor, c=c):
            LDC.wait(tensor)
            CE.wait(tensor)
            CP.inc(tensor.transpose(pstr2[:], bch[:, c * 128:(c + 1) * 128], eyeb[:]))
            CP.wait(tensor)
            CP.inc(tensor.matmul(psG2[:], bch[:, c * 128:(c + 1) * 128],
                                 cch[:, c * 128:(c + 1) * 128], start=True, stop=True))
        block.tensor(_pe_bg)

        def _ve_bt(vector, c=c):
            CP.wait(vector)
            CE.inc(vector.tensor_copy(btok[:], pstr2[:]))
        block.vector(_ve_bt)

        for hg in range(8):
            h0 = hg * 8

            def _ld_k(sync, c=c, h0=h0):
                CP.wait(sync)
                LDR.inc(sync.dma_start(
                    out=klhs[0:1, :].rearrange("p (h t) -> p h t", h=8),
                    in_=ncgd[h0:h0 + 8, c * 128:(c + 1) * 128].unsqueeze(0)), 16)
                LDR.inc(sync.dma_start(
                    out=krhs[1:2, :].rearrange("p (h t) -> p h t", h=8),
                    in_=cgd[h0:h0 + 8, c * 128:(c + 1) * 128].unsqueeze(0)), 16)
            block.sync(_ld_k)

            def _pe_d(tensor, c=c, h0=h0):
                LDR.wait(tensor)
                CE.wait(tensor)
                for j in range(8):
                    mm = tensor.matmul(psD[:, j * 128:(j + 1) * 128],
                                       klhs[:, j * 128:(j + 1) * 128],
                                       krhs[:, j * 128:(j + 1) * 128],
                                       start=True, stop=True)
                CP.inc(mm)
            block.tensor(_pe_d)

            def _ve_d(vector, c=c, h0=h0):
                CP.wait(vector)
                mb3 = mneg[:].unsqueeze(1).broadcast_to([128, 8, 128])
                CE.inc(vector.tensor_tensor(dq[:].rearrange("p (j t) -> p j t", j=8),
                                            psD[:].rearrange("p (j t) -> p j t", j=8),
                                            mb3, ALU.add))
            block.vector(_ve_d)

            def _sc_d(scalar, c=c, h0=h0):
                CE.wait(scalar)
                CS.inc(scalar.activation(dex[:], dq[:], AF.Exp))
            block.scalar(_sc_d)

            def _ve_mgt(vector, c=c, h0=h0):
                CS.wait(vector)
                gb3 = psG2[:].unsqueeze(1).broadcast_to([128, 8, 128])
                CE.inc(vector.tensor_tensor(mgt[:].rearrange("p (j t) -> p j t", j=8),
                                            dex[:].rearrange("p (j t) -> p j t", j=8),
                                            gb3, ALU.mult))
            block.vector(_ve_mgt)

            def _pe_y(tensor, c=c, h0=h0):
                CE.wait(tensor)
                for j in range(8):
                    h = h0 + j
                    tensor.matmul(psY[:, j * 64:(j + 1) * 64],
                                  mgt[:, j * 128:(j + 1) * 128],
                                  xdt[:, h * 64:(h + 1) * 64], start=True, stop=True)
                for j in range(8):
                    h = h0 + j
                    tensor.matmul(psI[:, j * 64:(j + 1) * 64],
                                  cch[:, c * 128:(c + 1) * 128],
                                  st_b[:, h * 64:(h + 1) * 64], start=True, stop=True)
                for j in range(8):
                    h = h0 + j
                    mm = tensor.matmul(psS2[:, j * 64:(j + 1) * 64], btok[:],
                                       xw[:, h * 64:(h + 1) * 64], start=True, stop=True)
                CP.inc(mm)
            block.tensor(_pe_y)

            def _ve_y(vector, c=c, h0=h0, hg=hg):
                CP.wait(vector)
                if hg == 0:
                    STO.wait(vector)
                eb3 = ecirel[:, h0:h0 + 8].unsqueeze(2).broadcast_to([128, 8, 64])
                CE.inc(vector.tensor_tensor(ty[:].rearrange("p (j d) -> p j d", j=8),
                                            psI[:].rearrange("p (j d) -> p j d", j=8),
                                            eb3, ALU.mult))
                CE.swait(vector)
                CE.inc(vector.tensor_tensor(ty2[:], xt_tok[:, h0 * 64:(h0 + 8) * 64],
                                            dbc[:, h0 * 64:(h0 + 8) * 64], ALU.mult))
                CE.swait(vector)
                CE.inc(vector.tensor_add(ty[:], ty[:], ty2[:]))
                CE.swait(vector)
                CE.inc(vector.tensor_add(yrow[:, h0 * 64:(h0 + 8) * 64], ty[:], psY[:]))
                CE.swait(vector)
                et3 = etb[:, h0:h0 + 8].unsqueeze(2).broadcast_to([128, 8, 64])
                CE.inc(vector.tensor_tensor(
                    st_f[:, h0 * 64:(h0 + 8) * 64].rearrange("p (j d) -> p j d", j=8),
                    st_f[:, h0 * 64:(h0 + 8) * 64].rearrange("p (j d) -> p j d", j=8),
                    et3, ALU.mult))
                CE.swait(vector)
                CE.inc(vector.tensor_add(st_f[:, h0 * 64:(h0 + 8) * 64],
                                         st_f[:, h0 * 64:(h0 + 8) * 64], psS2[:]))
                CE.swait(vector)
                CE.inc(vector.tensor_copy(st_b[:, h0 * 64:(h0 + 8) * 64],
                                          st_f[:, h0 * 64:(h0 + 8) * 64]))
            block.vector(_ve_y)

        def _st_y(sync, c=c):
            CE.wait(sync)
            STO.inc(sync.dma_start(out=yd[c * 128:(c + 1) * 128, :], in_=yrow[:]), 16)
        block.sync(_st_y)

    def _fin(sync):
        STO.wait(sync)
        LDC.wait(sync)
        LDR.wait(sync)
        LDT.wait(sync)
    block.sync(_fin)
    return {"yd": (STO.sem, STO.n)}


VARS["yd"] = ((N, 4096), F32)
CONSTS_SHAPES.update({
    "dt_biasv": ((64, 1), F32), "A_logv": ((64, 1), F32),
    "d_rep": ((1, 4096), F32), "mnegf": ((128, 128), F32),
})


@phase("l1post", ins=["yd", "zt", "W_outT", "norm_wv", "rowsel", "onesr", "eye_bf"],
       outs=["att1h", "ynd"])
def ph_l1post(ctx, deps):
    block = ctx.block
    A = ctx.aps
    yd, zt, att1h, ynd = A["yd"], A["zt"], A["att1h"], A["ynd"]
    DS = 4096

    ix = ctx.sbuf("po_ix", [128, 8], mybir.dt.int32)
    nwrow = ctx.sbuf("po_nwr", [1, DS], F32)
    nwb = ctx.sbuf("po_nwb", [128, DS], F32)
    onesr = ctx.sbuf("po_onesr", [1, 128], F32)
    yrow = ctx.sbuf("po_yrow", [128, DS], F32)
    zrow = ctx.sbuf("po_zrow", [128, DS], F32)
    sq = ctx.sbuf("po_sq", [128, DS], F32)
    stt = ctx.sbuf("po_stt", [128, 4], F32)
    ynb = ctx.sbuf("po_ynb", [128, DS], BF16)
    psB = ctx.psum("po_psB", [128, 512], F32)
    LDI, LDG, CP, CE, CS, STO = (ctx.chain() for _ in range(6))

    for nm in ("yd", "zt"):
        dv = deps.get(nm) if deps else None
        if dv:
            def _w(sync, dv=dv):
                sync.wait_ge(dv[0], dv[1])
            block.sync(_w)

    def _ld0(sync):
        LDI.inc(sync.dma_start(out=ix[:], in_=A["rowsel"][:]), 16)
        LDI.inc(sync.dma_start(out=nwrow[:], in_=A["norm_wv"][:]), 16)
        LDI.inc(sync.dma_start(out=onesr[:], in_=A["onesr"][:]), 16)
    block.sync(_ld0)

    for b8 in range(8):
        def _pe(tensor, b8=b8):
            LDI.wait(tensor)
            CE.wait(tensor)
            CP.inc(tensor.matmul(psB[:], onesr[:], nwrow[0:1, b8 * 512:(b8 + 1) * 512],
                                 start=True, stop=True))
        block.tensor(_pe)

        def _ve(vector, b8=b8):
            CP.wait(vector)
            CE.inc(vector.tensor_copy(nwb[:, b8 * 512:(b8 + 1) * 512], psB[:]))
        block.vector(_ve)

    for g in range(8):
        def _gth(gpsimd, g=g):
            CE.wait(gpsimd)
            CS.wait(gpsimd)
            LDG.inc(gpsimd.indirect_dma_start(
                yrow[:], None, yd[:, :], IndirectOffsetOnAxis(ix[:, g:g + 1], 0)), 16)
            LDG.inc(gpsimd.indirect_dma_start(
                zrow[:], None, zt[:, :], IndirectOffsetOnAxis(ix[:, g:g + 1], 0)), 16)
        block.gpsimd(_gth)

        def _sc(scalar, g=g):
            LDG.wait(scalar)
            CS.inc(scalar.activation(zrow[:], zrow[:], AF.Silu))
        block.scalar(_sc)

        def _ve(vector, g=g):
            CS.wait(vector)
            CE.inc(vector.tensor_tensor(yrow[:], yrow[:], zrow[:], ALU.mult))
        block.vector(_ve)

        def _sc2(scalar, g=g):
            CE.wait(scalar)
            CS.inc(scalar.activation(sq[:], yrow[:], AF.Square, accum_out=stt[:, 0:1]))
        block.scalar(_sc2)

        def _ve2(vector, g=g):
            CS.wait(vector)
            CE.inc(vector.tensor_scalar(stt[:, 1:2], stt[:, 0:1], 1.0 / DS, None,
                                        ALU.mult))
            CE.wait(vector)
            CE.inc(vector.tensor_scalar(stt[:, 1:2], stt[:, 1:2], 1e-5, None, ALU.add))
        block.vector(_ve2)

        def _sc3(scalar, g=g):
            CE.wait(scalar)
            CS.inc(scalar.activation(stt[:, 2:3], stt[:, 1:2], AF.Ln))
            CS.wait(scalar)
            CS.inc(scalar.activation(stt[:, 2:3], stt[:, 2:3], AF.Exp, scale=-0.5))
        block.scalar(_sc3)

        def _ve3(vector, g=g):
            CS.wait(vector)
            if g >= 1:
                STO.wait(vector)
            CE.inc(vector.tensor_scalar(yrow[:], yrow[:], stt[:, 2:3], None, ALU.mult))
            CE.wait(vector)
            CE.inc(vector.tensor_tensor(ynb[:], yrow[:], nwb[:], ALU.mult))
        block.vector(_ve3)

        def _st(sync, g=g):
            CE.wait(sync)
            STO.inc(sync.dma_start(out=ynd[g * 128:(g + 1) * 128, :], in_=ynb[:]), 16)
        block.sync(_st)

    wog = ctx.sbuf("po_wog", [128, 8 * N], BF16)
    ynf = ctx.sbuf("po_ynf", [128, 8 * 128], BF16)
    drow = ctx.sbuf("po_drow", [128, N], F32)
    psW = ctx.psum("po_psW", [128, 512], F32)
    LDW2, LDT2, MST = (ctx.chain() for _ in range(3))

    for cg in range(4):
        def _ldw(gpsimd, cg=cg):
            CP.wait(gpsimd)
            MST.wait(gpsimd)
            for j in range(8):
                LDW2.inc(gpsimd.dma_start(
                    out=wog[:, j * N:(j + 1) * N],
                    in_=A["W_outT"][(cg * 8 + j) * 128:(cg * 8 + j + 1) * 128, :]), 16)
        block.gpsimd(_ldw)

        for g in range(8):
            def _ldt(sync, cg=cg, g=g):
                CP.wait(sync)
                STO.wait(sync)
                for j in range(8):
                    LDT2.inc(sync.dma_start(
                        out=ynf[:, j * 128:(j + 1) * 128],
                        in_=ynd[g * 128:(g + 1) * 128,
                                (cg * 8 + j) * 128:(cg * 8 + j + 1) * 128],
                        transpose=True), 16)
            block.sync(_ldt)

            for d4 in range(4):
                def _pe(tensor, cg=cg, g=g, d4=d4):
                    LDW2.wait(tensor)
                    LDT2.wait(tensor)
                    CE.wait(tensor)
                    for j in range(8):
                        mm = tensor.matmul(psW[:], ynf[:, j * 128:(j + 1) * 128],
                                           wog[:, j * N + d4 * 512: j * N + (d4 + 1) * 512],
                                           start=(j == 0), stop=(j == 7))
                    CP.inc(mm)
                block.tensor(_pe)

                def _ve(vector, cg=cg, g=g, d4=d4):
                    CP.wait(vector)
                    if d4 == 0:
                        MST.wait(vector)
                    CE.inc(vector.tensor_copy(drow[:, d4 * 512:(d4 + 1) * 512], psW[:]))
                block.vector(_ve)

            def _acc(gpsimd, cg=cg, g=g):
                CE.wait(gpsimd)
                MST.wait(gpsimd)
                op = ALU.bypass if cg == 0 else ALU.add
                MST.inc(gpsimd.dma_start(out=att1h[g * 128:(g + 1) * 128, :],
                                         in_=drow[:], accum_op=op), 16)
            block.gpsimd(_acc)

    def _fin(sync):
        STO.wait(sync)
        LDI.wait(sync)
        LDG.wait(sync)
        LDW2.wait(sync)
        LDT2.wait(sync)
        MST.wait(sync)
    block.sync(_fin)
    return {"att1h": (MST.sem, MST.n), "ynd": (STO.sem, STO.n)}


VARS["att1h"] = ((1024, D), F32)
VARS["ynd"] = ((1024, 4096), BF16)
CONSTS_SHAPES.update({
    "W_outT": ((4096, D), F32), "norm_wv": ((1, 4096), F32),
    "rowsel": ((128, 8), mybir.dt.int32),
})


HH1 = 8192


@phase("l1quant", ins=["w_gate_up", "w_proj"], outs=["whgud", "whpd"])
def ph_l1quant(ctx, deps):
    block = ctx.block
    A = ctx.aps

    wqf = ctx.sbuf("fq_wqf", [128, 2 * HH1], F32)
    absw_b = [ctx.sbuf("fq_absw", [128, HH1], BF16) for _ in range(2)]
    sgn_b = [ctx.sbuf("fq_sgn", [128, HH1], BF16) for _ in range(2)]
    gs = ctx.sbuf("fq_gs", [128, 128], F32)
    gsb = ctx.sbuf("fq_gsb", [128, 128], BF16)
    hgs = ctx.sbuf("fq_hgs", [128, 128], F32)
    what = [ctx.sbuf("fq_what", [128, HH1], BF16) for _ in range(2)]
    LDQ0, LDQ1, CE, CS, STO = (ctx.chain() for _ in range(5))
    SC_DONE, DVE_DONE = {}, {}

    jobs = []
    for rc in range(128):
        jobs.append((A["w_gate_up"], A["whgud"], rc, D))
    for rc in range(16):
        jobs.append((A["w_proj"], A["whpd"], rc, HH1))

    for j, (wsrc, wdst, rc, ncols) in enumerate(jobs):
        LQ = LDQ0 if j % 2 == 0 else LDQ1
        s = j % 2
        ng = ncols // 64
        wv = wqf[:, s * HH1:s * HH1 + ncols]
        absw = absw_b[s]
        sgn = sgn_b[s]

        def _ldq(sync, j=j, wsrc=wsrc, rc=rc, ncols=ncols, LQ=LQ, s=s):
            if j >= 2:
                sync.wait_ge(CS.sem, SC_DONE.get(j - 2, 0))
            LQ.inc(sync.dma_start(out=wqf[:, s * HH1:s * HH1 + ncols],
                                  in_=wsrc[rc * 128:(rc + 1) * 128, :]), 16)
        block.sync(_ldq)

        def _sc(scalar, j=j, wv=wv, ncols=ncols, LQ=LQ, s=s, absw=absw, sgn=sgn):
            scalar.wait_ge(LQ.sem, 16 * (j // 2 + 1))
            if j >= 2:
                scalar.wait_ge(CE.sem, DVE_DONE.get(j - 2, 0))
            CS.inc(scalar.activation(absw[:, 0:ncols], wv, AF.Abs))
            CS.wait(scalar)
            CS.inc(scalar.activation(sgn[:, 0:ncols], wv, AF.Sign))
            SC_DONE[j] = CS.n
        block.scalar(_sc)
        sc_done = CS.n

        def _ve(vector, j=j, ncols=ncols, ng=ng, s=s, sc_done=sc_done,
                absw=absw, sgn=sgn):
            vector.wait_ge(CS.sem, sc_done)
            CE.inc(vector.tensor_reduce(
                gs[:, 0:ng], absw[:, 0:ncols].rearrange("p (g k) -> p g k", k=64),
                AX.X, ALU.add))
            CE.wait(vector)
            CE.inc(vector.tensor_scalar(gs[:, 0:ng], gs[:, 0:ng], 1.0 / 64, 1e-8,
                                        ALU.mult, ALU.max))
            CE.wait(vector)
            CE.inc(vector.tensor_copy(gsb[:, 0:ng], gs[:, 0:ng]))
            CE.wait(vector)
            CE.inc(vector.tensor_scalar(hgs[:, 0:ng], gsb[:, 0:ng], 0.501953125,
                                        None, ALU.mult))
            CE.wait(vector)
            CE.inc(vector.tensor_tensor(
                absw[:, 0:ncols].rearrange("p (g k) -> p g k", k=64),
                absw[:, 0:ncols].rearrange("p (g k) -> p g k", k=64),
                hgs[:, 0:ng].unsqueeze(2).broadcast_to([128, ng, 64]), ALU.is_gt))
            CE.swait(vector)
            CE.inc(vector.tensor_tensor(
                absw[:, 0:ncols].rearrange("p (g k) -> p g k", k=64),
                absw[:, 0:ncols].rearrange("p (g k) -> p g k", k=64),
                gsb[:, 0:ng].unsqueeze(2).broadcast_to([128, ng, 64]), ALU.mult))
            CE.swait(vector)
            if j >= 2:
                vector.wait_ge(STO.sem, 32 * (j // 2))
            CE.inc(vector.tensor_tensor(what[s][:, 0:ncols], sgn[:, 0:ncols],
                                        absw[:, 0:ncols], ALU.mult))
            DVE_DONE[j] = CE.n
        block.vector(_ve)
        dv_done = CE.n

        def _st(sync, j=j, wdst=wdst, rc=rc, ncols=ncols, s=s, dv_done=dv_done):
            sync.wait_ge(CE.sem, dv_done)
            STO.inc(sync.dma_start(out=wdst[rc * 128:(rc + 1) * 128, :],
                                   in_=what[s][:, 0:ncols]), 16)
        block.sync(_st)

    def _fin(sync):
        STO.wait(sync)
        LDQ0.wait(sync)
        LDQ1.wait(sync)
    block.sync(_fin)
    done = (STO.sem, STO.n)
    return {"whgud": done, "whpd": done}


VARS["whgud"] = ((16384, D), BF16)
VARS["whpd"] = ((D, HH1), BF16)


@phase("l1gu", ins=["xn1t", "whgud", "rowsel", "onesc"], outs=["hfd", "rsd"])
def ph_l1gu(ctx, deps):
    nc, block = ctx.nc, ctx.block
    A = ctx.aps
    whgud, hfd, rsd = A["whgud"], A["hfd"], A["rsd"]
    HT = 1024
    xn1hd = nc.dram_tensor("xn1hd", [HT, D], BF16, kind="Internal").ap()
    gufd = nc.dram_tensor("gufd", [16384, HT], BF16, kind="Internal").ap()

    ix = ctx.sbuf("fg_ix", [128, 8], mybir.dt.int32)
    onesc = ctx.sbuf("fg_onesc", [128, 1], F32)
    xrow = ctx.sbuf("fg_xrow", [128, D], BF16)
    xnf1 = ctx.sbuf("fg_xnf1", [128, ND * HT], BF16)
    wtall = ctx.sbuf("fg_wtall", [128, ND * 2048], BF16)
    gur = ctx.sbuf("fg_gur", [128, HT], BF16)
    psA = ctx.psum("fg_psA", [128, 1024], F32)
    LDI, LDQ, STOG, LDT2, LDW2, CP, CE, CS, STO = (ctx.chain() for _ in range(9))
    for nm in ("xn1t", "whgud"):
        dv = deps.get(nm) if deps else None
        if dv:
            def _w(sync, dv=dv):
                sync.wait_ge(dv[0], dv[1])
            block.sync(_w)

    def _ld0(sync):
        LDI.inc(sync.dma_start(out=ix[:], in_=A["rowsel"][:]), 16)
        LDI.inc(sync.dma_start(out=onesc[:], in_=A["onesc"][:]), 16)
    block.sync(_ld0)

    for g in range(8):
        def _gth(gpsimd, g=g):
            LDI.wait(gpsimd)
            LDQ.inc(gpsimd.indirect_dma_start(
                xrow[:], None, A["xn1t"][:, :], IndirectOffsetOnAxis(ix[:, g:g + 1], 0)), 16)
            LDQ.wait(gpsimd)
            STOG.inc(gpsimd.dma_start(out=xn1hd[g * 128:(g + 1) * 128, :], in_=xrow[:]), 16)
            STOG.wait(gpsimd)
        block.gpsimd(_gth)

    def _ldxn(sync):
        STOG.wait(sync)
        for c in range(ND):
            LDT2.inc(sync.dma_start(out=xnf1[:, c * HT:(c + 1) * HT],
                                    in_=xn1hd[:, c * 128:(c + 1) * 128], transpose=True), 16)
    block.sync(_ldxn)

    GUEV = {}
    for hb in range(8):
        def _ldw(sync, hb=hb):
            CP.wait(sync)
            for c in range(ND):
                LDW2.inc(sync.dma_start(
                    out=wtall[:, c * 2048:(c + 1) * 2048],
                    in_=whgud[hb * 2048:(hb + 1) * 2048, c * 128:(c + 1) * 128],
                    transpose=True), 16)
        block.sync(_ldw)

        for hc in range(16):
            for t4 in range(2):
                gg = hb * 32 + hc * 2 + t4

                def _pe(tensor, hb=hb, hc=hc, t4=t4, gg=gg):
                    LDW2.wait(tensor)
                    LDT2.wait(tensor)
                    tensor.wait_ge(CE.sem, GUEV.get(gg - 2, 0))
                    sl = (t4 % 2) * 512
                    for c in range(ND):
                        mm = tensor.matmul(
                            psA[:, sl:sl + 512],
                            wtall[:, c * 2048 + hc * 128: c * 2048 + (hc + 1) * 128],
                            xnf1[:, c * HT + t4 * 512: c * HT + (t4 + 1) * 512],
                            start=(c == 0), stop=(c == ND - 1))
                    CP.inc(mm)
                block.tensor(_pe)

                def _ve(vector, hb=hb, hc=hc, t4=t4, gg=gg):
                    CP.wait(vector)
                    if t4 == 0:
                        STO.wait(vector)
                    CE.inc(vector.tensor_copy(gur[:, t4 * 512:(t4 + 1) * 512],
                                              psA[:, (t4 % 2) * 512:(t4 % 2) * 512 + 512]))
                    GUEV[gg] = CE.n
                block.vector(_ve)

            def _st(sync, hb=hb, hc=hc):
                CE.wait(sync)
                STO.inc(sync.dma_start(
                    out=gufd[(hb * 16 + hc) * 128:(hb * 16 + hc + 1) * 128, :],
                    in_=gur[:]), 16)
            block.sync(_st)

    gate_s = ctx.sbuf("fg_gate", [128, HT], BF16)
    up_s = ctx.sbuf("fg_up", [128, HT], BF16)
    sgf = ctx.sbuf("fg_sgf", [128, HT], F32)
    hf32 = ctx.sbuf("fg_hf32", [128, HT], F32)
    hbf = ctx.sbuf("fg_hbf", [128, HT], BF16)
    sqh = ctx.sbuf("fg_sqh", [128, HT], F32)
    ssr = ctx.sbuf("fg_ssr", [1, HT], F32)
    psR = ctx.psum("fg_psR", [1, 1024], F32)
    LDH = ctx.chain()

    for hc in range(64):
        def _ldh(sync, hc=hc):
            CS.wait(sync)
            CE.wait(sync)
            CP.wait(sync)
            LDH.inc(sync.dma_start(out=gate_s[:], in_=gufd[hc * 128:(hc + 1) * 128, :]), 16)
            LDH.inc(sync.dma_start(out=up_s[:],
                                   in_=gufd[HH1 + hc * 128: HH1 + (hc + 1) * 128, :]), 16)
        block.sync(_ldh)

        def _sc(scalar, hc=hc):
            LDH.wait(scalar)
            CS.inc(scalar.activation(sgf[:], gate_s[:], AF.Silu))
        block.scalar(_sc)

        def _ve(vector, hc=hc):
            CS.wait(vector)
            CE.inc(vector.tensor_tensor(hf32[:], sgf[:], up_s[:], ALU.mult))
            CE.swait(vector)
            STO.wait(vector)
            CE.inc(vector.tensor_copy(hbf[:], hf32[:]))
        block.vector(_ve)

        def _sc2(scalar, hc=hc):
            CE.wait(scalar)
            CS.inc(scalar.activation(sqh[:], hf32[:], AF.Square))
        block.scalar(_sc2)

        def _pe2(tensor, hc=hc):
            CS.wait(tensor)
            if hc == 0:
                CE.wait(tensor)
            tensor.matmul(psR[:, 0:512], onesc[:], sqh[:, 0:512],
                          start=(hc == 0), stop=(hc == 63))
            mm = tensor.matmul(psR[:, 512:1024], onesc[:], sqh[:, 512:1024],
                               start=(hc == 0), stop=(hc == 63))
            CP.inc(mm)
        block.tensor(_pe2)

        def _st(sync, hc=hc):
            CE.wait(sync)
            STO.inc(sync.dma_start(out=hfd[hc * 128:(hc + 1) * 128, :], in_=hbf[:]), 16)
        block.sync(_st)

    def _ver(vector):
        CP.wait(vector)
        CE.inc(vector.tensor_copy(ssr[:, 0:1024], psR[:, 0:1024]))
        CE.wait(vector)
        CE.inc(vector.tensor_scalar(ssr[:, 0:1024], ssr[:, 0:1024], 1.0 / HH1, None,
                                    ALU.mult))
        CE.wait(vector)
        CE.inc(vector.tensor_scalar(ssr[:, 0:1024], ssr[:, 0:1024],
                                    1.1920928955078125e-07, None, ALU.add))
    block.vector(_ver)

    def _scr(scalar):
        CE.wait(scalar)
        CS.inc(scalar.activation(ssr[:, 0:1024], ssr[:, 0:1024], AF.Ln))
        CS.wait(scalar)
        CS.inc(scalar.activation(ssr[:, 0:1024], ssr[:, 0:1024], AF.Exp, scale=-0.5))
    block.scalar(_scr)

    def _str(sync):
        CS.wait(sync)
        STO.inc(sync.dma_start(out=A["rsd"][:].flatten().unsqueeze(0), in_=ssr[:, 0:1024]), 16)
    block.sync(_str)

    def _fin(sync):
        STO.wait(sync)
        LDI.wait(sync)
        LDQ.wait(sync)
        STOG.wait(sync)
        LDT2.wait(sync)
        LDW2.wait(sync)
        LDH.wait(sync)
    block.sync(_fin)
    done = (STO.sem, STO.n)
    return {"hfd": done, "rsd": done}


VARS["hfd"] = ((HH1, 1024), BF16)
VARS["rsd"] = ((8, 128), F32)


@phase("l1proj2", ins=["hfd", "whpd", "rsd", "x1b", "att1h", "rowsel", "eye_f32"],
       outs=["outh"])
def ph_l1proj2(ctx, deps):
    block = ctx.block
    A = ctx.aps
    hfd, whpd, x1b, att1h, outh = A["hfd"], A["whpd"], A["x1b"], A["att1h"], A["outh"]
    HT = 1024

    ix = ctx.sbuf("fp_ix", [128, 8], mybir.dt.int32)
    eyef = ctx.sbuf("fp_eyef", [128, 128], F32)
    rr8 = ctx.sbuf("fp_rr8", [8, 128], F32)
    rstd_tok = ctx.sbuf("fp_rstok", [128, 8], F32)
    wptg = ctx.sbuf("fp_wptg", [128, 8 * 2048], BF16)
    hfb = ctx.sbuf("fp_hfb", [128, 8 * 128], BF16)
    pacc8 = ctx.sbuf("fp_pacc8", [128, 8 * 2048], F32)
    x1br = ctx.sbuf("fp_x1br", [128, D], F32)
    a1r = ctx.sbuf("fp_a1r", [128, D], F32)
    orow = ctx.sbuf("fp_orow", [128, D], F32)
    psQ = ctx.psum("fp_psQ", [128, 1024], F32)
    psT8 = ctx.psum("fp_psT8", [128, 8], F32)
    LDI, LDG, LDW, LDH, LDF, CP, CE, STO = (ctx.chain() for _ in range(8))

    for nm in ("hfd", "whpd", "rsd", "att1h"):
        dv = deps.get(nm) if deps else None
        if dv:
            def _w(sync, dv=dv):
                sync.wait_ge(dv[0], dv[1])
            block.sync(_w)

    def _ld0(sync):
        LDI.inc(sync.dma_start(out=ix[:], in_=A["rowsel"][:]), 16)
        LDI.inc(sync.dma_start(out=eyef[:], in_=A["eye_f32"][:]), 16)
        LDI.inc(sync.dma_start(out=rr8[:], in_=A["rsd"][:]), 16)
    block.sync(_ld0)

    def _per(tensor):
        LDI.wait(tensor)
        CP.inc(tensor.transpose(psT8[:], rr8[:], eyef[0:8, 0:8]))
    block.tensor(_per)

    def _ver2(vector):
        CP.wait(vector)
        CE.inc(vector.tensor_copy(rstd_tok[:], psT8[:]))
    block.vector(_ver2)

    for hg in range(8):
        def _ldw(sync, hg=hg):
            CP.wait(sync)
            for j in range(8):
                LDW.inc(sync.dma_start(
                    out=wptg[:, j * 2048:(j + 1) * 2048],
                    in_=whpd[:, (hg * 8 + j) * 128:(hg * 8 + j + 1) * 128],
                    transpose=True), 16)
        block.sync(_ldw)

        for tt in range(8):
            def _ldh(sync, hg=hg, tt=tt):
                CP.wait(sync)
                LDH.inc(sync.dma_start(
                    out=hfb[:].rearrange("p (c t) -> p c t", c=8),
                    in_=hfd[hg * 1024:(hg + 1) * 1024, tt * 128:(tt + 1) * 128]
                    .rearrange("(c p) t -> p c t", p=128)), 16)
            block.sync(_ldh)

            for d2 in range(2):
                def _pe(tensor, hg=hg, tt=tt, d2=d2):
                    LDW.wait(tensor)
                    LDH.wait(tensor)
                    CE.wait(tensor)
                    sl = d2 * 1024
                    for j in range(8):
                        tensor.matmul(psQ[:, 0:512],
                                      hfb[:, j * 128:(j + 1) * 128],
                                      wptg[:, j * 2048 + sl: j * 2048 + sl + 512],
                                      start=(j == 0), stop=(j == 7))
                    for j in range(8):
                        mm = tensor.matmul(psQ[:, 512:1024],
                                           hfb[:, j * 128:(j + 1) * 128],
                                           wptg[:, j * 2048 + sl + 512: j * 2048 + sl + 1024],
                                           start=(j == 0), stop=(j == 7))
                    CP.inc(mm)
                block.tensor(_pe)

                def _ve(vector, hg=hg, tt=tt, d2=d2):
                    CP.wait(vector)
                    o = tt * 2048 + d2 * 1024
                    if hg == 0:
                        CE.inc(vector.tensor_copy(pacc8[:, o:o + 1024], psQ[:, 0:1024]))
                    else:
                        CE.inc(vector.tensor_add(pacc8[:, o:o + 1024],
                                                 pacc8[:, o:o + 1024], psQ[:, 0:1024]))
                block.vector(_ve)

    for tt in range(8):
        def _gth(gpsimd, tt=tt):
            LDI.wait(gpsimd)
            CE.wait(gpsimd)
            LDG.inc(gpsimd.indirect_dma_start(
                x1br[:], None, x1b[:, :], IndirectOffsetOnAxis(ix[:, tt:tt + 1], 0)), 16)
        block.gpsimd(_gth)

        def _ldf(sync, tt=tt):
            CE.wait(sync)
            STO.wait(sync)
            LDF.inc(sync.dma_start(out=a1r[:], in_=att1h[tt * 128:(tt + 1) * 128, :]), 16)
        block.sync(_ldf)

        def _vef(vector, tt=tt):
            LDG.wait(vector)
            LDF.wait(vector)
            CE.inc(vector.tensor_scalar(orow[:], pacc8[:, tt * 2048:(tt + 1) * 2048],
                                        rstd_tok[:, tt:tt + 1], None, ALU.mult))
            CE.wait(vector)
            CE.inc(vector.tensor_add(orow[:], orow[:], a1r[:]))
            CE.wait(vector)
            CE.inc(vector.tensor_add(orow[:], orow[:], x1br[:]))
        block.vector(_vef)

        def _stf(sync, tt=tt):
            CE.wait(sync)
            STO.inc(sync.dma_start(out=outh[tt * 128:(tt + 1) * 128, :], in_=orow[:]), 16)
        block.sync(_stf)

    def _fin(sync):
        STO.wait(sync)
        LDI.wait(sync)
        LDG.wait(sync)
        LDW.wait(sync)
        LDH.wait(sync)
        LDF.wait(sync)
    block.sync(_fin)
    return {"outh": (STO.sem, STO.n)}


VARS["outh"] = ((1024, D), F32)
CONSTS_SHAPES.update({
    "w_gate_up": ((16384, D), F32), "w_proj": ((D, HH if False else 8192), F32),
})


LAUNCH_GROUPS = [
    ["ln0", "l0qkv", "l0v", "l0reg", "l0attn", "l0moe", "ln1",
     "l1proj", "l1conv", "l1scan", "l1post", "l1quant", "l1gu", "l1proj2"],
]


def build_group(group, ext_out=None):
    nc = bass.Bass("TRN2", target_bir_lowering=False)
    need_in, produced = set(), set()
    for pname in group:
        fn, ins, outs = PHASES[pname]
        for n in ins:
            if n not in produced:
                need_in.add(n)
        produced.update(outs)
    if ext_out is None:
        ext_out = produced
    aps = {}
    for n in sorted(need_in):
        shape, dt = CONSTS_SHAPES[n] if n in CONSTS_SHAPES else VARS[n]
        aps[n] = nc.dram_tensor(n, list(shape), dt, kind="ExternalInput").ap()
    for n in sorted(produced):
        shape, dt = VARS[n]
        kind = "ExternalOutput" if n in ext_out else "Internal"
        aps[n] = nc.dram_tensor(n, list(shape), dt, kind=kind).ap()
    with contextlib.ExitStack() as gstack:
        with nc.Block() as block:
            ctx = Ctx(nc, block, aps, gstack)
            ctx.phd = ctx.chain()
            deps = {}
            for pi, pname in enumerate(group):
                fn, ins, outs = PHASES[pname]
                ctx.phase_idx = pi
                ctx.phase_chains = []
                with contextlib.ExitStack() as pstack:
                    ctx.stack = pstack
                    newdeps = fn(ctx, deps)
                if newdeps:
                    deps.update(newdeps)
                chains = ctx.phase_chains

                def _pb(sync, chains=chains):
                    for ch in chains:
                        ch.wait(sync)
                    ctx.phd.inc(sync.nop())
                ctx.block.sync(_pb)
    return nc, sorted(need_in), sorted(n for n in produced if n in ext_out)


def prepare_consts():
    import ml_dtypes
    c = {}
    c["eye_bf"] = np.eye(128, dtype=ml_dtypes.bfloat16)
    c["eye_f32"] = np.eye(128, dtype=np.float32)
    c["onesc"] = np.ones((128, 1), dtype=np.float32)
    c["onesr"] = np.ones((1, 128), dtype=np.float32)
    inv = (10000.0 ** (-np.arange(0, 128, 2, dtype=np.float64) / 128.0))
    th = np.arange(N, dtype=np.float64)[None, :] * inv[:, None]
    cos = np.cos(th).astype(np.float32)
    sin = np.sin(th).astype(np.float32)
    s = np.float32(128.0 ** -0.5)
    c["ccq"] = np.concatenate([cos * s, cos * s], 0).astype(ml_dtypes.bfloat16)
    c["ssq"] = np.concatenate([-sin * s, sin * s], 0).astype(ml_dtypes.bfloat16)
    c["cck"] = np.concatenate([cos, cos], 0).astype(ml_dtypes.bfloat16)
    c["ssk"] = np.concatenate([-sin, sin], 0).astype(ml_dtypes.bfloat16)
    c["mtriu"] = (-np.triu(np.ones((64, 64), dtype=np.float32), 1))
    cm = np.where(np.tril(np.ones((128, 128), bool)), 0.0, -1e30).astype(np.float32)
    c["cmaskb"] = cm.astype(ml_dtypes.bfloat16)
    jj, ii = np.meshgrid(np.arange(128), np.arange(128), indexing="ij")
    c["mnegf"] = np.where(jj <= ii, 0.0, -1e30).astype(np.float32)
    return c


NCORES = 8


def kernel(**inputs):
    consts = prepare_consts()
    asc = np.ascontiguousarray
    shared = {
        "Wq": inputs["L0_attn_Wq"], "Wk": inputs["L0_attn_Wk"],
        "Wv": inputs["L0_attn_Wv"], "Wb": inputs["L0_attn_Wb"],
        "Wo": inputs["L0_attn_Wo"],
        "gwt": asc(inputs["L0_ffn_gate_w"].T),
        "sgw": asc(inputs["L0_ffn_shared_gate_w"].reshape(16, 128).T),
        "w_gate": inputs["L0_ffn_w_gate"], "w_up": inputs["L0_ffn_w_up"],
        "w_down": inputs["L0_ffn_w_down"],
        "sw_gate": inputs["L0_ffn_sw_gate"], "sw_up": inputs["L0_ffn_sw_up"],
        "sw_down": inputs["L0_ffn_sw_down"],
        "W_inT": asc(inputs["L1_attn_W_in"].T),
        "convw": asc(inputs["L1_attn_conv_w"].reshape(4352, 4)),
        "convb": asc(inputs["L1_attn_conv_b"].reshape(34, 128).T),
        "dt_biasv": asc(inputs["L1_attn_dt_bias"].reshape(64, 1)),
        "A_logv": asc(inputs["L1_attn_A_log"].reshape(64, 1)),
        "d_rep": asc(np.repeat(inputs["L1_attn_D"], 64).reshape(1, 4096)),
        "norm_wv": asc(inputs["L1_attn_norm_w"].reshape(1, 4096)),
        "W_outT": asc(inputs["L1_attn_W_out"].T),
        "w_gate_up": inputs["L1_ffn_w_gate_up"], "w_proj": inputs["L1_ffn_w_proj"],
    }
    shared.update(consts)
    x0 = inputs["L0_attn_x"]
    percore = []
    for c in range(NCORES):
        p = c % 2
        rs = (p * 1024 + np.arange(1024)).astype(np.int32).reshape(8, 128).T.copy()
        percore.append({"x0": asc(x0[c // 2]), "rowsel": rs})

    state = [dict() for _ in range(NCORES)]

    def get(c, n):
        if n in percore[c]:
            return percore[c][n]
        if n in shared:
            return shared[n]
        return state[c][n]

    produced_by = {}
    for gi, group in enumerate(LAUNCH_GROUPS):
        for pname in group:
            for n in PHASES[pname][2]:
                produced_by[n] = gi
    for gi, group in enumerate(LAUNCH_GROUPS):
        ext = {"outh"}
        for gj in range(gi + 1, len(LAUNCH_GROUPS)):
            for pname in LAUNCH_GROUPS[gj]:
                for n in PHASES[pname][1]:
                    if produced_by.get(n, 99) <= gi:
                        ext.add(n)
        nc, in_names, out_names = build_group(group, ext_out=ext)
        in_maps = [{n: get(c, n) for n in in_names} for c in range(NCORES)]
        res = run_bass_kernel_spmd(nc, in_maps, core_ids=list(range(NCORES)))
        for c in range(NCORES):
            for n in out_names:
                state[c][n] = np.asarray(res.results[c][n])

    out = np.empty((B, N, D), dtype=np.float32)
    for b in range(B):
        out[b, 0:1024] = state[2 * b]["outh"]
        out[b, 1024:2048] = state[2 * b + 1]["outh"]
    return out
